# version 01
# baseline (speedup 1.0000x reference)
import numpy as np
import concourse.bass as bass
import concourse.mybir as mybir

F32 = mybir.dt.float32
BF16 = mybir.dt.bfloat16
I8 = mybir.dt.int8
U32 = mybir.dt.uint32
AF = mybir.ActivationFunctionType
ALU = mybir.AluOpType
Pool = mybir.EngineType.Pool

CONST_VALS = [1e-30, 1e-5, 1.0 / 128.0, -0.5, -0.5 * float(np.log(128.0)),
              1.0 / 2048.0, -1.0, 1.0, 0.5, 2.0, 0.0,
              1.0 / float(np.sqrt(8192.0)), 1.0 / float(np.sqrt(2048.0)),
              1.0 / 8192.0, 64.0, 3.0]

HID = 2048
DH = 128
NH = 8
B = 2
N_CORES = 8
EPS = 1e-5



class Sem:
    def __init__(self, h, name):
        self.h = h
        self.name = name
        self.v = 0


class Buf:

    __slots__ = ("name", "w", "readers")

    def __init__(self, name):
        self.name = name
        self.w = None
        self.readers = []


class Ctx:

    def __init__(self, nc, block, stack):
        self.nc = nc
        self.block = block
        names = ["te", "dve", "act", "pool", "cc"]
        for q in ("dmas", "dmag", "dmaa"):
            for i in range(4):
                names.append(f"{q}{i}")
        self.sem = {}
        for n in names:
            h = stack.enter_context(nc.semaphore("sem_" + n))
            self.sem[n] = Sem(h, n)
        self.qrr = {"dmas": 0, "dmag": 0, "dmaa": 0}
        self.waited = {k: {} for k in ["te", "dve", "act", "pool", "sync", "scalar"]}
        self.bufs = {}

    def buf(self, name):
        if name not in self.bufs:
            self.bufs[name] = Buf(name)
        return self.bufs[name]

    def _wait(self, eng, engname, tok):
        if tok is None:
            return
        sem, val = tok
        if sem.name.startswith("dma") or sem.name == "cc":
            val = sem.v
        wmap = self.waited[engname]
        if wmap.get(sem.name, 0) >= val:
            return
        eng.wait_ge(sem.h, val)
        wmap[sem.name] = val

    def emit(self, eng, engname, semname, fn, ins=(), outs=(), inc=1):
        for b in ins:
            self._wait(eng, engname, self.buf(b).w)
        for b in outs:
            bb = self.buf(b)
            self._wait(eng, engname, bb.w)
            for tok in bb.readers:
                self._wait(eng, engname, tok)
        inst = fn(eng)
        s = self.sem[semname]
        s.v += inc
        inst.then_inc(s.h, inc)
        tok = (s, s.v)
        for b in ins:
            self.buf(b).readers.append(tok)
        for b in outs:
            bb = self.buf(b)
            bb.w = tok
            bb.readers = []
        return tok

    def barrier_engine(self, eng, engname, toks):
        for tok in toks:
            self._wait(eng, engname, tok)



def mk_psum_banks(nc, stack):
    banks = {}
    for i in range(6):
        banks[f"mm{i}"] = stack.enter_context(
            nc.psum_tensor(f"pb{i}", [128, 512], F32))
    banks["tr"] = stack.enter_context(nc.psum_tensor("pbtr", [128, 1024], BF16))
    banks["sm"] = stack.enter_context(nc.psum_tensor("pbsm", [128, 512], F32))
    return banks


class BankPool:
    def __init__(self, names):
        self.names = list(names)
        self.i = 0

    def next(self):
        n = self.names[self.i % len(self.names)]
        self.i += 1
        return n



def host_constants():
    c = {}
    c["cmisc"] = np.tile(np.array([CONST_VALS], dtype=np.float32), (128, 1))
    c["ident"] = np.eye(128, dtype=np.float32)
    c["ones_col"] = np.ones((128, 1), dtype=np.float32)
    c["ones_row"] = np.ones((1, 128), dtype=np.float32)
    sl = np.tril(np.ones((128, 128), dtype=np.float32), k=-1)
    c["maskSL"] = sl
    a = np.arange(128)[:, None]
    tau = np.arange(64)[None, :]
    c["mask64T"] = (a <= 2 * tau + 1).astype(np.float32)
    return c


def host_prep_A(inputs, L=8192):
    x = np.asarray(inputs["L0_attn_x"])
    Wq = np.asarray(inputs["L0_attn_Wq"]); Wk = np.asarray(inputs["L0_attn_Wk"])
    Wv = np.asarray(inputs["L0_attn_Wv"]); Wb = np.asarray(inputs["L0_attn_Wb"])
    Wa = np.asarray(inputs["L0_attn_Wa"]); Wg = np.asarray(inputs["L0_attn_Wg"])
    Wo = np.asarray(inputs["L0_attn_Wo"])
    cq = np.asarray(inputs["L0_attn_conv_q_w"])
    ck = np.asarray(inputs["L0_attn_conv_k_w"])
    cv = np.asarray(inputs["L0_attn_conv_v_w"])
    A_log = np.asarray(inputs["L0_attn_A_log"]); dtb = np.asarray(inputs["L0_attn_dt_bias"])
    norm_w = np.asarray(inputs["L0_attn_norm_w"])
    consts = host_constants()

    maps = []
    for c in range(N_CORES):
        b = c // 4
        h0 = 2 * (c % 4)
        m = {}
        m["xT"] = np.ascontiguousarray(x[b, :L, :].T.astype(np.float32))
        q = c % 4
        LQ = L // 4
        m["xq"] = np.ascontiguousarray(x[b, q * LQ:(q + 1) * LQ, :].T.astype(np.float32))
        cols = []
        for h in (h0, h0 + 1):
            cols.append(Wq[:, h * DH:(h + 1) * DH])
        for h in (h0, h0 + 1):
            for j in (0, 1):
                cols.append(Wk[:, j * 1024 + h * DH: j * 1024 + (h + 1) * DH])
        for h in (h0, h0 + 1):
            for j in (0, 1):
                cols.append(Wv[:, j * 1024 + h * DH: j * 1024 + (h + 1) * DH])
        for h in (h0, h0 + 1):
            cols.append(Wg[:, h * DH:(h + 1) * DH])
        m["wbig"] = np.ascontiguousarray(np.concatenate(cols, axis=1))
        wb_small = np.zeros((HID, 128), dtype=np.float32)
        for li in (0, 1):
            for j in (0, 1):
                wb_small[:, li * 64 + j * 32] = Wb[:, j * NH + (h0 + li)]
        m["wbeta"] = wb_small
        wa_small = np.zeros((HID, 64), dtype=np.float32)
        for li in (0, 1):
            wa_small[:, li * 32] = Wa[:, h0 + li]
        m["wa"] = wa_small
        convs = np.zeros((128, 48), dtype=np.float32)
        tlist = []
        for h in (h0, h0 + 1):
            tlist.append(cq[h * DH:(h + 1) * DH, 0, :])
        for h in (h0, h0 + 1):
            for j in (0, 1):
                tlist.append(ck[j * 1024 + h * DH: j * 1024 + (h + 1) * DH, 0, :])
        for h in (h0, h0 + 1):
            for j in (0, 1):
                tlist.append(cv[j * 1024 + h * DH: j * 1024 + (h + 1) * DH, 0, :])
        for i, t in enumerate(tlist):
            convs[:, 4 * i: 4 * i + 4] = t
        m["convw"] = convs
        asm = np.zeros((64, 1), dtype=np.float32)
        dsm = np.zeros((64, 1), dtype=np.float32)
        asm[0, 0] = A_log[h0]; asm[32, 0] = A_log[h0 + 1]
        dsm[0, 0] = dtb[h0]; dsm[32, 0] = dtb[h0 + 1]
        m["alog"] = asm
        m["dtbias"] = dsm
        m["normw"] = norm_w.reshape(128, 1).astype(np.float32)
        m["wo"] = np.ascontiguousarray(Wo.astype(np.float32))
        m["tqsel"] = np.array([[c % 4]], dtype=np.uint32)
        for k, v in consts.items():
            m[k] = v
        maps.append(m)
    return maps



def declare_A(nc, L):
    t = {}
    t["xT"] = nc.dram_tensor("xT", [HID, L], F32, kind="ExternalInput").ap()
    t["xq"] = nc.dram_tensor("xq", [HID, L // 4], F32, kind="ExternalInput").ap()
    t["wbig"] = nc.dram_tensor("wbig", [HID, 1536], F32, kind="ExternalInput").ap()
    t["wbeta"] = nc.dram_tensor("wbeta", [HID, 128], F32, kind="ExternalInput").ap()
    t["wa"] = nc.dram_tensor("wa", [HID, 64], F32, kind="ExternalInput").ap()
    t["convw"] = nc.dram_tensor("convw", [128, 48], F32, kind="ExternalInput").ap()
    t["alog"] = nc.dram_tensor("alog", [64, 1], F32, kind="ExternalInput").ap()
    t["dtbias"] = nc.dram_tensor("dtbias", [64, 1], F32, kind="ExternalInput").ap()
    t["normw"] = nc.dram_tensor("normw", [128, 1], F32, kind="ExternalInput").ap()
    t["wo"] = nc.dram_tensor("wo", [1024, HID], F32, kind="ExternalInput").ap()
    t["tqsel"] = nc.dram_tensor("tqsel", [1, 1], U32, kind="ExternalInput").ap()
    for nme, shp in (("ident", [128, 128]), ("ones_col", [128, 1]),
                     ("ones_row", [1, 128]), ("cmisc", [128, len(CONST_VALS)]),
                     ("maskSL", [128, 128]), ("mask64T", [128, 64])):
        t[nme] = nc.dram_tensor(nme, shp, F32, kind="ExternalInput").ap()
    t["kscr"] = nc.dram_tensor("kscr", [2, 128, 2 * L], BF16, kind="Internal").ap()
    t["vscr"] = nc.dram_tensor("vscr", [2, 128, 2 * L], BF16, kind="Internal").ap()
    t["qscr"] = nc.dram_tensor("qscr", [2, 128, L], BF16, kind="Internal").ap()
    t["e1_in"] = nc.dram_tensor("e1_in", [256, L], BF16, kind="Internal").ap()
    t["e1_out"] = nc.dram_tensor("e1_out", [1024, L], BF16, kind="Internal").ap()
    return t



ENG2SECT = {"te": "tensor", "dve": "vector", "act": "scalar",
            "pool": "gpsimd", "sync": "sync", "scalar": "scalar"}


class Emitter:

    def __init__(self, nc, block, stack):
        self.ctx = Ctx(nc, block, stack)
        self.block = block
        self.pend_eng = None
        self.pend = []

    def flush(self):
        if not self.pend:
            return
        acts = self.pend
        sect = getattr(self.block, ENG2SECT[self.pend_eng])

        def run(eng):
            for a in acts:
                a(eng)
        sect(run)
        self.pend = []
        self.pend_eng = None

    def _q(self, engname, action):
        if self.pend_eng != engname:
            self.flush()
            self.pend_eng = engname
        self.pend.append(action)

    def op(self, engname, semname, fn, ins=(), outs=(), inc=1):
        ctx = self.ctx

        def action(eng):
            ctx.emit(eng, engname, semname, fn, ins=ins, outs=outs, inc=inc)
        self._q(engname, action)

    def te(self, fn, ins=(), outs=()):
        self.op("te", "te", fn, ins, outs)

    def dve(self, fn, ins=(), outs=()):
        self.op("dve", "dve", fn, ins, outs)

    def act(self, fn, ins=(), outs=()):
        self.op("act", "act", fn, ins, outs)

    def pool(self, fn, ins=(), outs=()):
        self.op("pool", "pool", fn, ins, outs)

    def dma(self, queue, fn, ins=(), outs=()):
        base = {"sync": "dmas", "pool": "dmag", "scalar": "dmaa"}[queue]
        engname = {"sync": "sync", "pool": "pool", "scalar": "scalar"}[queue]
        i = self.ctx.qrr[base]
        self.ctx.qrr[base] = (i + 1) % 4
        self.op(engname, f"{base}{i}", fn, ins, outs, inc=16)

    def cc(self, fn, ins=(), outs=()):
        self.op("pool", "cc", fn, ins, outs, inc=1)

    def wait_all(self, engname, toks):
        ctx = self.ctx

        def action(eng):
            ctx.barrier_engine(eng, engname, toks)
        self._q(engname, action)

    def raw(self, engname, action):
        self._q(engname, action)



def emit_A(nc, em, t, sb, pb, L, bscr, gscr, gatescr, dbg_dal=None):
    NPCH = L // 512
    NSC = L // 64
    mmpool = BankPool(["mm0", "mm1", "mm2", "mm3"])

    em.dma("pool", lambda g: g.dma_start(
        out=sb["wbig"][:].rearrange("p (a o) -> p a o", a=16),
        in_=t["wbig"].rearrange("(a p) o -> p a o", p=128)), outs=("wbig",))
    em.dma("pool", lambda g: g.dma_start(
        out=sb["wbeta"][:].rearrange("p (a o) -> p a o", a=16),
        in_=t["wbeta"].rearrange("(a p) o -> p a o", p=128)), outs=("wbeta",))
    em.dma("pool", lambda g: g.dma_start(
        out=sb["wa"][:].rearrange("p (a o) -> p a o", a=16),
        in_=t["wa"].rearrange("(a p) o -> p a o", p=128)), outs=("wa",))
    em.dma("pool", lambda g: g.dma_start(out=sb["ident"][:], in_=t["ident"][:]), outs=("ident",))
    em.dma("pool", lambda g: g.dma_start(out=sb["onesrow_b"][:], in_=t["ones_row"][:]), outs=("onesrow_b",))
    em.dma("pool", lambda g: g.dma_start(out=sb["onescol_b"][:], in_=t["ones_col"][:]), outs=("onescol_b",))
    em.dma("sync", lambda s: s.dma_start(out=sb["convw"][:], in_=t["convw"][:]), outs=("convw",))
    em.dma("sync", lambda s: s.dma_start(out=sb["alog"][:], in_=t["alog"][:]), outs=("alog",))
    em.dma("sync", lambda s: s.dma_start(out=sb["dtbias"][:], in_=t["dtbias"][:]), outs=("dtbias",))
    em.dma("sync", lambda s: s.dma_start(out=sb["normw"][:], in_=t["normw"][:]), outs=("normw",))
    em.dma("sync", lambda s: s.dma_start(out=sb["maskSL"][:], in_=t["maskSL"][:]), outs=("maskSL",))
    em.dma("sync", lambda s: s.dma_start(out=sb["mask64T"][:], in_=t["mask64T"][:]), outs=("mask64T",))
    em.dma("sync", lambda s: s.dma_start(out=sb["ones_row"][:], in_=t["ones_row"][:]), outs=("ones_row",))
    em.dma("sync", lambda s: s.dma_start(out=sb["ones_col"][:], in_=t["ones_col"][:]), outs=("ones_col",))
    em.dma("sync", lambda s: s.dma_start(out=sb["one1"][:], in_=t["ones_col"][0:1, 0:1]), outs=("one1",))
    em.dma("sync", lambda s: s.dma_start(out=sb["cmisc"][:], in_=t["cmisc"][:]), outs=("cmisc",))
    for i, vv in enumerate(CONST_VALS):
        if vv in (0.0, 1.0, -1.0, 2.0, 0.5):
            continue
        try:
            nc.const_aps.register(F32, vv, sb["cmisc"][:, i:i + 1])
        except Exception:
            pass
    em.wait_all("act", [em.ctx.buf("cmisc").w])
    em.wait_all("dve", [em.ctx.buf("cmisc").w])
    em.wait_all("te", [em.ctx.buf("cmisc").w])
    em.wait_all("pool", [em.ctx.buf("cmisc").w])
    em.dma("sync", lambda s: s.dma_start(out=sb["tqsel"][:], in_=t["tqsel"][:]), outs=("tqsel",))
    em.act(lambda sc: sc.activation(sb["negea"][:], sb["alog"][:], AF.Exp, scale=1.0),
           ins=("alog",), outs=("negea",))
    em.dve(lambda v: v.tensor_scalar(sb["negea"][:], sb["negea"][:], -1.0, None, ALU.mult),
           ins=("negea",), outs=("negea",))
    if dbg_dal is not None:
        em.dma("sync", lambda s: s.dma_start(out=dbg_dal[:, 3:4], in_=sb["negea"][:],
                                             allow_slow_non_contiguous=True),
               ins=("negea",), outs=("dbgal",))

    em.dve(lambda v: v.memset(sb["raw"][:], 0.0),
           outs=tuple(f"raw{ot}" for ot in range(10)))
    for pc in range(NPCH):
        tc = slice(pc * 512, (pc + 1) * 512)
        for a in range(16):
            em.dma("pool", lambda g, a=a, tc=tc: g.dma_start(
                out=sb["xc"][:, a * 512:(a + 1) * 512],
                in_=t["xT"][a * 128:(a + 1) * 128, tc]), outs=(f"xc{a}",))
        xins = tuple(f"xc{a}" for a in range(16))
        for ot in range(10):
            bank = mmpool.next()

            def mmseq(te, ot=ot, bank=bank):
                last = None
                for a in range(16):
                    last = te.matmul(pb[bank][:, 0:512],
                                     sb["wbig"][:, a * 1536 + ot * 128: a * 1536 + (ot + 1) * 128],
                                     sb["xc"][:, a * 512:(a + 1) * 512],
                                     start=(a == 0), stop=(a == 15))
                return last
            em.op("te", "te", mmseq, ins=("wbig",) + xins, outs=(bank,), inc=1)
            em.dve(lambda v, ot=ot, bank=bank: v.tensor_copy(
                sb["raw"][:, ot * 515 + 3: ot * 515 + 515], pb[bank][:, 0:512]),
                ins=(bank,), outs=(f"raw{ot}",))
            em.dve(lambda v, ot=ot: v.tensor_scalar(
                sb["cv"][:, 0:512], sb["raw"][:, ot * 515: ot * 515 + 512],
                sb["convw"][:, 4 * ot: 4 * ot + 1], None, ALU.mult),
                ins=(f"raw{ot}", "convw"), outs=("cv",))
            for tau in (1, 2, 3):
                em.dve(lambda v, ot=ot, tau=tau: v.scalar_tensor_tensor(
                    sb["cv"][:, 0:512], sb["raw"][:, ot * 515 + tau: ot * 515 + tau + 512],
                    sb["convw"][:, 4 * ot + tau: 4 * ot + tau + 1], sb["cv"][:, 0:512],
                    ALU.mult, ALU.add),
                    ins=(f"raw{ot}", "convw", "cv"), outs=("cv",))
            em.dve(lambda v, ot=ot: v.tensor_copy(
                sb["raw"][:, ot * 515: ot * 515 + 3],
                sb["raw"][:, ot * 515 + 512: ot * 515 + 515]),
                ins=(f"raw{ot}",), outs=(f"raw{ot}",))
            em.act(lambda sc: sc.activation(sb["cs"][:, 0:512], sb["cv"][:, 0:512], AF.Silu),
                   ins=("cv",), outs=("cs",))
            if ot < 6:
                em.act(lambda sc: sc.activation(sb["cv"][:, 0:512], sb["cs"][:, 0:512], AF.Square),
                       ins=("cs",), outs=("cv",))
                em.te(lambda te: te.matmul(pb["sm"][0:1, 0:512], sb["ones_col"][:], sb["cv"][:, 0:512],
                                           start=True, stop=True),
                      ins=("cv", "ones_col"), outs=("sm",))
                isq = ot < 2
                em.act(lambda sc, isq=isq: sc.activation(sb["r1"][0:1, 0:512], pb["sm"][0:1, 0:512],
                                                         AF.Ln, bias=1e-30),
                       ins=("sm",), outs=("r1",))
                em.act(lambda sc, isq=isq: sc.activation(
                    sb["r2"][0:1, 0:512], sb["r1"][0:1, 0:512], AF.Exp, scale=-0.5,
                    bias=(-0.5 * np.log(128.0) if isq else 0.0)),
                    ins=("r1",), outs=("r2",))
                em.te(lambda te: te.matmul(pb["sm"][:, 0:512], sb["ones_row"][:], sb["r2"][0:1, 0:512],
                                           start=True, stop=True),
                      ins=("r2", "ones_row"), outs=("sm",))
                em.dve(lambda v: v.tensor_tensor(sb["cs"][:, 0:512], sb["cs"][:, 0:512],
                                                 pb["sm"][:, 0:512], ALU.mult),
                       ins=("cs", "sm"), outs=("cs",))
            if ot < 2:
                em.dve(lambda v: v.tensor_copy(sb["qout"][:, 0:512], sb["cs"][:, 0:512]),
                       ins=("cs",), outs=("qout",))
                em.dma("sync", lambda s, ot=ot, pc=pc: s.dma_start(
                    out=t["qscr"][ot, :, pc * 512:(pc + 1) * 512], in_=sb["qout"][:, 0:512]),
                    ins=("qout",), outs=(f"qscr{ot}",))
            elif ot < 6:
                lane, j = (ot - 2) // 2, (ot - 2) % 2
                em.dve(lambda v, j=j: v.tensor_copy(sb["kint"][:, j:1024:2], sb["cs"][:, 0:512]),
                       ins=("cs",), outs=("kint",))
                if j == 1:
                    em.dma("sync", lambda s, lane=lane, pc=pc: s.dma_start(
                        out=t["kscr"][lane, :, pc * 1024:(pc + 1) * 1024], in_=sb["kint"][:]),
                        ins=("kint",), outs=(f"kscr{lane}",))
            elif ot < 10:
                lane, j = (ot - 6) // 2, (ot - 6) % 2
                em.dve(lambda v, j=j: v.tensor_copy(sb["vint"][:, j:1024:2], sb["cs"][:, 0:512]),
                       ins=("cs",), outs=("vint",))
                if j == 1:
                    em.dma("sync", lambda s, lane=lane, pc=pc: s.dma_start(
                        out=t["vscr"][lane, :, pc * 1024:(pc + 1) * 1024], in_=sb["vint"][:]),
                        ins=("vint",), outs=(f"vscr{lane}",))
        for li in (0, 1):
            ot = 10 + li
            bank = mmpool.next()

            def mmseq2(te, ot=ot, bank=bank):
                last = None
                for a in range(16):
                    last = te.matmul(pb[bank][:, 0:512],
                                     sb["wbig"][:, a * 1536 + ot * 128: a * 1536 + (ot + 1) * 128],
                                     sb["xc"][:, a * 512:(a + 1) * 512],
                                     start=(a == 0), stop=(a == 15))
                return last
            em.op("te", "te", mmseq2, ins=("wbig",) + xins, outs=(bank,), inc=1)
            em.act(lambda sc, bank=bank: sc.activation(
                sb["gatec"][:], pb[bank][:, 0:512], AF.Silu),
                ins=(bank,), outs=("gatec",))
            em.dma("sync", lambda s, li=li, pc=pc: s.dma_start(
                out=gatescr[li, :, pc * 512:(pc + 1) * 512], in_=sb["gatec"][:]),
                ins=("gatec",), outs=(f"gatescr{li}",))
        for li in (0, 1):
            bank = "mm4"

            def mmbeta(te, li=li, bank=bank):
                last = None
                for a in range(16):
                    last = te.matmul(pb[bank][0:64, 0:512],
                                     sb["wbeta"][:, a * 128 + li * 64: a * 128 + (li + 1) * 64],
                                     sb["xc"][:, a * 512:(a + 1) * 512],
                                     start=(a == 0), stop=(a == 15))
                return last
            em.op("te", "te", mmbeta, ins=("wbeta",) + xins, outs=("mm4",), inc=1)
            em.act(lambda sc: sc.activation(sb["sig"][0:64, 0:512], pb["mm4"][0:64, 0:512],
                                            AF.Sigmoid),
                   ins=("mm4",), outs=("sig",))
            for j in (0, 1):
                row = j * 32
                em.dve(lambda v, row=row: v.tensor_scalar(
                    sb["b2row"][0:1, 0:512], sb["sig"][row:row + 1, 0:512],
                    2.0, None, ALU.mult),
                    ins=("sig",), outs=("b2row",))
                em.dma("sync", lambda s, li=li, j=j, pc=pc: s.dma_start(
                    out=bscr[li, j, pc * 512:(pc + 1) * 512].unsqueeze(0),
                    in_=sb["b2row"][0:1, 0:512]),
                    ins=("b2row",), outs=("bscr",))
        bank = "mm5"

        def mmg(te, bank=bank):
            last = None
            for a in range(16):
                last = te.matmul(pb[bank][0:64, 0:512], sb["wa"][:, a * 64:(a + 1) * 64],
                                 sb["xc"][:, a * 512:(a + 1) * 512], start=(a == 0), stop=(a == 15))
            return last
        em.op("te", "te", mmg, ins=("wa",) + xins, outs=(bank,), inc=1)
        em.dve(lambda v: v.tensor_scalar(sb["gu"][0:64, 0:512], pb[bank][0:64, 0:512],
                                         sb["dtbias"][:, 0:1], None, ALU.add),
               ins=(bank, "dtbias"), outs=("gu",))
        em.act(lambda sc: sc.activation(sb["g1"][0:64, 0:512], sb["gu"][0:64, 0:512], AF.Abs),
               ins=("gu",), outs=("g1",))
        em.act(lambda sc: sc.activation(sb["g2"][0:64, 0:512], sb["g1"][0:64, 0:512],
                                        AF.Exp, scale=-1.0),
               ins=("g1",), outs=("g2",))
        em.act(lambda sc: sc.activation(sb["g1"][0:64, 0:512], sb["g2"][0:64, 0:512],
                                        AF.Ln, bias=1.0),
               ins=("g2",), outs=("g1",))
        em.dve(lambda v: v.tensor_scalar(sb["g2"][0:64, 0:512], sb["gu"][0:64, 0:512],
                                         0.0, None, ALU.max),
               ins=("gu",), outs=("g2",))
        em.dve(lambda v: v.tensor_tensor(sb["g3"][0:64, 0:512], sb["g2"][0:64, 0:512],
                                         sb["g1"][0:64, 0:512], ALU.add),
               ins=("g2", "g1"), outs=("g3",))
        em.dve(lambda v: v.tensor_scalar(sb["g3"][0:64, 0:512], sb["g3"][0:64, 0:512],
                                         sb["negea"][:, 0:1], None, ALU.mult),
               ins=("g3", "negea"), outs=("g3",))
        for li in (0, 1):
            em.dma("sync", lambda s, li=li, pc=pc: s.dma_start(
                out=gscr[li, pc * 512:(pc + 1) * 512].unsqueeze(0),
                in_=sb["g3"][li * 32:li * 32 + 1, 0:512]),
                ins=("g3",), outs=("gscr",))


def alloc_A(nc, stack, L):
    sb = {}

    def A(name, shape, dt):
        sb[name] = stack.enter_context(nc.sbuf_tensor("sb_" + name, shape, dt))
    NSC = L // 64
    A("wbig", [128, 16 * 1536], BF16)
    A("wbeta", [128, 16 * 128], BF16)
    A("wa", [128, 16 * 64], BF16)
    A("ident", [128, 128], BF16)
    A("maskSL", [128, 128], F32)
    A("mask64T", [128, 64], F32)
    A("ones_row", [1, 128], F32)
    A("ones_col", [128, 1], F32)
    A("onesrow_b", [1, 128], BF16)
    A("onescol_b", [128, 1], BF16)
    A("one1", [1, 1], F32)
    A("cmisc", [128, len(CONST_VALS)], F32)
    A("convw", [128, 48], F32)
    A("alog", [64, 1], F32)
    A("dtbias", [64, 1], F32)
    A("negea", [64, 1], F32)
    A("normw", [128, 1], F32)
    A("tqsel", [1, 1], U32)
    A("xc", [128, 16 * 512], BF16)
    A("raw", [128, 10 * 515], BF16)
    A("cv", [128, 512], F32)
    A("cs", [128, 512], F32)
    A("b2row", [1, 512], F32)
    A("r1", [1, 512], F32)
    A("r2", [1, 512], F32)
    A("kint", [128, 1024], BF16)
    A("vint", [128, 1024], BF16)
    A("qout", [128, 512], BF16)
    A("sig", [128, 512], F32)
    A("gu", [64, 512], F32)
    A("g1", [64, 512], F32)
    A("g2", [64, 512], F32)
    A("g3", [64, 512], F32)
    A("gatec", [128, 512], BF16)
    for p in (0, 1):
        A(f"Kd{p}", [128, 128], BF16)
        A(f"Vd{p}", [128, 128], BF16)
        A(f"Qd{p}", [128, 64], BF16)
        A(f"Npow{p}", [128, 7 * 128], BF16)
        A(f"Ppow{p}", [128, 7 * 128], BF16)
        A(f"X{p}", [128, 256], BF16)
        A(f"Ks{p}", [128, 128], BF16)
        A(f"Vs{p}", [128, 128], BF16)
        A(f"Ktil{p}", [128, 128], BF16)
        A(f"GTsb{p}", [128, 128], BF16)
        A(f"Zneg{p}", [128, 64], BF16)
        A(f"qbar{p}", [128, 64], BF16)
        A(f"A2T{p}", [128, 64], BF16)
        A(f"sE{p}", [128, 128], F32)
        A(f"sEx{p}", [128, 128], F32)
        A(f"sEm{p}", [128, 128], F32)
        A(f"sD2{p}", [128, 64], F32)
        A(f"sE2{p}", [128, 64], F32)
        A(f"sE2m{p}", [128, 64], F32)
        A(f"crel64{p}", [1, 64], F32)
        A(f"crelrow{p}", [1, 128], F32)
        A(f"coutc{p}", [1, 64], F32)
        A(f"goutr{p}", [1, 64], F32)
        A(f"bA{p}", [1, 64], F32)
        A(f"bB{p}", [1, 64], F32)
        A(f"gsl{p}", [1, 64], F32)
        A(f"ccol{p}", [128, 1], F32)
        A(f"bcol{p}", [128, 1], F32)
        A(f"negb{p}", [128, 1], F32)
        A(f"clastcol{p}", [128, 1], F32)
        A(f"farg{p}", [128, 1], F32)
        A(f"fcol{p}", [128, 1], F32)
        A(f"gamcol{p}", [128, 1], F32)
        A(f"Gcol{p}", [128, 1], F32)
        A(f"combo{p}", [128, 1], F32)
        A(f"osb{p}", [128, 64], F32)
        A(f"osq{p}", [128, 64], F32)
        A(f"r1r{p}", [1, 64], F32)
        A(f"r2r{p}", [1, 64], F32)
        A(f"t1o{p}", [128, 64], F32)
        A(f"t2o{p}", [128, 64], F32)
    A("zeros64", [1, 64], F32)
    for l in (0, 1):
        A(f"S{l}", [128, 128], F32)
        A(f"Sbf{l}", [128, 128], BF16)
        A(f"stmp{l}", [128, 128], F32)
    for p in (0, 1):
        A(f"gch{p}", [128, 64], BF16)
        A(f"gout{p}", [128, 64], BF16)
    A("wo_sb", [128, 8 * 128], BF16)
    A("ogth", [128, 8 * 256], BF16)
    A("xres", [128, 256], F32)
    A("preLN", [128, 16 * 256], F32)
    A("pbf", [128, 256], BF16)
    A("sqbf", [128, 256], BF16)
    A("mrow", [1, 256], F32)
    A("m2row", [1, 256], F32)
    A("v1row", [1, 256], F32)
    A("varrow", [1, 256], F32)
    A("lnr1", [1, 256], F32)
    A("lnr2", [1, 256], F32)
    A("outt", [128, 512], F32)
    return sb


def declare_A_scratch_beta(nc, L):
    bscr = nc.dram_tensor("bscr", [2, 2, L], F32, kind="Internal").ap()
    gscr = nc.dram_tensor("gscr", [2, L], F32, kind="Internal").ap()
    gatescr = nc.dram_tensor("gatescr", [2, 128, L], BF16, kind="Internal").ap()
    return bscr, gscr, gatescr


def emit_A_scan(nc, em, t, sb, pb, L, bscr, gscr, gatescr):
    NSC = L // 64
    for l in (0, 1):
        em.dve(lambda v, l=l: v.memset(sb[f"S{l}"][:], 0.0), outs=(f"S{l}",))
        em.dve(lambda v, l=l: v.memset(sb[f"Sbf{l}"][:], 0.0), outs=(f"Sbf{l}",))
    em.dve(lambda v: v.memset(sb["zeros64"][:], 0.0), outs=("zeros64",))

    for sc in range(NSC):
        for l in (0, 1):
            p = (2 * sc + l) % 2
            P = lambda n: f"{n}{p}"
            SL = f"S{l}"; SBFL = f"Sbf{l}"
            em.dma("sync", lambda s, l=l, sc=sc, p=p: s.dma_start(
                out=sb[f"Kd{p}"][:], in_=t["kscr"][l, :, sc * 128:(sc + 1) * 128]),
                ins=(f"kscr{l}",), outs=(P("Kd"),))
            em.dma("sync", lambda s, l=l, sc=sc, p=p: s.dma_start(
                out=sb[f"Vd{p}"][:], in_=t["vscr"][l, :, sc * 128:(sc + 1) * 128]),
                ins=(f"vscr{l}",), outs=(P("Vd"),))
            em.dma("sync", lambda s, l=l, sc=sc, p=p: s.dma_start(
                out=sb[f"Qd{p}"][:], in_=t["qscr"][l, :, sc * 64:(sc + 1) * 64]),
                ins=(f"qscr{l}",), outs=(P("Qd"),))
            em.dma("scalar", lambda s, l=l, sc=sc, p=p: s.dma_start(
                out=sb[f"bA{p}"][:], in_=bscr[l, 0, sc * 64:(sc + 1) * 64].unsqueeze(0)),
                ins=("bscr",), outs=(P("bA"),))
            em.dma("scalar", lambda s, l=l, sc=sc, p=p: s.dma_start(
                out=sb[f"bB{p}"][:], in_=bscr[l, 1, sc * 64:(sc + 1) * 64].unsqueeze(0)),
                ins=("bscr",), outs=(P("bB"),))
            em.dma("scalar", lambda s, l=l, sc=sc, p=p: s.dma_start(
                out=sb[f"gsl{p}"][:], in_=gscr[l, sc * 64:(sc + 1) * 64].unsqueeze(0)),
                ins=("gscr",), outs=(P("gsl"),))
            em.dve(lambda v, p=p: v.tensor_tensor_scan(
                sb[f"crel64{p}"][:], sb[f"gsl{p}"][:], sb["zeros64"][:], 0.0,
                ALU.add, ALU.add), ins=(P("gsl"), "zeros64"), outs=(P("crel64"),))
            em.dve(lambda v, p=p: v.tensor_copy(sb[f"crelrow{p}"][0:1, 0:128:2], sb[f"crel64{p}"][:]),
                   ins=(P("crel64"),), outs=(P("crelrow"),))
            em.dve(lambda v, p=p: v.tensor_copy(sb[f"crelrow{p}"][0:1, 1:128:2], sb[f"crel64{p}"][:]),
                   ins=(P("crel64"),), outs=(P("crelrow"),))
            em.dve(lambda v, p=p: v.tensor_copy(sb[f"sE{p}"][0:1, 0:128:2], sb[f"bA{p}"][:]),
                   ins=(P("bA"),), outs=(P("brow"),))
            em.dve(lambda v, p=p: v.tensor_copy(sb[f"sE{p}"][0:1, 1:128:2], sb[f"bB{p}"][:]),
                   ins=(P("bB"),), outs=(P("brow"),))
            em.te(lambda te, p=p: te.matmul(pb["sm"][:, 0:1], sb[f"crelrow{p}"][:], sb["one1"][:],
                                            start=True, stop=True),
                  ins=(P("crelrow"), "one1"), outs=("sm",))
            em.act(lambda sc_, p=p: sc_.copy(sb[f"ccol{p}"][:], pb["sm"][:, 0:1]),
                   ins=("sm",), outs=(P("ccol"),))
            em.te(lambda te, p=p: te.matmul(pb["sm"][:, 0:1], sb[f"sE{p}"][0:1, 0:128], sb["one1"][:],
                                            start=True, stop=True),
                  ins=(P("brow"), "one1"), outs=("sm",))
            em.act(lambda sc_, p=p: sc_.copy(sb[f"bcol{p}"][:], pb["sm"][:, 0:1]),
                   ins=("sm",), outs=(P("bcol"),))
            em.te(lambda te, p=p: te.matmul(pb["sm"][:, 0:1], sb["ones_row"][:],
                                            sb[f"crelrow{p}"][0:1, 127:128], start=True, stop=True),
                  ins=(P("crelrow"), "ones_row"), outs=("sm",))
            em.act(lambda sc_, p=p: sc_.copy(sb[f"clastcol{p}"][:], pb["sm"][:, 0:1]),
                   ins=("sm",), outs=(P("clastcol"),))
            em.dve(lambda v, p=p: v.tensor_scalar(sb[f"negb{p}"][:], sb[f"bcol{p}"][:],
                                                  -1.0, None, ALU.mult),
                   ins=(P("bcol"),), outs=(P("negb"),))
            em.dve(lambda v, p=p: v.tensor_tensor(sb[f"farg{p}"][:], sb[f"clastcol{p}"][:],
                                                  sb[f"ccol{p}"][:], ALU.subtract),
                   ins=(P("clastcol"), P("ccol")), outs=(P("farg"),))
            em.act(lambda sc_, p=p: sc_.activation(sb[f"Gcol{p}"][:], sb[f"ccol{p}"][:], AF.Exp),
                   ins=(P("ccol"),), outs=(P("Gcol"),))
            em.act(lambda sc_, p=p: sc_.activation(sb[f"fcol{p}"][:], sb[f"farg{p}"][:], AF.Exp),
                   ins=(P("farg"),), outs=(P("fcol"),))
            em.act(lambda sc_, p=p: sc_.activation(sb[f"gamcol{p}"][:], sb[f"clastcol{p}"][:], AF.Exp),
                   ins=(P("clastcol"),), outs=(P("gamcol"),))
            em.dve(lambda v, p=p: v.tensor_tensor(sb[f"combo{p}"][:], sb[f"bcol{p}"][:],
                                                  sb[f"Gcol{p}"][:], ALU.mult),
                   ins=(P("bcol"), P("Gcol")), outs=(P("combo"),))
            em.te(lambda te, p=p: te.matmul(pb["mm0"][:, 0:128], sb[f"Kd{p}"][:], sb[f"Kd{p}"][:],
                                            start=True, stop=True),
                  ins=(P("Kd"),), outs=("mm0",))
            em.te(lambda te, p=p: te.matmul(pb["mm1"][:, 0:128], sb["ones_row"][:],
                                            sb[f"crelrow{p}"][:], start=True, stop=True),
                  ins=("ones_row", P("crelrow")), outs=("mm1",))
            em.dve(lambda v, p=p: v.tensor_scalar(sb[f"sEx{p}"][:], pb["mm1"][:, 0:128],
                                                  sb[f"ccol{p}"][:], 0.0, ALU.subtract, ALU.max),
                   ins=("mm1", P("ccol")), outs=(P("sEx"),))
            em.act(lambda sc_, p=p: sc_.activation(sb[f"sEm{p}"][:], sb[f"sEx{p}"][:],
                                                   AF.Exp, scale=-1.0),
                   ins=(P("sEx"),), outs=(P("sEm"),))
            em.dve(lambda v, p=p: v.tensor_tensor(sb[f"sEm{p}"][:], sb[f"sEm{p}"][:],
                                                  sb["maskSL"][:], ALU.mult),
                   ins=(P("sEm"), "maskSL"), outs=(P("sEm"),))
            em.dve(lambda v, p=p: v.scalar_tensor_tensor(
                sb[f"Npow{p}"][:, 0:128], pb["mm0"][:, 0:128], sb[f"negb{p}"][:],
                sb[f"sEm{p}"][:], ALU.mult, ALU.mult),
                ins=("mm0", P("negb"), P("sEm")), outs=(P("Npow"),))
            em.te(lambda te, p=p: te.transpose(pb["tr"][:, 0:128], sb[f"Npow{p}"][:, 0:128],
                                               sb["ident"][:]),
                  ins=(P("Npow"), "ident"), outs=("tr",))
            em.act(lambda sc_, p=p: sc_.copy(sb[f"Ppow{p}"][:, 0:128], pb["tr"][:, 0:128]),
                   ins=("tr",), outs=(P("Ppow"),))
            for k in range(6):
                em.te(lambda te, p=p, k=k: te.matmul(
                    pb["mm2"][:, 0:128], sb[f"Ppow{p}"][:, k * 128:(k + 1) * 128],
                    sb[f"Npow{p}"][:, k * 128:(k + 1) * 128], start=True, stop=True),
                    ins=(P("Ppow"), P("Npow")), outs=("mm2",))
                em.act(lambda sc_, p=p, k=k: sc_.copy(
                    sb[f"Npow{p}"][:, (k + 1) * 128:(k + 2) * 128], pb["mm2"][:, 0:128]),
                    ins=("mm2",), outs=(P("Npow"),))
                em.te(lambda te, p=p, k=k: te.matmul(
                    pb["mm3"][:, 0:128], sb[f"Npow{p}"][:, k * 128:(k + 1) * 128],
                    sb[f"Ppow{p}"][:, k * 128:(k + 1) * 128], start=True, stop=True),
                    ins=(P("Ppow"), P("Npow")), outs=("mm3",))
                em.act(lambda sc_, p=p, k=k: sc_.copy(
                    sb[f"Ppow{p}"][:, (k + 1) * 128:(k + 2) * 128], pb["mm3"][:, 0:128]),
                    ins=("mm3",), outs=(P("Ppow"),))
            em.te(lambda te, p=p: te.transpose(pb["tr"][:, 128:256], sb[f"Kd{p}"][:], sb["ident"][:]),
                  ins=(P("Kd"), "ident"), outs=("tr",))
            em.act(lambda sc_, p=p: sc_.copy(sb[f"Ks{p}"][:], pb["tr"][:, 128:256]),
                   ins=("tr",), outs=(P("Ks"),))
            em.te(lambda te, p=p: te.transpose(pb["tr"][:, 256:384], sb[f"Vd{p}"][:], sb["ident"][:]),
                  ins=(P("Vd"), "ident"), outs=("tr",))
            em.act(lambda sc_, p=p: sc_.copy(sb[f"Vs{p}"][:], pb["tr"][:, 256:384]),
                   ins=("tr",), outs=(P("Vs"),))
            em.dve(lambda v, p=p: v.tensor_scalar(sb[f"X{p}"][:, 0:128], sb[f"Vs{p}"][:],
                                                  sb[f"bcol{p}"][:], None, ALU.mult),
                   ins=(P("Vs"), P("bcol")), outs=(P("X"),))
            em.dve(lambda v, p=p: v.tensor_scalar(sb[f"X{p}"][:, 128:256], sb[f"Ks{p}"][:],
                                                  sb[f"combo{p}"][:], None, ALU.mult),
                   ins=(P("Ks"), P("combo")), outs=(P("X"),))
            for k in (6, 5, 4, 3, 2, 1, 0):
                em.te(lambda te, p=p, k=k: te.matmul(
                    pb["mm4"][:, 0:256], sb[f"Ppow{p}"][:, k * 128:(k + 1) * 128],
                    sb[f"X{p}"][:], start=True, stop=True),
                    ins=(P("Ppow"), P("X")), outs=("mm4",))
                em.dve(lambda v, p=p: v.tensor_tensor(sb[f"X{p}"][:], sb[f"X{p}"][:],
                                                      pb["mm4"][:, 0:256], ALU.add),
                       ins=(P("X"), "mm4"), outs=(P("X"),))
            em.dve(lambda v, p=p: v.tensor_scalar(sb[f"Ktil{p}"][:], sb[f"Ks{p}"][:],
                                                  sb[f"fcol{p}"][:], None, ALU.mult),
                   ins=(P("Ks"), P("fcol")), outs=(P("Ktil"),))
            em.dve(lambda v, p=p: v.tensor_copy(sb[f"coutc{p}"][:], sb[f"crelrow{p}"][0:1, 1:128:2]),
                   ins=(P("crelrow"),), outs=(P("coutc"),))
            em.te(lambda te, p=p: te.matmul(pb["mm0"][:, 0:64], sb[f"Kd{p}"][:], sb[f"Qd{p}"][:],
                                            start=True, stop=True),
                  ins=(P("Kd"), P("Qd")), outs=("mm0",))
            em.te(lambda te, p=p: te.matmul(pb["mm1"][:, 0:64], sb["ones_row"][:], sb[f"coutc{p}"][:],
                                            start=True, stop=True),
                  ins=("ones_row", P("coutc")), outs=("mm1",))
            em.dve(lambda v, p=p: v.tensor_scalar(sb[f"sD2{p}"][:], pb["mm1"][:, 0:64],
                                                  sb[f"ccol{p}"][:], 0.0, ALU.subtract, ALU.min),
                   ins=("mm1", P("ccol")), outs=(P("sD2"),))
            em.act(lambda sc_, p=p: sc_.activation(sb[f"sE2{p}"][:], sb[f"sD2{p}"][:], AF.Exp),
                   ins=(P("sD2"),), outs=(P("sE2"),))
            em.dve(lambda v, p=p: v.tensor_tensor(sb[f"sE2m{p}"][:], sb[f"sE2{p}"][:],
                                                  sb["mask64T"][:], ALU.mult),
                   ins=(P("sE2"), "mask64T"), outs=(P("sE2m"),))
            em.dve(lambda v, p=p: v.tensor_tensor(sb[f"A2T{p}"][:], pb["mm0"][:, 0:64],
                                                  sb[f"sE2m{p}"][:], ALU.mult),
                   ins=("mm0", P("sE2m")), outs=(P("A2T"),))
            em.te(lambda te, p=p: te.matmul(pb["mm2"][:, 0:64], sb[f"X{p}"][:, 128:256],
                                            sb[f"A2T{p}"][:], start=True, stop=True),
                  ins=(P("X"), P("A2T")), outs=("mm2",))
            em.dve(lambda v, p=p: v.tensor_scalar(sb[f"Zneg{p}"][:], pb["mm2"][:, 0:64],
                                                  -1.0, None, ALU.mult),
                   ins=("mm2",), outs=(P("Zneg"),))
            em.act(lambda sc_, p=p: sc_.activation(sb[f"goutr{p}"][:], sb[f"coutc{p}"][:], AF.Exp),
                   ins=(P("coutc"),), outs=(P("goutr"),))
            em.te(lambda te, p=p: te.matmul(pb["mm3"][:, 0:64], sb["ones_row"][:], sb[f"goutr{p}"][:],
                                            start=True, stop=True),
                  ins=("ones_row", P("goutr")), outs=("mm3",))
            em.dve(lambda v, p=p: v.tensor_tensor(sb[f"qbar{p}"][:], sb[f"Qd{p}"][:],
                                                  pb["mm3"][:, 0:64], ALU.mult),
                   ins=(P("Qd"), "mm3"), outs=(P("qbar"),))

            def omm(te, p=p, SBFL=SBFL):
                te.matmul(pb["mm5"][:, 0:64], sb[f"X{p}"][:, 0:128], sb[f"A2T{p}"][:],
                          start=True, stop=False)
                te.matmul(pb["mm5"][:, 0:64], sb[SBFL][:], sb[f"qbar{p}"][:],
                          start=False, stop=False)
                return te.matmul(pb["mm5"][:, 0:64], sb[SBFL][:], sb[f"Zneg{p}"][:],
                                 start=False, stop=True)
            em.op("te", "te", omm,
                  ins=(P("X"), P("A2T"), SBFL, P("qbar"), P("Zneg")), outs=("mm5",))
            em.te(lambda te, p=p: te.matmul(pb["mm0"][:, 0:128], sb[f"X{p}"][:, 128:256],
                                            sb[f"Ktil{p}"][:], start=True, stop=True),
                  ins=(P("X"), P("Ktil")), outs=("mm0",))
            em.act(lambda sc_, p=p: sc_.copy(sb[f"GTsb{p}"][:], pb["mm0"][:, 0:128]),
                   ins=("mm0",), outs=(P("GTsb"),))
            em.te(lambda te, p=p: te.matmul(pb["mm1"][:, 0:128], sb[f"Ktil{p}"][:],
                                            sb[f"X{p}"][:, 0:128], start=True, stop=True),
                  ins=(P("Ktil"), P("X")), outs=("mm1",))
            em.te(lambda te, p=p, SBFL=SBFL: te.matmul(pb["mm2"][:, 0:128], sb[f"GTsb{p}"][:],
                                                       sb[SBFL][:], start=True, stop=True),
                  ins=(P("GTsb"), SBFL), outs=("mm2",))
            em.dve(lambda v, p=p, SL=SL, l=l: v.scalar_tensor_tensor(
                sb[f"stmp{l}"][:], sb[SL][:], sb[f"gamcol{p}"][:], pb["mm2"][:, 0:128],
                ALU.mult, ALU.subtract),
                ins=(SL, P("gamcol"), "mm2"), outs=(f"stmp{l}",))
            em.dve(lambda v, p=p, SL=SL, l=l: v.tensor_tensor(
                sb[SL][:], sb[f"stmp{l}"][:], pb["mm1"][:, 0:128], ALU.add),
                ins=(f"stmp{l}", "mm1"), outs=(SL,))
            em.dve(lambda v, SL=SL, SBFL=SBFL: v.tensor_copy(sb[SBFL][:], sb[SL][:]),
                   ins=(SL,), outs=(SBFL,))
            em.act(lambda sc_, p=p: sc_.copy(sb[f"osb{p}"][:], pb["mm5"][:, 0:64]),
                   ins=("mm5",), outs=(P("osb"),))
            em.dve(lambda v, p=p: v.tensor_tensor(sb[f"osq{p}"][:], sb[f"osb{p}"][:],
                                                  sb[f"osb{p}"][:], ALU.mult),
                   ins=(P("osb"),), outs=(P("osq"),))
            em.te(lambda te, p=p: te.matmul(pb["sm"][0:1, 64:128], sb["ones_col"][:],
                                            sb[f"osq{p}"][:], start=True, stop=True),
                  ins=("ones_col", P("osq")), outs=("sm",))
            em.act(lambda sc_, p=p: sc_.activation(sb[f"r1r{p}"][:], pb["sm"][0:1, 64:128],
                                                   AF.Ln, scale=1.0 / 128.0, bias=EPS),
                   ins=("sm",), outs=(P("r1r"),))
            em.act(lambda sc_, p=p: sc_.activation(sb[f"r2r{p}"][:], sb[f"r1r{p}"][:],
                                                   AF.Exp, scale=-0.5),
                   ins=(P("r1r"),), outs=(P("r2r"),))
            em.te(lambda te, p=p: te.matmul(pb["mm4"][:, 0:64], sb["ones_row"][:], sb[f"r2r{p}"][:],
                                            start=True, stop=True),
                  ins=("ones_row", P("r2r")), outs=("mm4",))
            em.dve(lambda v, p=p: v.tensor_scalar(sb[f"t1o{p}"][:], sb[f"osb{p}"][:],
                                                  sb["normw"][:, 0:1], None, ALU.mult),
                   ins=(P("osb"), "normw"), outs=(P("t1o"),))
            em.dve(lambda v, p=p: v.tensor_tensor(sb[f"t2o{p}"][:], sb[f"t1o{p}"][:],
                                                  pb["mm4"][:, 0:64], ALU.mult),
                   ins=(P("t1o"), "mm4"), outs=(P("t2o"),))
            em.dma("scalar", lambda s, l=l, sc=sc, p=p: s.dma_start(
                out=sb[f"gch{p}"][:], in_=gatescr[l, :, sc * 64:(sc + 1) * 64]),
                ins=(f"gatescr{l}",), outs=(P("gch"),))
            em.dve(lambda v, p=p: v.tensor_tensor(
                sb[f"gout{p}"][:], sb[f"t2o{p}"][:], sb[f"gch{p}"][:], ALU.mult),
                ins=(P("t2o"), P("gch")), outs=(P("gout"),))
            em.dma("sync", lambda s, p=p, l=l, sc=sc: s.dma_start(
                out=t["e1_in"][l * 128:(l + 1) * 128, sc * 64:(sc + 1) * 64],
                in_=sb[f"gout{p}"][:]),
                ins=(P("gout"),), outs=("e1_in",))


def emit_A_wo(nc, em, t, sb, pb, L, resid0, resid0bf=None):
    em.cc(lambda g: g.collective_compute(
        "AllGather", ALU.bypass,
        replica_groups=[[0, 1, 2, 3], [4, 5, 6, 7]],
        ins=[t["e1_in"].opt()], outs=[t["e1_out"].opt()]),
        ins=("e1_in",), outs=("e1_out",))
    em.wait_all("pool", [em.ctx.buf("tqsel").w])
    LQ = L // 4
    W = 256
    NTC = LQ // W

    def dynld(g):
        return nc.values_load(sb["tqsel"][0:1, 0:1], engines=[Pool],
                              min_val=0, max_val=3, skip_runtime_bounds_check=True)
    for tc in range(NTC):
        em.dma("pool", lambda g, tc=tc: g.dma_start(
            out=sb["ogth"][:].rearrange("p (a w) -> p a w", a=8),
            in_=t["e1_out"].rearrange("(a p) t -> p a t", p=128)[
                :, :, bass.ts(dynld(g) * NTC + tc, W)]),
            ins=("e1_out",), outs=tuple(f"ogth{a}" for a in range(8)))
        oins = tuple(f"ogth{a}" for a in range(8))
        for ft in range(16):
            bank = ("mm0", "mm1")[ft % 2]
            em.dma("pool", lambda g, ft=ft: g.dma_start(
                out=sb["wo_sb"][:].rearrange("p (a o) -> p a o", a=8),
                in_=t["wo"][:, ft * 128:(ft + 1) * 128].rearrange("(a p) o -> p a o", p=128)),
                outs=("wo_sb",))

            def mmwo(te, ft=ft, bank=bank):
                last = None
                for a in range(8):
                    last = te.matmul(pb[bank][:, 0:W],
                                     sb["wo_sb"][:, a * 128:(a + 1) * 128],
                                     sb["ogth"][:, a * W:(a + 1) * W],
                                     start=(a == 0), stop=(a == 7))
                return last
            em.op("te", "te", mmwo, ins=("wo_sb",) + oins, outs=(bank,))
            em.dma("sync", lambda s, ft=ft, tc=tc: s.dma_start(
                out=sb["xres"][:],
                in_=t["xq"][ft * 128:(ft + 1) * 128, tc * W:(tc + 1) * W]),
                ins=(), outs=("xres",))
            em.dve(lambda v, ft=ft, bank=bank: v.tensor_tensor(
                sb["preLN"][:, ft * W:(ft + 1) * W], pb[bank][:, 0:W],
                sb["xres"][:], ALU.add),
                ins=(bank, "xres"), outs=(f"preLN{ft}",))
        emit_LN(nc, em, sb, pb, ntiles=16, width=W,
                consume=mk_r0_writer(em, sb, t, resid0, resid0bf, tc, W))
    return


def mk_r0_writer(em, sb, t, resid0, resid0bf, tc, W):
    def consume(ft):
        half = ft % 2
        em.dma("sync", lambda s, ft=ft, tc=tc: s.dma_start(
            out=resid0[ft * 128:(ft + 1) * 128, tc * W:(tc + 1) * W],
            in_=sb["outt"][:, (ft % 2) * W:(ft % 2 + 1) * W]),
            ins=(f"outt{half}",), outs=(f"r0out_{ft}_{tc}",))
        if resid0bf is not None:
            em.dma("pool", lambda g, ft=ft, tc=tc: g.dma_start(
                out=resid0bf[ft * 128:(ft + 1) * 128, tc * W:(tc + 1) * W],
                in_=sb["outt"][:, (ft % 2) * W:(ft % 2 + 1) * W]),
                ins=(f"outt{half}",), outs=(f"r0bf_{ft}_{tc}",))
    return consume


def emit_LN(nc, em, sb, pb, ntiles, width, consume):
    for ft in range(ntiles):
        em.dve(lambda v, ft=ft: v.tensor_copy(sb["pbf"][:],
                                              sb["preLN"][:, ft * width:(ft + 1) * width]),
               ins=(f"preLN{ft}",), outs=("pbf",))
        em.act(lambda sc_, ft=ft: sc_.activation(sb["sqbf"][:],
                                                 sb["preLN"][:, ft * width:(ft + 1) * width],
                                                 AF.Square),
               ins=(f"preLN{ft}",), outs=("sqbf",))
        em.te(lambda te, ft=ft: te.matmul(pb["sm"][0:1, 0:width], sb["onescol_b"][:],
                                          sb["pbf"][:], start=(ft == 0), stop=(ft == ntiles - 1)),
              ins=("pbf", "onescol_b"), outs=("sm",))
        em.te(lambda te, ft=ft: te.matmul(pb["mm5"][0:1, 0:width], sb["onescol_b"][:],
                                          sb["sqbf"][:], start=(ft == 0), stop=(ft == ntiles - 1)),
              ins=("sqbf", "onescol_b"), outs=("mm5",))
    n = float(ntiles * 128)
    em.dve(lambda v: v.tensor_scalar(sb["mrow"][0:1, 0:width], pb["sm"][0:1, 0:width],
                                     1.0 / n, None, ALU.mult),
           ins=("sm",), outs=("mrow",))
    em.dve(lambda v: v.tensor_tensor(sb["m2row"][0:1, 0:width], sb["mrow"][0:1, 0:width],
                                     sb["mrow"][0:1, 0:width], ALU.mult),
           ins=("mrow",), outs=("m2row",))
    em.dve(lambda v: v.tensor_scalar(sb["v1row"][0:1, 0:width], pb["mm5"][0:1, 0:width],
                                     1.0 / n, None, ALU.mult),
           ins=("mm5",), outs=("v1row",))
    em.dve(lambda v: v.tensor_tensor(sb["varrow"][0:1, 0:width], sb["v1row"][0:1, 0:width],
                                     sb["m2row"][0:1, 0:width], ALU.subtract),
           ins=("v1row", "m2row"), outs=("varrow",))
    em.act(lambda sc_: sc_.activation(sb["lnr1"][0:1, 0:width], sb["varrow"][0:1, 0:width],
                                      AF.Ln, bias=EPS),
           ins=("varrow",), outs=("lnr1",))
    em.act(lambda sc_: sc_.activation(sb["lnr2"][0:1, 0:width], sb["lnr1"][0:1, 0:width],
                                      AF.Exp, scale=-0.5),
           ins=("lnr1",), outs=("lnr2",))
    em.te(lambda te: te.matmul(pb["mm2"][:, 0:width], sb["ones_row"][:],
                               sb["mrow"][0:1, 0:width], start=True, stop=True),
          ins=("ones_row", "mrow"), outs=("mm2",))
    em.te(lambda te: te.matmul(pb["mm3"][:, 0:width], sb["ones_row"][:],
                               sb["lnr2"][0:1, 0:width], start=True, stop=True),
          ins=("ones_row", "lnr2"), outs=("mm3",))
    for ft in range(ntiles):
        half = ft % 2
        em.dve(lambda v, ft=ft: v.tensor_tensor(sb["xres"][:],
                                                sb["preLN"][:, ft * width:(ft + 1) * width],
                                                pb["mm2"][:, 0:width], ALU.subtract),
               ins=(f"preLN{ft}", "mm2"), outs=("xres",))
        em.dve(lambda v, half=half: v.tensor_tensor(sb["outt"][:, half * width:(half + 1) * width],
                                                    sb["xres"][:], pb["mm3"][:, 0:width],
                                                    ALU.mult),
               ins=("xres", "mm3"), outs=(f"outt{half}",))
        consume(ft)



def host_prep_B(inputs, L=8192):
    wg = np.asarray(inputs["L0_ffn_w_gate"])
    wu = np.asarray(inputs["L0_ffn_w_up"])
    wd = np.asarray(inputs["L0_ffn_w_down"])
    sg = np.asarray(inputs["L0_ffn_s_gate"]).astype(np.float32)
    su = np.asarray(inputs["L0_ffn_s_up"]).astype(np.float32)
    sd = np.asarray(inputs["L0_ffn_s_down"]).astype(np.float32)
    consts = host_constants()
    maps = []
    for c in range(N_CORES):
        m = {}
        m["wg8"] = wg
        m["wu8"] = wu
        m["wd8"] = wd
        m["sgate"] = np.ascontiguousarray(sg.reshape(64, 128).T)
        m["sup"] = np.ascontiguousarray(su.reshape(64, 128).T)
        m["sdown"] = np.ascontiguousarray(sd.reshape(16, 128).T)
        for k in ("ones_row", "ones_col", "cmisc"):
            m[k] = consts[k]
        maps.append(m)
    return maps


def declare_B(nc, L):
    t = {}
    t["wg8"] = nc.dram_tensor("wg8", [HID, 8192], I8, kind="ExternalInput").ap()
    t["wu8"] = nc.dram_tensor("wu8", [HID, 8192], I8, kind="ExternalInput").ap()
    t["wd8"] = nc.dram_tensor("wd8", [8192, HID], I8, kind="ExternalInput").ap()
    t["sgate"] = nc.dram_tensor("sgate", [128, 64], F32, kind="ExternalInput").ap()
    t["sup"] = nc.dram_tensor("sup", [128, 64], F32, kind="ExternalInput").ap()
    t["sdown"] = nc.dram_tensor("sdown", [128, 16], F32, kind="ExternalInput").ap()
    for nme, shp in (("ones_row", [1, 128]), ("ones_col", [128, 1]),
                     ("cmisc", [128, len(CONST_VALS)])):
        t[nme] = nc.dram_tensor(nme, shp, F32, kind="ExternalInput").ap()
    return t


def alloc_B(nc, stack, L):
    sb = {}

    def A(name, shape, dt):
        sb[name] = stack.enter_context(nc.sbuf_tensor("sbB_" + name, shape, dt))
    A("cmisc", [128, len(CONST_VALS)], F32)
    A("ones_row", [1, 128], F32)
    A("ones_col", [128, 1], F32)
    A("onescol_b", [128, 1], BF16)
    A("sgate", [128, 64], F32)
    A("sup", [128, 64], F32)
    A("sdown", [128, 16], F32)
    A("r0f", [128, 16 * 512], F32)
    A("r0b", [128, 16 * 512], BF16)
    A("wgt", [128, 2 * 2048], BF16)
    A("wut", [128, 2 * 2048], BF16)
    A("wdt", [128, 2 * 768], BF16)
    A("ybuf", [128, 64 * 512], BF16)
    A("gtile", [128, 512], BF16)
    A("utile", [128, 512], BF16)
    A("preLN", [128, 16 * 512], F32)
    A("pbf", [128, 512], BF16)
    A("sqbf", [128, 512], BF16)
    A("xres", [128, 512], F32)
    A("outt", [128, 1024], F32)
    for nme in ("mrow", "m2row", "v1row", "varrow", "lnr1", "lnr2"):
        A(nme, [1, 512], F32)
    return sb


def emit_const_load(nc, em, t, sb, names=("ones_row", "ones_col", "cmisc")):
    for nme in names:
        em.dma("sync", lambda s, nme=nme: s.dma_start(out=sb[nme][:], in_=t[nme][:]),
               outs=(nme,))
    if "onescol_b" in sb:
        em.dma("pool", lambda g: g.dma_start(out=sb["onescol_b"][:], in_=t["ones_col"][:]),
               outs=("onescol_b",))
    for i, vv in enumerate(CONST_VALS):
        if vv in (0.0, 1.0, -1.0, 2.0, 0.5):
            continue
        try:
            nc.const_aps.register(F32, vv, sb["cmisc"][:, i:i + 1])
        except Exception:
            pass
    tok = em.ctx.buf("cmisc").w
    for e in ("act", "dve", "te", "pool", "sync", "scalar"):
        em.wait_all(e, [tok])


def emit_B(nc, em, t, sb, pb, L, resid0_in, resid1_out, resid1bf_out=None):
    LQ = L // 4
    NTC = LQ // 512
    emit_const_load(nc, em, t, sb)
    for nme in ("sgate", "sup", "sdown"):
        em.dma("sync", lambda s, nme=nme: s.dma_start(out=sb[nme][:], in_=t[nme][:]),
               outs=(nme,))
    for tc in range(NTC):
        cs = slice(tc * 512, (tc + 1) * 512)
        for ft in range(16):
            em.dma("sync", lambda s, ft=ft, cs=cs: s.dma_start(
                out=sb["r0f"][:, ft * 512:(ft + 1) * 512],
                in_=resid0_in[ft * 128:(ft + 1) * 128, cs]), outs=(f"r0f{ft}",))
            em.dma("pool", lambda g, ft=ft, cs=cs: g.dma_start(
                out=sb["r0b"][:, ft * 512:(ft + 1) * 512],
                in_=resid0_in[ft * 128:(ft + 1) * 128, cs]), outs=(f"r0b{ft}",))
        rins = tuple(f"r0b{ft}" for ft in range(16))
        for ot in range(64):
            par = ot % 2
            em.dma("pool", lambda g, ot=ot, par=par: g.dma_start(
                out=sb["wgt"][:, par * 2048:(par + 1) * 2048].rearrange("p (a o) -> p a o", a=16),
                in_=t["wg8"][:, ot * 128:(ot + 1) * 128].rearrange("(a p) o -> p a o", p=128)),
                outs=(f"wgt{par}",))
            em.dma("pool", lambda g, ot=ot, par=par: g.dma_start(
                out=sb["wut"][:, par * 2048:(par + 1) * 2048].rearrange("p (a o) -> p a o", a=16),
                in_=t["wu8"][:, ot * 128:(ot + 1) * 128].rearrange("(a p) o -> p a o", p=128)),
                outs=(f"wut{par}",))

            def mmg(te, par=par):
                last = None
                for a in range(16):
                    last = te.matmul(pb["mm0"][:, 0:512],
                                     sb["wgt"][:, par * 2048 + a * 128: par * 2048 + (a + 1) * 128],
                                     sb["r0b"][:, a * 512:(a + 1) * 512],
                                     start=(a == 0), stop=(a == 15))
                return last
            em.op("te", "te", mmg, ins=(f"wgt{par}",) + rins, outs=("mm0",))

            def mmu(te, par=par):
                last = None
                for a in range(16):
                    last = te.matmul(pb["mm1"][:, 0:512],
                                     sb["wut"][:, par * 2048 + a * 128: par * 2048 + (a + 1) * 128],
                                     sb["r0b"][:, a * 512:(a + 1) * 512],
                                     start=(a == 0), stop=(a == 15))
                return last
            em.op("te", "te", mmu, ins=(f"wut{par}",) + rins, outs=("mm1",))
            em.act(lambda sc_, ot=ot: sc_.activation(sb["gtile"][:], pb["mm0"][:, 0:512],
                                                     AF.Silu, scale=sb["sgate"][:, ot:ot + 1]),
                   ins=("mm0", "sgate"), outs=("gtile",))
            em.dve(lambda v, ot=ot: v.tensor_scalar(sb["utile"][:], pb["mm1"][:, 0:512],
                                                    sb["sup"][:, ot:ot + 1], None, ALU.mult),
                   ins=("mm1", "sup"), outs=("utile",))
            em.dve(lambda v, ot=ot: v.tensor_tensor(sb["ybuf"][:, ot * 512:(ot + 1) * 512],
                                                    sb["gtile"][:], sb["utile"][:], ALU.mult),
                   ins=("gtile", "utile"), outs=(f"yb{ot}",))
        yins = tuple(f"yb{ot}" for ot in range(64))
        groups = [(0, 6), (6, 12), (12, 16)]
        for g0, g1 in groups:
            ncols = (g1 - g0) * 128
            for kt in range(64):
                par = kt % 2
                em.dma("pool", lambda g, kt=kt, g0=g0, ncols=ncols, par=par: g.dma_start(
                    out=sb["wdt"][:, par * 768: par * 768 + ncols],
                    in_=t["wd8"][kt * 128:(kt + 1) * 128, g0 * 128: g0 * 128 + ncols]),
                    outs=(f"wdt{par}",))

                def mmd(te, kt=kt, g0=g0, g1=g1, par=par):
                    last = None
                    for i, o2 in enumerate(range(g0, g1)):
                        last = te.matmul(pb[f"mm{i}"][:, 0:512],
                                         sb["wdt"][:, par * 768 + i * 128: par * 768 + (i + 1) * 128],
                                         sb["ybuf"][:, kt * 512:(kt + 1) * 512],
                                         start=(kt == 0), stop=(kt == 63))
                    return last
                em.op("te", "te", mmd, ins=(f"wdt{par}",) + yins,
                      outs=tuple(f"mm{i}" for i in range(g1 - g0)))
            for i, o2 in enumerate(range(g0, g1)):
                em.dve(lambda v, i=i, o2=o2: v.scalar_tensor_tensor(
                    sb["preLN"][:, o2 * 512:(o2 + 1) * 512], pb[f"mm{i}"][:, 0:512],
                    sb["sdown"][:, o2:o2 + 1], sb["r0f"][:, o2 * 512:(o2 + 1) * 512],
                    ALU.mult, ALU.add),
                    ins=(f"mm{i}", "sdown", f"r0f{o2}"), outs=(f"preLN{o2}",))
        emit_LN(nc, em, sb, pb, ntiles=16, width=512,
                consume=mk_ln_writer(em, sb, resid1_out, resid1bf_out, tc, 512, "r1o"))


def mk_ln_writer(em, sb, out_ap, outbf_ap, tc, W, tag):
    def consume(ft):
        half = ft % 2
        em.dma("sync", lambda s, ft=ft, tc=tc: s.dma_start(
            out=out_ap[ft * 128:(ft + 1) * 128, tc * W:(tc + 1) * W],
            in_=sb["outt"][:, (ft % 2) * W:(ft % 2 + 1) * W]),
            ins=(f"outt{half}",), outs=(f"{tag}_{ft}_{tc}",))
        if outbf_ap is not None:
            em.dma("pool", lambda g, ft=ft, tc=tc: g.dma_start(
                out=outbf_ap[ft * 128:(ft + 1) * 128, tc * W:(tc + 1) * W],
                in_=sb["outt"][:, (ft % 2) * W:(ft % 2 + 1) * W]),
                ins=(f"outt{half}",), outs=(f"{tag}bf_{ft}_{tc}",))
    return consume



def host_prep_C(inputs, L=8192):
    Wq = np.asarray(inputs["L1_attn_Wq"]).astype(np.float32)
    Wk = np.asarray(inputs["L1_attn_Wk"]).astype(np.float32)
    Wv = np.asarray(inputs["L1_attn_Wv"]).astype(np.float32)
    Wo = np.asarray(inputs["L1_attn_Wo"]).astype(np.float32)
    consts = host_constants()
    NTQ = L // 512
    cm = np.zeros((128, 4 * 512), dtype=np.float32)
    tk = np.arange(128)[:, None]
    tq = np.arange(512)[None, :]
    for j in range(4):
        cm[:, j * 512:(j + 1) * 512] = ((tk + 128 * j) <= tq).astype(np.float32)
    denoff = np.zeros((1, NTQ), dtype=np.float32)
    for c in range(NTQ):
        denoff[0, c] = float(L - (c + 1) * 512)
    maps = []
    for c in range(N_CORES):
        h0 = 4 * (c % 4)
        m = {}
        cols = []
        for W in (Wq, Wk, Wv):
            for h in range(h0, h0 + 4):
                cols.append(W[:, h * 128:(h + 1) * 128])
        m["wqkvC"] = np.ascontiguousarray(np.concatenate(cols, axis=1))
        m["woC"] = Wo
        m["cmask"] = cm
        m["denoff"] = denoff
        m["tqselC"] = np.array([[c % 4]], dtype=np.uint32)
        m["identC"] = np.eye(128, dtype=np.float32)
        for k in ("ones_row", "ones_col", "cmisc"):
            m[k + "C"] = consts[k]
        maps.append(m)
    return maps


def declare_C(nc, L):
    t = {}
    t["wqkvC"] = nc.dram_tensor("wqkvC", [HID, 1536], F32, kind="ExternalInput").ap()
    t["woC"] = nc.dram_tensor("woC", [HID, HID], F32, kind="ExternalInput").ap()
    t["cmask"] = nc.dram_tensor("cmask", [128, 2048], F32, kind="ExternalInput").ap()
    t["denoff"] = nc.dram_tensor("denoff", [1, L // 512], F32, kind="ExternalInput").ap()
    t["tqsel"] = nc.dram_tensor("tqselC", [1, 1], U32, kind="ExternalInput").ap()
    t["identC"] = nc.dram_tensor("identC", [128, 128], F32, kind="ExternalInput").ap()
    for nme, shp in (("ones_row", [1, 128]), ("ones_col", [128, 1]),
                     ("cmisc", [128, len(CONST_VALS)])):
        t[nme] = nc.dram_tensor(nme + "C", shp, F32, kind="ExternalInput").ap()
    t["r1g_in"] = nc.dram_tensor("r1g_in", [HID, L // 4], BF16, kind="Internal").ap()
    t["r1g"] = nc.dram_tensor("r1g", [4 * HID, L // 4], BF16, kind="Internal").ap()
    t["qkvscr"] = nc.dram_tensor("qkvscrC", [12, 128, L], BF16, kind="Internal").ap()
    t["e3_in"] = nc.dram_tensor("e3_in", [512, L], BF16, kind="Internal").ap()
    t["e3_out"] = nc.dram_tensor("e3_out", [2048, L], BF16, kind="Internal").ap()
    return t


def alloc_C(nc, stack, L):
    sb = {}

    def A(name, shape, dt):
        sb[name] = stack.enter_context(nc.sbuf_tensor("sbC_" + name, shape, dt))
    A("cmisc", [128, len(CONST_VALS)], F32)
    A("ones_row", [1, 128], F32)
    A("ones_col", [128, 1], F32)
    A("onescol_b", [128, 1], BF16)
    A("wqkv", [128, 16 * 1536], BF16)
    A("cmask", [128, 2048], F32)
    A("denoff", [1, L // 512], F32)
    A("tqsel", [1, 1], U32)
    A("identC", [128, 128], BF16)
    A("rch", [128, 16 * 512], BF16)
    A("stg", [128, 512], BF16)
    A("ptile", [128, 512], BF16)
    A("kbuf", [128, L], BF16)
    A("vbuf", [128, L], BF16)
    A("qbuf", [128, L], BF16)
    A("vtb", [128, L], BF16)
    A("vsums", [128, 16], F32)
    A("vpref", [128, 16], F32)
    A("negsuf", [128, 16], F32)
    A("vtot", [128, 1], F32)
    A("zeros16", [128, 16], F32)
    A("sE", [128, 512], F32)
    A("E0", [128, 512], BF16)
    A("E1", [128, 512], BF16)
    A("numo", [128, 512], F32)
    A("oatt", [128, 512], BF16)
    A("denr", [1, 512], F32)
    A("brec", [1, 512], F32)
    A("wo_sb", [128, 16 * 128], BF16)
    A("ogth", [128, 16 * 256], BF16)
    A("xres", [128, 256], F32)
    A("preLN", [128, 16 * 256], F32)
    A("pbf", [128, 256], BF16)
    A("sqbf", [128, 256], BF16)
    A("outt", [128, 512], F32)
    for nme in ("mrow", "m2row", "v1row", "varrow", "lnr1", "lnr2"):
        A(nme, [1, 256], F32)
    return sb


def emit_C(nc, em, t, sb, pb, L, r1f_in, r2_out, r2bf_out=None, invsq_idx=12):
    LQ = L // 4
    NTCq = LQ // 512
    NTQ = L // 512
    NTK = L // 128
    emit_const_load(nc, em, t, sb)
    em.dma("pool", lambda g: g.dma_start(
        out=sb["wqkv"][:].rearrange("p (a o) -> p a o", a=16),
        in_=t["wqkvC"].rearrange("(a p) o -> p a o", p=128)), outs=("wqkv",))
    em.dma("sync", lambda s: s.dma_start(out=sb["cmask"][:], in_=t["cmask"][:]), outs=("cmask",))
    em.dma("sync", lambda s: s.dma_start(out=sb["denoff"][:], in_=t["denoff"][:]), outs=("denoff",))
    em.dma("sync", lambda s: s.dma_start(out=sb["tqsel"][:], in_=t["tqsel"][:]), outs=("tqsel",))
    em.dma("pool", lambda g: g.dma_start(out=sb["identC"][:], in_=t["identC"][:]), outs=("identC",))
    for ft in range(16):
        for tcq in range(NTCq):
            em.dma("pool", lambda g, ft=ft, tcq=tcq: g.dma_start(
                out=sb["stg"][:], in_=r1f_in[ft * 128:(ft + 1) * 128, tcq * 512:(tcq + 1) * 512]),
                outs=("stg",))
            em.dma("sync", lambda s, ft=ft, tcq=tcq: s.dma_start(
                out=t["r1g_in"][ft * 128:(ft + 1) * 128, tcq * 512:(tcq + 1) * 512],
                in_=sb["stg"][:]), ins=("stg",), outs=("r1g_in",))
    em.cc(lambda g: g.collective_compute(
        "AllGather", ALU.bypass,
        replica_groups=[[0, 1, 2, 3], [4, 5, 6, 7]],
        ins=[t["r1g_in"].opt()], outs=[t["r1g"].opt()]),
        ins=("r1g_in",), outs=("r1g",))
    for r in range(4):
        for tcq in range(NTCq):
            for ft in range(16):
                em.dma("pool", lambda g, r=r, tcq=tcq, ft=ft: g.dma_start(
                    out=sb["rch"][:, ft * 512:(ft + 1) * 512],
                    in_=t["r1g"][r * HID + ft * 128: r * HID + (ft + 1) * 128,
                                 tcq * 512:(tcq + 1) * 512]),
                    ins=("r1g",), outs=(f"rch{ft}",))
            rins = tuple(f"rch{ft}" for ft in range(16))
            for ot in range(12):
                bank = ("mm0", "mm1")[ot % 2]

                def mmp(te, ot=ot, bank=bank):
                    last = None
                    for a in range(16):
                        last = te.matmul(pb[bank][:, 0:512],
                                         sb["wqkv"][:, a * 1536 + ot * 128: a * 1536 + (ot + 1) * 128],
                                         sb["rch"][:, a * 512:(a + 1) * 512],
                                         start=(a == 0), stop=(a == 15))
                    return last
                em.op("te", "te", mmp, ins=("wqkv",) + rins, outs=(bank,))
                em.act(lambda sc_, bank=bank: sc_.copy(sb["ptile"][:], pb[bank][:, 0:512]),
                       ins=(bank,), outs=("ptile",))
                em.dma("sync", lambda s, ot=ot, r=r, tcq=tcq: s.dma_start(
                    out=t["qkvscr"][ot, :, r * LQ + tcq * 512: r * LQ + (tcq + 1) * 512],
                    in_=sb["ptile"][:]), ins=("ptile",), outs=(f"qkv{ot}",))
    for hi in range(4):
        em.dma("pool", lambda g, hi=hi: g.dma_start(out=sb["qbuf"][:], in_=t["qkvscr"][hi]),
               ins=(f"qkv{hi}",), outs=("qbuf",))
        em.dma("pool", lambda g, hi=hi: g.dma_start(out=sb["kbuf"][:], in_=t["qkvscr"][4 + hi]),
               ins=(f"qkv{4 + hi}",), outs=("kbuf",))
        em.dma("pool", lambda g, hi=hi: g.dma_start(out=sb["vbuf"][:], in_=t["qkvscr"][8 + hi]),
               ins=(f"qkv{8 + hi}",), outs=("vbuf",))
        for j in range(NTK):
            half = j % 2
            em.te(lambda te, j=j, half=half: te.transpose(
                pb["tr"][:, half * 128:(half + 1) * 128],
                sb["vbuf"][:, j * 128:(j + 1) * 128], sb["identC"][:]),
                ins=("vbuf", "identC"), outs=(f"trh{half}",))
            em.act(lambda sc_, j=j, half=half: sc_.copy(
                sb["vtb"][:, j * 128:(j + 1) * 128], pb["tr"][:, half * 128:(half + 1) * 128]),
                ins=(f"trh{half}",), outs=("vtb",))
        em.dve(lambda v: v.memset(sb["zeros16"][:], 0.0), outs=("zeros16",))
        for cch in range(NTQ):
            em.dve(lambda v, cch=cch: v.reduce_sum(
                sb["vsums"][:, cch:cch + 1], sb["vbuf"][:, cch * 512:(cch + 1) * 512],
                mybir.AxisListType.X), ins=("vbuf",), outs=("vsums",))
        em.dve(lambda v: v.tensor_tensor_scan(
            sb["vpref"][:, 0:NTQ], sb["vsums"][:, 0:NTQ], sb["zeros16"][:, 0:NTQ], 0.0,
            ALU.add, ALU.add), ins=("vsums", "zeros16"), outs=("vpref",))
        em.dve(lambda v: v.tensor_copy(sb["vtot"][:], sb["vpref"][:, NTQ - 1:NTQ]),
               ins=("vpref",), outs=("vtot",))
        em.dve(lambda v: v.tensor_scalar(sb["negsuf"][:, 0:NTQ], sb["vpref"][:, 0:NTQ],
                                         sb["vtot"][:, 0:1], None, ALU.subtract),
               ins=("vpref", "vtot"), outs=("negsuf",))
        for cq in range(NTQ):
            ntk = 4 * (cq + 1)
            for tkt in range(ntk):
                sbank = ("mm0", "mm1")[tkt % 2]
                em.te(lambda te, tkt=tkt, cq=cq, sbank=sbank: te.matmul(
                    pb[sbank][:, 0:512], sb["kbuf"][:, tkt * 128:(tkt + 1) * 128],
                    sb["qbuf"][:, cq * 512:(cq + 1) * 512], start=True, stop=True),
                    ins=("kbuf", "qbuf"), outs=(sbank,))
                eb = ("E0", "E1")[tkt % 2]
                if tkt >= 4 * cq:
                    j = tkt - 4 * cq
                    em.dve(lambda v, j=j, sbank=sbank: v.tensor_tensor(
                        sb["sE"][:], pb[sbank][:, 0:512],
                        sb["cmask"][:, j * 512:(j + 1) * 512], ALU.mult),
                        ins=(sbank, "cmask"), outs=("sE",))
                    em.act(lambda sc_, eb=eb: sc_.activation(
                        sb[eb][:], sb["sE"][:], AF.Exp, scale=sb["cmisc"][:, invsq_idx:invsq_idx + 1]),
                        ins=("sE",), outs=(eb,))
                else:
                    em.act(lambda sc_, eb=eb, sbank=sbank: sc_.activation(
                        sb[eb][:], pb[sbank][:, 0:512], AF.Exp,
                        scale=sb["cmisc"][:, invsq_idx:invsq_idx + 1]),
                        ins=(sbank,), outs=(eb,))
                em.te(lambda te, tkt=tkt, eb=eb: te.matmul(
                    pb["mm2"][:, 0:512], sb["vtb"][:, tkt * 128:(tkt + 1) * 128], sb[eb][:],
                    start=(tkt == 0), stop=(tkt == ntk - 1)),
                    ins=("vtb", eb), outs=("mm2",) if tkt == ntk - 1 else ("mm2",))
                em.te(lambda te, tkt=tkt, eb=eb: te.matmul(
                    pb["mm3"][0:1, 0:512], sb["onescol_b"][:], sb[eb][:],
                    start=(tkt == 0), stop=(tkt == ntk - 1)),
                    ins=("onescol_b", eb), outs=("mm3",))
            em.dve(lambda v, cq=cq: v.tensor_scalar(
                sb["denr"][0:1, 0:512], pb["mm3"][0:1, 0:512],
                sb["denoff"][0:1, cq:cq + 1], None, ALU.add),
                ins=("mm3", "denoff"), outs=("denr",))
            em.dve(lambda v: v.reciprocal(sb["brec"][0:1, 0:512], sb["denr"][0:1, 0:512]),
                   ins=("denr",), outs=("brec",))
            em.te(lambda te: te.matmul(pb["mm4"][:, 0:512], sb["ones_row"][:],
                                       sb["brec"][0:1, 0:512], start=True, stop=True),
                  ins=("ones_row", "brec"), outs=("mm4",))
            em.dve(lambda v, cq=cq: v.tensor_scalar(
                sb["numo"][:], pb["mm2"][:, 0:512], sb["negsuf"][:, cq:cq + 1], None,
                ALU.subtract), ins=("mm2", "negsuf"), outs=("numo",))
            em.dve(lambda v: v.tensor_tensor(sb["oatt"][:], sb["numo"][:],
                                             pb["mm4"][:, 0:512], ALU.mult),
                   ins=("numo", "mm4"), outs=("oatt",))
            em.dma("sync", lambda s, hi=hi, cq=cq: s.dma_start(
                out=t["e3_in"][hi * 128:(hi + 1) * 128, cq * 512:(cq + 1) * 512],
                in_=sb["oatt"][:]), ins=("oatt",), outs=("e3_in",))
    em.cc(lambda g: g.collective_compute(
        "AllGather", ALU.bypass,
        replica_groups=[[0, 1, 2, 3], [4, 5, 6, 7]],
        ins=[t["e3_in"].opt()], outs=[t["e3_out"].opt()]),
        ins=("e3_in",), outs=("e3_out",))
    em.wait_all("sync", [em.ctx.buf("tqsel").w])
    W = 256
    NTC = LQ // W

    def dynldC(s):
        return nc.values_load(sb["tqsel"][0:1, 0:1], engines=[mybir.EngineType.SP],
                              min_val=0, max_val=3, skip_runtime_bounds_check=True)
    for tc in range(NTC):
        em.dma("sync", lambda s, tc=tc: s.dma_start(
            out=sb["ogth"][:].rearrange("p (a w) -> p a w", a=16),
            in_=t["e3_out"].rearrange("(a p) t -> p a t", p=128)[
                :, :, bass.ts(dynldC(s) * NTC + tc, W)]),
            ins=("e3_out",), outs=tuple(f"ogth{a}" for a in range(16)))
        oins = tuple(f"ogth{a}" for a in range(16))
        for ft in range(16):
            bank = ("mm0", "mm1")[ft % 2]
            em.dma("pool", lambda g, ft=ft: g.dma_start(
                out=sb["wo_sb"][:].rearrange("p (a o) -> p a o", a=16),
                in_=t["woC"][:, ft * 128:(ft + 1) * 128].rearrange("(a p) o -> p a o", p=128)),
                outs=("wo_sb",))

            def mmwo(te, ft=ft, bank=bank):
                last = None
                for a in range(16):
                    last = te.matmul(pb[bank][:, 0:W],
                                     sb["wo_sb"][:, a * 128:(a + 1) * 128],
                                     sb["ogth"][:, a * W:(a + 1) * W],
                                     start=(a == 0), stop=(a == 15))
                return last
            em.op("te", "te", mmwo, ins=("wo_sb",) + oins, outs=(bank,))
            em.dma("scalar", lambda s, ft=ft, tc=tc: s.dma_start(
                out=sb["xres"][:],
                in_=r1f_in[ft * 128:(ft + 1) * 128, tc * W:(tc + 1) * W]),
                ins=(), outs=("xres",))
            em.dve(lambda v, ft=ft, bank=bank: v.tensor_tensor(
                sb["preLN"][:, ft * W:(ft + 1) * W], pb[bank][:, 0:W],
                sb["xres"][:], ALU.add),
                ins=(bank, "xres"), outs=(f"preLN{ft}",))
        emit_LN(nc, em, sb, pb, ntiles=16, width=W,
                consume=mk_ln_writer(em, sb, r2_out, r2bf_out, tc, W, "r2o"))



def _sylvester(n):
    H = np.array([[1.0]], dtype=np.float32)
    while H.shape[0] < n:
        H = np.block([[H, H], [H, -H]])
    return H.astype(np.float32)


def host_prep_D(inputs, L=8192):
    SU_ug = np.asarray(inputs["L1_ffn_SU_ug"]).astype(np.float32)
    SU_dp = np.asarray(inputs["L1_ffn_SU_dp"]).astype(np.float32)
    Wsug = np.asarray(inputs["L1_ffn_Wscale_ug"]).astype(np.float32)
    Wsdp = np.asarray(inputs["L1_ffn_Wscale_dp"]).astype(np.float32)
    W_up = np.asarray(inputs["L1_ffn_W_up"]).astype(np.float32)
    W_gate = np.asarray(inputs["L1_ffn_W_gate"]).astype(np.float32)
    W_down = np.asarray(inputs["L1_ffn_W_down"]).astype(np.float32)
    consts = host_constants()
    perm1 = np.array([p * 128 + q for q in range(128) for p in range(16)])
    perm2 = np.array([p * 128 + q for q in range(128) for p in range(64)])
    H128 = _sylvester(128)
    maps = []
    for c in range(N_CORES):
        m = {}
        m["wupP"] = np.ascontiguousarray(W_up.T[perm1, :])
        m["wgateP"] = np.ascontiguousarray(W_gate.T[perm1, :])
        m["wdownP"] = np.ascontiguousarray(W_down.T[perm2, :])
        m["suug"] = np.ascontiguousarray(SU_ug.reshape(16, 128).T)
        m["sudp"] = np.ascontiguousarray(SU_dp.reshape(64, 128).T)
        m["scup"] = np.ascontiguousarray(Wsug[:8192].reshape(64, 128).T)
        m["scgate"] = np.ascontiguousarray(Wsug[8192:].reshape(64, 128).T)
        m["scdown"] = np.ascontiguousarray(Wsdp.reshape(16, 128).T)
        m["h128"] = H128
        m["hbd16"] = np.kron(np.eye(8, dtype=np.float32), _sylvester(16))
        m["hbd64"] = np.kron(np.eye(2, dtype=np.float32), _sylvester(64))
        for k in ("ones_row", "ones_col", "cmisc"):
            m[k + "D"] = consts[k]
        maps.append(m)
    return maps


def declare_D(nc, L):
    t = {}
    t["wupP"] = nc.dram_tensor("wupP", [HID, 8192], F32, kind="ExternalInput").ap()
    t["wgateP"] = nc.dram_tensor("wgateP", [HID, 8192], F32, kind="ExternalInput").ap()
    t["wdownP"] = nc.dram_tensor("wdownP", [8192, HID], F32, kind="ExternalInput").ap()
    for nme, shp in (("suug", [128, 16]), ("sudp", [128, 64]), ("scup", [128, 64]),
                     ("scgate", [128, 64]), ("scdown", [128, 16]),
                     ("h128", [128, 128]), ("hbd16", [128, 128]), ("hbd64", [128, 128])):
        t[nme] = nc.dram_tensor(nme, shp, F32, kind="ExternalInput").ap()
    for nme, shp in (("ones_row", [1, 128]), ("ones_col", [128, 1]),
                     ("cmisc", [128, len(CONST_VALS)])):
        t[nme] = nc.dram_tensor(nme + "D", shp, F32, kind="ExternalInput").ap()
    t["Zs"] = nc.dram_tensor("ZsD", [HID, 512], BF16, kind="Internal").ap()
    t["Ys"] = nc.dram_tensor("YsD", [8192, 512], BF16, kind="Internal").ap()
    return t


def alloc_D(nc, stack, L):
    sb = {}

    def A(name, shape, dt):
        sb[name] = stack.enter_context(nc.sbuf_tensor("sbD_" + name, shape, dt))
    A("cmisc", [128, len(CONST_VALS)], F32)
    A("ones_row", [1, 128], F32)
    A("ones_col", [128, 1], F32)
    A("onescol_b", [128, 1], BF16)
    for nme in ("suug", "scdown"):
        A(nme, [128, 16], F32)
    for nme in ("sudp", "scup", "scgate"):
        A(nme, [128, 64], F32)
    for nme in ("h128b", "hbd16b", "hbd64b"):
        A(nme, [128, 128], BF16)
    A("r2f", [128, 16 * 512], F32)
    A("r2b", [128, 16 * 512], BF16)
    A("zt", [128, 512], BF16)
    A("zh", [128, 16 * 512], BF16)
    A("wgt", [128, 2 * 2048], BF16)
    A("wut", [128, 2 * 2048], BF16)
    A("wdt", [128, 2 * 768], BF16)
    A("ybuf", [128, 64 * 512], BF16)
    A("gtile", [128, 512], BF16)
    A("utile", [128, 512], BF16)
    A("ftile", [128, 512], BF16)
    A("preLN", [128, 16 * 512], F32)
    A("pbf", [128, 512], BF16)
    A("sqbf", [128, 512], BF16)
    A("xres", [128, 512], F32)
    A("outt", [128, 1024], F32)
    for nme in ("mrow", "m2row", "v1row", "varrow", "lnr1", "lnr2"):
        A(nme, [1, 512], F32)
    return sb


def emit_D(nc, em, t, sb, pb, L, r2f_in, out_ap, sqrt2048_idx=12, sqrt8192_idx=11):
    LQ = L // 4
    NTC = LQ // 512
    emit_const_load(nc, em, t, sb)
    for nme in ("suug", "sudp", "scup", "scgate", "scdown"):
        em.dma("sync", lambda s, nme=nme: s.dma_start(out=sb[nme][:], in_=t[nme][:]),
               outs=(nme,))
    for nme, src_ in (("h128b", "h128"), ("hbd16b", "hbd16"), ("hbd64b", "hbd64")):
        em.dma("pool", lambda g, nme=nme, src_=src_: g.dma_start(out=sb[nme][:], in_=t[src_][:]),
               outs=(nme,))
    em.dve(lambda v: v.tensor_scalar(sb["scup"][:], sb["scup"][:],
                                     sb["cmisc"][:, sqrt2048_idx:sqrt2048_idx + 1], None, ALU.mult),
           ins=("scup", "cmisc"), outs=("scup",))
    em.dve(lambda v: v.tensor_scalar(sb["scgate"][:], sb["scgate"][:],
                                     sb["cmisc"][:, sqrt2048_idx:sqrt2048_idx + 1], None, ALU.mult),
           ins=("scgate", "cmisc"), outs=("scgate",))
    em.dve(lambda v: v.tensor_scalar(sb["scdown"][:], sb["scdown"][:],
                                     sb["cmisc"][:, sqrt8192_idx:sqrt8192_idx + 1], None, ALU.mult),
           ins=("scdown", "cmisc"), outs=("scdown",))
    for tc in range(NTC):
        cs = slice(tc * 512, (tc + 1) * 512)
        for ft in range(16):
            em.dma("sync", lambda s, ft=ft, cs=cs: s.dma_start(
                out=sb["r2f"][:, ft * 512:(ft + 1) * 512],
                in_=r2f_in[ft * 128:(ft + 1) * 128, cs]), outs=(f"r2f{ft}",))
            em.dma("pool", lambda g, ft=ft, cs=cs: g.dma_start(
                out=sb["r2b"][:, ft * 512:(ft + 1) * 512],
                in_=r2f_in[ft * 128:(ft + 1) * 128, cs]), outs=(f"r2b{ft}",))
        for ft in range(16):
            em.dve(lambda v, ft=ft: v.tensor_scalar(
                sb["zt"][:], sb["r2b"][:, ft * 512:(ft + 1) * 512],
                sb["suug"][:, ft:ft + 1], None, ALU.mult),
                ins=(f"r2b{ft}", "suug"), outs=("zt",))
            em.te(lambda te: te.matmul(pb["mm0"][:, 0:512], sb["h128b"][:], sb["zt"][:],
                                       start=True, stop=True),
                  ins=("h128b", "zt"), outs=("mm0",))
            em.act(lambda sc_: sc_.copy(sb["zt"][:], pb["mm0"][:, 0:512]),
                   ins=("mm0",), outs=("zt",))
            em.dma("sync", lambda s, ft=ft: s.dma_start(
                out=t["Zs"][ft * 128:(ft + 1) * 128, :], in_=sb["zt"][:]),
                ins=("zt",), outs=("Zs",))
        for rt in range(16):
            em.dma("sync", lambda s, rt=rt: s.dma_start(
                out=sb["zt"][:],
                in_=t["Zs"].rearrange("(p q) t -> q p t", p=16)[rt * 8:(rt + 1) * 8, :, :]),
                ins=("Zs",), outs=("zt",))
            em.te(lambda te: te.matmul(pb["mm0"][:, 0:512], sb["hbd16b"][:], sb["zt"][:],
                                       start=True, stop=True),
                  ins=("hbd16b", "zt"), outs=("mm0",))
            em.act(lambda sc_, rt=rt: sc_.copy(sb["zh"][:, rt * 512:(rt + 1) * 512],
                                               pb["mm0"][:, 0:512]),
                   ins=("mm0",), outs=(f"zh{rt}",))
        zins = tuple(f"zh{rt}" for rt in range(16))
        for ot in range(64):
            par = ot % 2
            em.dma("pool", lambda g, ot=ot, par=par: g.dma_start(
                out=sb["wgt"][:, par * 2048:(par + 1) * 2048].rearrange("p (a o) -> p a o", a=16),
                in_=t["wgateP"][:, ot * 128:(ot + 1) * 128].rearrange("(a p) o -> p a o", p=128)),
                outs=(f"wgt{par}",))
            em.dma("pool", lambda g, ot=ot, par=par: g.dma_start(
                out=sb["wut"][:, par * 2048:(par + 1) * 2048].rearrange("p (a o) -> p a o", a=16),
                in_=t["wupP"][:, ot * 128:(ot + 1) * 128].rearrange("(a p) o -> p a o", p=128)),
                outs=(f"wut{par}",))

            def mmg(te, par=par):
                last = None
                for a in range(16):
                    last = te.matmul(pb["mm0"][:, 0:512],
                                     sb["wgt"][:, par * 2048 + a * 128: par * 2048 + (a + 1) * 128],
                                     sb["zh"][:, a * 512:(a + 1) * 512],
                                     start=(a == 0), stop=(a == 15))
                return last
            em.op("te", "te", mmg, ins=(f"wgt{par}",) + zins, outs=("mm0",))

            def mmu(te, par=par):
                last = None
                for a in range(16):
                    last = te.matmul(pb["mm1"][:, 0:512],
                                     sb["wut"][:, par * 2048 + a * 128: par * 2048 + (a + 1) * 128],
                                     sb["zh"][:, a * 512:(a + 1) * 512],
                                     start=(a == 0), stop=(a == 15))
                return last
            em.op("te", "te", mmu, ins=(f"wut{par}",) + zins, outs=("mm1",))
            em.act(lambda sc_, ot=ot: sc_.activation(sb["gtile"][:], pb["mm0"][:, 0:512],
                                                     AF.Silu, scale=sb["scgate"][:, ot:ot + 1]),
                   ins=("mm0", "scgate"), outs=("gtile",))
            em.dve(lambda v, ot=ot: v.tensor_scalar(sb["utile"][:], pb["mm1"][:, 0:512],
                                                    sb["scup"][:, ot:ot + 1], None, ALU.mult),
                   ins=("mm1", "scup"), outs=("utile",))
            em.dve(lambda v: v.tensor_tensor(sb["ftile"][:], sb["gtile"][:], sb["utile"][:],
                                             ALU.mult),
                   ins=("gtile", "utile"), outs=("ftile",))
            em.dve(lambda v, ot=ot: v.tensor_scalar(sb["ftile"][:], sb["ftile"][:],
                                                    sb["sudp"][:, ot:ot + 1], None, ALU.mult),
                   ins=("ftile", "sudp"), outs=("ftile",))
            em.te(lambda te: te.matmul(pb["mm2"][:, 0:512], sb["h128b"][:], sb["ftile"][:],
                                       start=True, stop=True),
                  ins=("h128b", "ftile"), outs=("mm2",))
            em.act(lambda sc_: sc_.copy(sb["ftile"][:], pb["mm2"][:, 0:512]),
                   ins=("mm2",), outs=("ftile",))
            em.dma("sync", lambda s, ot=ot: s.dma_start(
                out=t["Ys"][ot * 128:(ot + 1) * 128, :], in_=sb["ftile"][:]),
                ins=("ftile",), outs=("Ys",))
        for rt in range(64):
            em.dma("sync", lambda s, rt=rt: s.dma_start(
                out=sb["ftile"][:],
                in_=t["Ys"].rearrange("(p q) t -> q p t", p=64)[rt * 2:(rt + 1) * 2, :, :]),
                ins=("Ys",), outs=("ftile",))
            em.te(lambda te: te.matmul(pb["mm2"][:, 0:512], sb["hbd64b"][:], sb["ftile"][:],
                                       start=True, stop=True),
                  ins=("hbd64b", "ftile"), outs=("mm2",))
            em.act(lambda sc_, rt=rt: sc_.copy(sb["ybuf"][:, rt * 512:(rt + 1) * 512],
                                               pb["mm2"][:, 0:512]),
                   ins=("mm2",), outs=(f"yb{rt}",))
        yins = tuple(f"yb{rt}" for rt in range(64))
        groups = [(0, 6), (6, 12), (12, 16)]
        for g0, g1 in groups:
            ncols = (g1 - g0) * 128
            for kt in range(64):
                par = kt % 2
                em.dma("pool", lambda g, kt=kt, g0=g0, ncols=ncols, par=par: g.dma_start(
                    out=sb["wdt"][:, par * 768: par * 768 + ncols],
                    in_=t["wdownP"][kt * 128:(kt + 1) * 128, g0 * 128: g0 * 128 + ncols]),
                    outs=(f"wdt{par}",))

                def mmd(te, kt=kt, g0=g0, g1=g1, par=par):
                    last = None
                    for i in range(g1 - g0):
                        last = te.matmul(pb[f"mm{i}"][:, 0:512],
                                         sb["wdt"][:, par * 768 + i * 128: par * 768 + (i + 1) * 128],
                                         sb["ybuf"][:, kt * 512:(kt + 1) * 512],
                                         start=(kt == 0), stop=(kt == 63))
                    return last
                em.op("te", "te", mmd, ins=(f"wdt{par}",) + yins,
                      outs=tuple(f"mm{i}" for i in range(g1 - g0)))
            for i, o2 in enumerate(range(g0, g1)):
                em.dve(lambda v, i=i, o2=o2: v.scalar_tensor_tensor(
                    sb["preLN"][:, o2 * 512:(o2 + 1) * 512], pb[f"mm{i}"][:, 0:512],
                    sb["scdown"][:, o2:o2 + 1], sb["r2f"][:, o2 * 512:(o2 + 1) * 512],
                    ALU.mult, ALU.add),
                    ins=(f"mm{i}", "scdown", f"r2f{o2}"), outs=(f"preLN{o2}",))
        emit_LN(nc, em, sb, pb, ntiles=16, width=512,
                consume=mk_ln_writer(em, sb, out_ap, None, tc, 512, "outo"))


from contextlib import ExitStack
from concourse.bass_utils import run_bass_kernel_spmd

LFULL = 8192
_PROG_CACHE = {}


def _build_A(L):
    nc = bass.Bass("TRN2", target_bir_lowering=False)
    t = declare_A(nc, L)
    bscr, gscr, gatescr = declare_A_scratch_beta(nc, L)
    resid0 = nc.dram_tensor("resid0", [HID, L // 4], F32, kind="ExternalOutput").ap()
    with ExitStack() as stack:
        sb = alloc_A(nc, stack, L)
        pb = mk_psum_banks(nc, stack)
        block = stack.enter_context(nc.Block())
        em = Emitter(nc, block, stack)
        emit_A(nc, em, t, sb, pb, L, bscr, gscr, gatescr)
        emit_A_scan(nc, em, t, sb, pb, L, bscr, gscr, gatescr)
        emit_A_wo(nc, em, t, sb, pb, L, resid0)
        toks = [em.ctx.buf(f"r0out_{ft}_{tc}").w
                for ft in range(16) for tc in range(L // 4 // 256)]
        em.wait_all("sync", toks)
        em.flush()
    return nc


def _build_B(L):
    nc = bass.Bass("TRN2", target_bir_lowering=False)
    t = declare_B(nc, L)
    r0in = nc.dram_tensor("r0in", [HID, L // 4], F32, kind="ExternalInput").ap()
    r1out = nc.dram_tensor("r1out", [HID, L // 4], F32, kind="ExternalOutput").ap()
    with ExitStack() as stack:
        sb = alloc_B(nc, stack, L)
        pb = mk_psum_banks(nc, stack)
        block = stack.enter_context(nc.Block())
        em = Emitter(nc, block, stack)
        emit_B(nc, em, t, sb, pb, L, r0in, r1out)
        toks = [em.ctx.buf(f"r1o_{ft}_{tc}").w
                for ft in range(16) for tc in range(L // 4 // 512)]
        em.wait_all("sync", toks)
        em.flush()
    return nc


def _build_C(L):
    nc = bass.Bass("TRN2", target_bir_lowering=False)
    t = declare_C(nc, L)
    r1in = nc.dram_tensor("r1in", [HID, L // 4], F32, kind="ExternalInput").ap()
    r2out = nc.dram_tensor("r2out", [HID, L // 4], F32, kind="ExternalOutput").ap()
    with ExitStack() as stack:
        sb = alloc_C(nc, stack, L)
        pb = mk_psum_banks(nc, stack)
        block = stack.enter_context(nc.Block())
        em = Emitter(nc, block, stack)
        emit_C(nc, em, t, sb, pb, L, r1in, r2out,
               invsq_idx=(11 if L == 8192 else 12))
        toks = [em.ctx.buf(f"r2o_{ft}_{tc}").w
                for ft in range(16) for tc in range(L // 4 // 256)]
        em.wait_all("sync", toks)
        em.flush()
    return nc


def _build_D(L):
    nc = bass.Bass("TRN2", target_bir_lowering=False)
    t = declare_D(nc, L)
    r2in = nc.dram_tensor("r2in", [HID, L // 4], F32, kind="ExternalInput").ap()
    oout = nc.dram_tensor("oout", [HID, L // 4], F32, kind="ExternalOutput").ap()
    with ExitStack() as stack:
        sb = alloc_D(nc, stack, L)
        pb = mk_psum_banks(nc, stack)
        block = stack.enter_context(nc.Block())
        em = Emitter(nc, block, stack)
        emit_D(nc, em, t, sb, pb, L, r2in, oout)
        toks = [em.ctx.buf(f"outo_{ft}_{tc}").w
                for ft in range(16) for tc in range(L // 4 // 512)]
        em.wait_all("sync", toks)
        em.flush()
    return nc


def _prog(key, builder, L):
    if key not in _PROG_CACHE:
        _PROG_CACHE[key] = builder(L)
    return _PROG_CACHE[key]


def kernel(**inputs):
    L = int(np.asarray(inputs["L0_attn_x"]).shape[1])
    mapsA = host_prep_A(inputs, L=L)
    ncA = _prog(("A", L), _build_A, L)
    resA = run_bass_kernel_spmd(ncA, mapsA, core_ids=list(range(N_CORES)))
    r0 = [np.asarray(resA.results[c]["resid0"]) for c in range(N_CORES)]
    mapsB = host_prep_B(inputs, L=L)
    for c in range(N_CORES):
        mapsB[c]["r0in"] = r0[c]
    ncB = _prog(("B", L), _build_B, L)
    resB = run_bass_kernel_spmd(ncB, mapsB, core_ids=list(range(N_CORES)))
    r1 = [np.asarray(resB.results[c]["r1out"]) for c in range(N_CORES)]
    mapsC = host_prep_C(inputs, L=L)
    for c in range(N_CORES):
        mapsC[c]["r1in"] = r1[c]
    ncC = _prog(("C", L), _build_C, L)
    resC = run_bass_kernel_spmd(ncC, mapsC, core_ids=list(range(N_CORES)))
    r2 = [np.asarray(resC.results[c]["r2out"]) for c in range(N_CORES)]
    mapsD = host_prep_D(inputs, L=L)
    for c in range(N_CORES):
        mapsD[c]["r2in"] = r2[c]
    ncD = _prog(("D", L), _build_D, L)
    resD = run_bass_kernel_spmd(ncD, mapsD, core_ids=list(range(N_CORES)))
    out = np.zeros((B, L, HID), np.float32)
    LQ = L // 4
    for c in range(N_CORES):
        b, q = c // 4, c % 4
        out[b, q * LQ:(q + 1) * LQ, :] = np.asarray(resD.results[c]["oout"]).T
    return out


# version 02 (final)
# speedup vs baseline: 1.2119x; 1.2119x over previous
import numpy as np
import concourse.bass as bass
import concourse.mybir as mybir

F32 = mybir.dt.float32
BF16 = mybir.dt.bfloat16
I8 = mybir.dt.int8
U32 = mybir.dt.uint32
AF = mybir.ActivationFunctionType
ALU = mybir.AluOpType
Pool = mybir.EngineType.Pool

CONST_VALS = [1e-30, 1e-5, 1.0 / 128.0, -0.5, -0.5 * float(np.log(128.0)),
              1.0 / 2048.0, -1.0, 1.0, 0.5, 2.0, 0.0,
              1.0 / float(np.sqrt(8192.0)), 1.0 / float(np.sqrt(2048.0)),
              1.0 / 8192.0, 64.0, 3.0]

HID = 2048
DH = 128
NH = 8
B = 2
N_CORES = 8
EPS = 1e-5



class Sem:
    def __init__(self, h, name):
        self.h = h
        self.name = name
        self.v = 0


class Buf:

    __slots__ = ("name", "w", "readers")

    def __init__(self, name):
        self.name = name
        self.w = None
        self.readers = []


class Ctx:

    def __init__(self, nc, block, stack):
        self.nc = nc
        self.block = block
        names = ["te", "dve", "act", "pool", "cc"]
        for q in ("dmas", "dmag", "dmaa"):
            for i in range(4):
                names.append(f"{q}{i}")
        self.sem = {}
        for n in names:
            h = stack.enter_context(nc.semaphore("sem_" + n))
            self.sem[n] = Sem(h, n)
        self.qrr = {"dmas": 0, "dmag": 0, "dmaa": 0}
        self.waited = {k: {} for k in ["te", "dve", "act", "pool", "sync", "scalar"]}
        self.bufs = {}

    def buf(self, name):
        if name not in self.bufs:
            self.bufs[name] = Buf(name)
        return self.bufs[name]

    def _wait(self, eng, engname, tok):
        if tok is None:
            return
        sem, val = tok
        if sem.name.startswith("dma") or sem.name == "cc":
            val = sem.v
        wmap = self.waited[engname]
        if wmap.get(sem.name, 0) >= val:
            return
        eng.wait_ge(sem.h, val)
        wmap[sem.name] = val

    def emit(self, eng, engname, semname, fn, ins=(), outs=(), inc=1):
        for b in ins:
            self._wait(eng, engname, self.buf(b).w)
        for b in outs:
            bb = self.buf(b)
            self._wait(eng, engname, bb.w)
            for tok in bb.readers:
                self._wait(eng, engname, tok)
        inst = fn(eng)
        s = self.sem[semname]
        s.v += inc
        inst.then_inc(s.h, inc)
        tok = (s, s.v)
        for b in ins:
            self.buf(b).readers.append(tok)
        for b in outs:
            bb = self.buf(b)
            bb.w = tok
            bb.readers = []
        return tok

    def barrier_engine(self, eng, engname, toks):
        for tok in toks:
            self._wait(eng, engname, tok)



def mk_psum_banks(nc, stack):
    banks = {}
    for i in range(6):
        banks[f"mm{i}"] = stack.enter_context(
            nc.psum_tensor(f"pb{i}", [128, 512], F32))
    banks["tr"] = stack.enter_context(nc.psum_tensor("pbtr", [128, 1024], BF16))
    banks["sm"] = stack.enter_context(nc.psum_tensor("pbsm", [128, 512], F32))
    return banks


class BankPool:
    def __init__(self, names):
        self.names = list(names)
        self.i = 0

    def next(self):
        n = self.names[self.i % len(self.names)]
        self.i += 1
        return n



def host_constants():
    c = {}
    c["cmisc"] = np.tile(np.array([CONST_VALS], dtype=np.float32), (128, 1))
    c["ident"] = np.eye(128, dtype=np.float32)
    c["ones_col"] = np.ones((128, 1), dtype=np.float32)
    c["ones_row"] = np.ones((1, 128), dtype=np.float32)
    sl = np.tril(np.ones((128, 128), dtype=np.float32), k=-1)
    c["maskSL"] = sl
    a = np.arange(128)[:, None]
    tau = np.arange(64)[None, :]
    c["mask64T"] = (a <= 2 * tau + 1).astype(np.float32)
    return c


def host_prep_A(inputs, L=8192):
    x = np.asarray(inputs["L0_attn_x"])
    Wq = np.asarray(inputs["L0_attn_Wq"]); Wk = np.asarray(inputs["L0_attn_Wk"])
    Wv = np.asarray(inputs["L0_attn_Wv"]); Wb = np.asarray(inputs["L0_attn_Wb"])
    Wa = np.asarray(inputs["L0_attn_Wa"]); Wg = np.asarray(inputs["L0_attn_Wg"])
    Wo = np.asarray(inputs["L0_attn_Wo"])
    cq = np.asarray(inputs["L0_attn_conv_q_w"])
    ck = np.asarray(inputs["L0_attn_conv_k_w"])
    cv = np.asarray(inputs["L0_attn_conv_v_w"])
    A_log = np.asarray(inputs["L0_attn_A_log"]); dtb = np.asarray(inputs["L0_attn_dt_bias"])
    norm_w = np.asarray(inputs["L0_attn_norm_w"])
    consts = host_constants()

    maps = []
    for c in range(N_CORES):
        b = c // 4
        h0 = 2 * (c % 4)
        m = {}
        m["xT"] = np.ascontiguousarray(x[b, :L, :].T.astype(np.float32))
        q = c % 4
        LQ = L // 4
        m["xq"] = np.ascontiguousarray(x[b, q * LQ:(q + 1) * LQ, :].T.astype(np.float32))
        cols = []
        for h in (h0, h0 + 1):
            cols.append(Wq[:, h * DH:(h + 1) * DH])
        for h in (h0, h0 + 1):
            for j in (0, 1):
                cols.append(Wk[:, j * 1024 + h * DH: j * 1024 + (h + 1) * DH])
        for h in (h0, h0 + 1):
            for j in (0, 1):
                cols.append(Wv[:, j * 1024 + h * DH: j * 1024 + (h + 1) * DH])
        for h in (h0, h0 + 1):
            cols.append(Wg[:, h * DH:(h + 1) * DH])
        m["wbig"] = np.ascontiguousarray(np.concatenate(cols, axis=1))
        wb_small = np.zeros((HID, 128), dtype=np.float32)
        for li in (0, 1):
            for j in (0, 1):
                wb_small[:, li * 64 + j * 32] = Wb[:, j * NH + (h0 + li)]
        m["wbeta"] = wb_small
        wa_small = np.zeros((HID, 64), dtype=np.float32)
        for li in (0, 1):
            wa_small[:, li * 32] = Wa[:, h0 + li]
        m["wa"] = wa_small
        convs = np.zeros((128, 48), dtype=np.float32)
        tlist = []
        for h in (h0, h0 + 1):
            tlist.append(cq[h * DH:(h + 1) * DH, 0, :])
        for h in (h0, h0 + 1):
            for j in (0, 1):
                tlist.append(ck[j * 1024 + h * DH: j * 1024 + (h + 1) * DH, 0, :])
        for h in (h0, h0 + 1):
            for j in (0, 1):
                tlist.append(cv[j * 1024 + h * DH: j * 1024 + (h + 1) * DH, 0, :])
        for i, t in enumerate(tlist):
            convs[:, 4 * i: 4 * i + 4] = t
        m["convw"] = convs
        asm = np.zeros((64, 1), dtype=np.float32)
        dsm = np.zeros((64, 1), dtype=np.float32)
        asm[0, 0] = A_log[h0]; asm[32, 0] = A_log[h0 + 1]
        dsm[0, 0] = dtb[h0]; dsm[32, 0] = dtb[h0 + 1]
        m["alog"] = asm
        m["dtbias"] = dsm
        m["normw"] = norm_w.reshape(128, 1).astype(np.float32)
        m["wo"] = np.ascontiguousarray(Wo.astype(np.float32))
        m["tqsel"] = np.array([[c % 4]], dtype=np.uint32)
        for k, v in consts.items():
            m[k] = v
        maps.append(m)
    return maps



def declare_A(nc, L):
    t = {}
    t["xT"] = nc.dram_tensor("xT", [HID, L], F32, kind="ExternalInput").ap()
    t["xq"] = nc.dram_tensor("xq", [HID, L // 4], F32, kind="ExternalInput").ap()
    t["wbig"] = nc.dram_tensor("wbig", [HID, 1536], F32, kind="ExternalInput").ap()
    t["wbeta"] = nc.dram_tensor("wbeta", [HID, 128], F32, kind="ExternalInput").ap()
    t["wa"] = nc.dram_tensor("wa", [HID, 64], F32, kind="ExternalInput").ap()
    t["convw"] = nc.dram_tensor("convw", [128, 48], F32, kind="ExternalInput").ap()
    t["alog"] = nc.dram_tensor("alog", [64, 1], F32, kind="ExternalInput").ap()
    t["dtbias"] = nc.dram_tensor("dtbias", [64, 1], F32, kind="ExternalInput").ap()
    t["normw"] = nc.dram_tensor("normw", [128, 1], F32, kind="ExternalInput").ap()
    t["wo"] = nc.dram_tensor("wo", [1024, HID], F32, kind="ExternalInput").ap()
    t["tqsel"] = nc.dram_tensor("tqsel", [1, 1], U32, kind="ExternalInput").ap()
    for nme, shp in (("ident", [128, 128]), ("ones_col", [128, 1]),
                     ("ones_row", [1, 128]), ("cmisc", [128, len(CONST_VALS)]),
                     ("maskSL", [128, 128]), ("mask64T", [128, 64])):
        t[nme] = nc.dram_tensor(nme, shp, F32, kind="ExternalInput").ap()
    t["kscr"] = nc.dram_tensor("kscr", [2, 128, 2 * L], BF16, kind="Internal").ap()
    t["vscr"] = nc.dram_tensor("vscr", [2, 128, 2 * L], BF16, kind="Internal").ap()
    t["qscr"] = nc.dram_tensor("qscr", [2, 128, L], BF16, kind="Internal").ap()
    t["e1_in"] = nc.dram_tensor("e1_in", [256, L], BF16, kind="Internal").ap()
    t["e1_out"] = nc.dram_tensor("e1_out", [1024, L], BF16, kind="Internal").ap()
    return t



ENG2SECT = {"te": "tensor", "dve": "vector", "act": "scalar",
            "pool": "gpsimd", "sync": "sync", "scalar": "scalar"}


class Emitter:

    def __init__(self, nc, block, stack):
        self.ctx = Ctx(nc, block, stack)
        self.block = block
        self.pend_eng = None
        self.pend = []

    def flush(self):
        if not self.pend:
            return
        acts = self.pend
        sect = getattr(self.block, ENG2SECT[self.pend_eng])

        def run(eng):
            for a in acts:
                a(eng)
        sect(run)
        self.pend = []
        self.pend_eng = None

    def _q(self, engname, action):
        if self.pend_eng != engname:
            self.flush()
            self.pend_eng = engname
        self.pend.append(action)

    def op(self, engname, semname, fn, ins=(), outs=(), inc=1):
        ctx = self.ctx

        def action(eng):
            ctx.emit(eng, engname, semname, fn, ins=ins, outs=outs, inc=inc)
        self._q(engname, action)

    def te(self, fn, ins=(), outs=()):
        self.op("te", "te", fn, ins, outs)

    def dve(self, fn, ins=(), outs=()):
        self.op("dve", "dve", fn, ins, outs)

    def act(self, fn, ins=(), outs=()):
        self.op("act", "act", fn, ins, outs)

    def pool(self, fn, ins=(), outs=()):
        self.op("pool", "pool", fn, ins, outs)

    def dma(self, queue, fn, ins=(), outs=()):
        base = {"sync": "dmas", "pool": "dmag", "scalar": "dmaa"}[queue]
        engname = {"sync": "sync", "pool": "pool", "scalar": "scalar"}[queue]
        i = self.ctx.qrr[base]
        self.ctx.qrr[base] = (i + 1) % 4
        self.op(engname, f"{base}{i}", fn, ins, outs, inc=16)

    def cc(self, fn, ins=(), outs=()):
        self.op("pool", "cc", fn, ins, outs, inc=1)

    def wait_all(self, engname, toks):
        ctx = self.ctx

        def action(eng):
            ctx.barrier_engine(eng, engname, toks)
        self._q(engname, action)

    def raw(self, engname, action):
        self._q(engname, action)



def emit_A(nc, em, t, sb, pb, L, bscr, gscr, gatescr, dbg_dal=None):
    NPCH = L // 512
    NSC = L // 64
    mmpool = BankPool(["mm0", "mm1", "mm2", "mm3"])

    em.dma("pool", lambda g: g.dma_start(
        out=sb["wbig"][:].rearrange("p (a o) -> p a o", a=16),
        in_=t["wbig"].rearrange("(a p) o -> p a o", p=128)), outs=("wbig",))
    em.dma("pool", lambda g: g.dma_start(
        out=sb["wbeta"][:].rearrange("p (a o) -> p a o", a=16),
        in_=t["wbeta"].rearrange("(a p) o -> p a o", p=128)), outs=("wbeta",))
    em.dma("pool", lambda g: g.dma_start(
        out=sb["wa"][:].rearrange("p (a o) -> p a o", a=16),
        in_=t["wa"].rearrange("(a p) o -> p a o", p=128)), outs=("wa",))
    em.dma("pool", lambda g: g.dma_start(out=sb["ident"][:], in_=t["ident"][:]), outs=("ident",))
    em.dma("pool", lambda g: g.dma_start(out=sb["onesrow_b"][:], in_=t["ones_row"][:]), outs=("onesrow_b",))
    em.dma("pool", lambda g: g.dma_start(out=sb["onescol_b"][:], in_=t["ones_col"][:]), outs=("onescol_b",))
    em.dma("sync", lambda s: s.dma_start(out=sb["convw"][:], in_=t["convw"][:]), outs=("convw",))
    em.dma("sync", lambda s: s.dma_start(out=sb["alog"][:], in_=t["alog"][:]), outs=("alog",))
    em.dma("sync", lambda s: s.dma_start(out=sb["dtbias"][:], in_=t["dtbias"][:]), outs=("dtbias",))
    em.dma("sync", lambda s: s.dma_start(out=sb["normw"][:], in_=t["normw"][:]), outs=("normw",))
    em.dma("sync", lambda s: s.dma_start(out=sb["maskSL"][:], in_=t["maskSL"][:]), outs=("maskSL",))
    em.dma("sync", lambda s: s.dma_start(out=sb["mask64T"][:], in_=t["mask64T"][:]), outs=("mask64T",))
    em.dma("sync", lambda s: s.dma_start(out=sb["ones_row"][:], in_=t["ones_row"][:]), outs=("ones_row",))
    em.dma("sync", lambda s: s.dma_start(out=sb["ones_col"][:], in_=t["ones_col"][:]), outs=("ones_col",))
    em.dma("sync", lambda s: s.dma_start(out=sb["one1"][:], in_=t["ones_col"][0:1, 0:1]), outs=("one1",))
    em.dma("sync", lambda s: s.dma_start(out=sb["cmisc"][:], in_=t["cmisc"][:]), outs=("cmisc",))
    for i, vv in enumerate(CONST_VALS):
        if vv in (0.0, 1.0, -1.0, 2.0, 0.5):
            continue
        try:
            nc.const_aps.register(F32, vv, sb["cmisc"][:, i:i + 1])
        except Exception:
            pass
    em.wait_all("act", [em.ctx.buf("cmisc").w])
    em.wait_all("dve", [em.ctx.buf("cmisc").w])
    em.wait_all("te", [em.ctx.buf("cmisc").w])
    em.wait_all("pool", [em.ctx.buf("cmisc").w])
    em.dma("sync", lambda s: s.dma_start(out=sb["tqsel"][:], in_=t["tqsel"][:]), outs=("tqsel",))
    em.act(lambda sc: sc.activation(sb["negea"][:], sb["alog"][:], AF.Exp, scale=1.0),
           ins=("alog",), outs=("negea",))
    em.dve(lambda v: v.tensor_scalar(sb["negea"][:], sb["negea"][:], -1.0, None, ALU.mult),
           ins=("negea",), outs=("negea",))
    if dbg_dal is not None:
        em.dma("sync", lambda s: s.dma_start(out=dbg_dal[:, 3:4], in_=sb["negea"][:],
                                             allow_slow_non_contiguous=True),
               ins=("negea",), outs=("dbgal",))

    em.dve(lambda v: v.memset(sb["raw"][:], 0.0),
           outs=tuple(f"raw{ot}" for ot in range(10)))
    for pc in range(NPCH):
        tc = slice(pc * 512, (pc + 1) * 512)
        for a in range(16):
            em.dma("pool", lambda g, a=a, tc=tc: g.dma_start(
                out=sb["xc"][:, a * 512:(a + 1) * 512],
                in_=t["xT"][a * 128:(a + 1) * 128, tc]), outs=(f"xc{a}",))
        xins = tuple(f"xc{a}" for a in range(16))
        for ot in range(10):
            bank = mmpool.next()

            def mmseq(te, ot=ot, bank=bank):
                last = None
                for a in range(16):
                    last = te.matmul(pb[bank][:, 0:512],
                                     sb["wbig"][:, a * 1536 + ot * 128: a * 1536 + (ot + 1) * 128],
                                     sb["xc"][:, a * 512:(a + 1) * 512],
                                     start=(a == 0), stop=(a == 15))
                return last
            em.op("te", "te", mmseq, ins=("wbig",) + xins, outs=(bank,), inc=1)
            em.dve(lambda v, ot=ot, bank=bank: v.tensor_copy(
                sb["raw"][:, ot * 515 + 3: ot * 515 + 515], pb[bank][:, 0:512]),
                ins=(bank,), outs=(f"raw{ot}",))
            em.dve(lambda v, ot=ot: v.tensor_scalar(
                sb["cv"][:, 0:512], sb["raw"][:, ot * 515: ot * 515 + 512],
                sb["convw"][:, 4 * ot: 4 * ot + 1], None, ALU.mult),
                ins=(f"raw{ot}", "convw"), outs=("cv",))
            for tau in (1, 2, 3):
                em.dve(lambda v, ot=ot, tau=tau: v.scalar_tensor_tensor(
                    sb["cv"][:, 0:512], sb["raw"][:, ot * 515 + tau: ot * 515 + tau + 512],
                    sb["convw"][:, 4 * ot + tau: 4 * ot + tau + 1], sb["cv"][:, 0:512],
                    ALU.mult, ALU.add),
                    ins=(f"raw{ot}", "convw", "cv"), outs=("cv",))
            em.dve(lambda v, ot=ot: v.tensor_copy(
                sb["raw"][:, ot * 515: ot * 515 + 3],
                sb["raw"][:, ot * 515 + 512: ot * 515 + 515]),
                ins=(f"raw{ot}",), outs=(f"raw{ot}",))
            em.act(lambda sc: sc.activation(sb["cs"][:, 0:512], sb["cv"][:, 0:512], AF.Silu),
                   ins=("cv",), outs=("cs",))
            if ot < 6:
                em.act(lambda sc: sc.activation(sb["cv"][:, 0:512], sb["cs"][:, 0:512], AF.Square),
                       ins=("cs",), outs=("cv",))
                em.te(lambda te: te.matmul(pb["sm"][0:1, 0:512], sb["ones_col"][:], sb["cv"][:, 0:512],
                                           start=True, stop=True),
                      ins=("cv", "ones_col"), outs=("sm",))
                isq = ot < 2
                em.act(lambda sc, isq=isq: sc.activation(sb["r1"][0:1, 0:512], pb["sm"][0:1, 0:512],
                                                         AF.Ln, bias=1e-30),
                       ins=("sm",), outs=("r1",))
                em.act(lambda sc, isq=isq: sc.activation(
                    sb["r2"][0:1, 0:512], sb["r1"][0:1, 0:512], AF.Exp, scale=-0.5,
                    bias=(-0.5 * np.log(128.0) if isq else 0.0)),
                    ins=("r1",), outs=("r2",))
                em.te(lambda te: te.matmul(pb["sm"][:, 0:512], sb["ones_row"][:], sb["r2"][0:1, 0:512],
                                           start=True, stop=True),
                      ins=("r2", "ones_row"), outs=("sm",))
                em.dve(lambda v: v.tensor_tensor(sb["cs"][:, 0:512], sb["cs"][:, 0:512],
                                                 pb["sm"][:, 0:512], ALU.mult),
                       ins=("cs", "sm"), outs=("cs",))
            if ot < 2:
                em.dve(lambda v: v.tensor_copy(sb["qout"][:, 0:512], sb["cs"][:, 0:512]),
                       ins=("cs",), outs=("qout",))
                em.dma("sync", lambda s, ot=ot, pc=pc: s.dma_start(
                    out=t["qscr"][ot, :, pc * 512:(pc + 1) * 512], in_=sb["qout"][:, 0:512]),
                    ins=("qout",), outs=(f"qscr{ot}",))
            elif ot < 6:
                lane, j = (ot - 2) // 2, (ot - 2) % 2
                em.dve(lambda v, j=j: v.tensor_copy(sb["kint"][:, j:1024:2], sb["cs"][:, 0:512]),
                       ins=("cs",), outs=("kint",))
                if j == 1:
                    em.dma("sync", lambda s, lane=lane, pc=pc: s.dma_start(
                        out=t["kscr"][lane, :, pc * 1024:(pc + 1) * 1024], in_=sb["kint"][:]),
                        ins=("kint",), outs=(f"kscr{lane}",))
            elif ot < 10:
                lane, j = (ot - 6) // 2, (ot - 6) % 2
                em.dve(lambda v, j=j: v.tensor_copy(sb["vint"][:, j:1024:2], sb["cs"][:, 0:512]),
                       ins=("cs",), outs=("vint",))
                if j == 1:
                    em.dma("sync", lambda s, lane=lane, pc=pc: s.dma_start(
                        out=t["vscr"][lane, :, pc * 1024:(pc + 1) * 1024], in_=sb["vint"][:]),
                        ins=("vint",), outs=(f"vscr{lane}",))
        for li in (0, 1):
            ot = 10 + li
            bank = mmpool.next()

            def mmseq2(te, ot=ot, bank=bank):
                last = None
                for a in range(16):
                    last = te.matmul(pb[bank][:, 0:512],
                                     sb["wbig"][:, a * 1536 + ot * 128: a * 1536 + (ot + 1) * 128],
                                     sb["xc"][:, a * 512:(a + 1) * 512],
                                     start=(a == 0), stop=(a == 15))
                return last
            em.op("te", "te", mmseq2, ins=("wbig",) + xins, outs=(bank,), inc=1)
            em.act(lambda sc, bank=bank: sc.activation(
                sb["gatec"][:], pb[bank][:, 0:512], AF.Silu),
                ins=(bank,), outs=("gatec",))
            em.dma("sync", lambda s, li=li, pc=pc: s.dma_start(
                out=gatescr[li, :, pc * 512:(pc + 1) * 512], in_=sb["gatec"][:]),
                ins=("gatec",), outs=(f"gatescr{li}",))
        for li in (0, 1):
            bank = "mm4"

            def mmbeta(te, li=li, bank=bank):
                last = None
                for a in range(16):
                    last = te.matmul(pb[bank][0:64, 0:512],
                                     sb["wbeta"][:, a * 128 + li * 64: a * 128 + (li + 1) * 64],
                                     sb["xc"][:, a * 512:(a + 1) * 512],
                                     start=(a == 0), stop=(a == 15))
                return last
            em.op("te", "te", mmbeta, ins=("wbeta",) + xins, outs=("mm4",), inc=1)
            em.act(lambda sc: sc.activation(sb["sig"][0:64, 0:512], pb["mm4"][0:64, 0:512],
                                            AF.Sigmoid),
                   ins=("mm4",), outs=("sig",))
            for j in (0, 1):
                row = j * 32
                em.dve(lambda v, row=row: v.tensor_scalar(
                    sb["b2row"][0:1, 0:512], sb["sig"][row:row + 1, 0:512],
                    2.0, None, ALU.mult),
                    ins=("sig",), outs=("b2row",))
                em.dma("sync", lambda s, li=li, j=j, pc=pc: s.dma_start(
                    out=bscr[li, j, pc * 512:(pc + 1) * 512].unsqueeze(0),
                    in_=sb["b2row"][0:1, 0:512]),
                    ins=("b2row",), outs=("bscr",))
        bank = "mm5"

        def mmg(te, bank=bank):
            last = None
            for a in range(16):
                last = te.matmul(pb[bank][0:64, 0:512], sb["wa"][:, a * 64:(a + 1) * 64],
                                 sb["xc"][:, a * 512:(a + 1) * 512], start=(a == 0), stop=(a == 15))
            return last
        em.op("te", "te", mmg, ins=("wa",) + xins, outs=(bank,), inc=1)
        em.dve(lambda v: v.tensor_scalar(sb["gu"][0:64, 0:512], pb[bank][0:64, 0:512],
                                         sb["dtbias"][:, 0:1], None, ALU.add),
               ins=(bank, "dtbias"), outs=("gu",))
        em.act(lambda sc: sc.activation(sb["g1"][0:64, 0:512], sb["gu"][0:64, 0:512], AF.Abs),
               ins=("gu",), outs=("g1",))
        em.act(lambda sc: sc.activation(sb["g2"][0:64, 0:512], sb["g1"][0:64, 0:512],
                                        AF.Exp, scale=-1.0),
               ins=("g1",), outs=("g2",))
        em.act(lambda sc: sc.activation(sb["g1"][0:64, 0:512], sb["g2"][0:64, 0:512],
                                        AF.Ln, bias=1.0),
               ins=("g2",), outs=("g1",))
        em.dve(lambda v: v.tensor_scalar(sb["g2"][0:64, 0:512], sb["gu"][0:64, 0:512],
                                         0.0, None, ALU.max),
               ins=("gu",), outs=("g2",))
        em.dve(lambda v: v.tensor_tensor(sb["g3"][0:64, 0:512], sb["g2"][0:64, 0:512],
                                         sb["g1"][0:64, 0:512], ALU.add),
               ins=("g2", "g1"), outs=("g3",))
        em.dve(lambda v: v.tensor_scalar(sb["g3"][0:64, 0:512], sb["g3"][0:64, 0:512],
                                         sb["negea"][:, 0:1], None, ALU.mult),
               ins=("g3", "negea"), outs=("g3",))
        for li in (0, 1):
            em.dma("sync", lambda s, li=li, pc=pc: s.dma_start(
                out=gscr[li, pc * 512:(pc + 1) * 512].unsqueeze(0),
                in_=sb["g3"][li * 32:li * 32 + 1, 0:512]),
                ins=("g3",), outs=("gscr",))


def alloc_A(nc, stack, L):
    sb = {}

    def A(name, shape, dt):
        sb[name] = stack.enter_context(nc.sbuf_tensor("sb_" + name, shape, dt))
    NSC = L // 64
    A("wbig", [128, 16 * 1536], BF16)
    A("wbeta", [128, 16 * 128], BF16)
    A("wa", [128, 16 * 64], BF16)
    A("ident", [128, 128], BF16)
    A("maskSL", [128, 128], F32)
    A("mask64T", [128, 64], F32)
    A("ones_row", [1, 128], F32)
    A("ones_col", [128, 1], F32)
    A("onesrow_b", [1, 128], BF16)
    A("onescol_b", [128, 1], BF16)
    A("one1", [1, 1], F32)
    A("cmisc", [128, len(CONST_VALS)], F32)
    A("convw", [128, 48], F32)
    A("alog", [64, 1], F32)
    A("dtbias", [64, 1], F32)
    A("negea", [64, 1], F32)
    A("normw", [128, 1], F32)
    A("tqsel", [1, 1], U32)
    A("xc", [128, 16 * 512], BF16)
    A("raw", [128, 10 * 515], BF16)
    A("cv", [128, 512], F32)
    A("cs", [128, 512], F32)
    A("b2row", [1, 512], F32)
    A("r1", [1, 512], F32)
    A("r2", [1, 512], F32)
    A("kint", [128, 1024], BF16)
    A("vint", [128, 1024], BF16)
    A("qout", [128, 512], BF16)
    A("sig", [128, 512], F32)
    A("gu", [64, 512], F32)
    A("g1", [64, 512], F32)
    A("g2", [64, 512], F32)
    A("g3", [64, 512], F32)
    A("gatec", [128, 512], BF16)
    for p in (0, 1):
        A(f"Kd{p}", [128, 128], BF16)
        A(f"Vd{p}", [128, 128], BF16)
        A(f"Qd{p}", [128, 64], BF16)
        A(f"Npow{p}", [128, 7 * 128], BF16)
        A(f"Ppow{p}", [128, 7 * 128], BF16)
        A(f"X{p}", [128, 256], BF16)
        A(f"Ks{p}", [128, 128], BF16)
        A(f"Vs{p}", [128, 128], BF16)
        A(f"Ktil{p}", [128, 128], BF16)
        A(f"GTsb{p}", [128, 128], BF16)
        A(f"Zneg{p}", [128, 64], BF16)
        A(f"qbar{p}", [128, 64], BF16)
        A(f"A2T{p}", [128, 64], BF16)
        A(f"sE{p}", [128, 128], F32)
        A(f"sEx{p}", [128, 128], F32)
        A(f"sEm{p}", [128, 128], F32)
        A(f"sD2{p}", [128, 64], F32)
        A(f"sE2{p}", [128, 64], F32)
        A(f"sE2m{p}", [128, 64], F32)
        A(f"crel64{p}", [1, 64], F32)
        A(f"crelrow{p}", [1, 128], F32)
        A(f"coutc{p}", [1, 64], F32)
        A(f"goutr{p}", [1, 64], F32)
        A(f"bA{p}", [1, 64], F32)
        A(f"bB{p}", [1, 64], F32)
        A(f"gsl{p}", [1, 64], F32)
        A(f"ccol{p}", [128, 1], F32)
        A(f"bcol{p}", [128, 1], F32)
        A(f"negb{p}", [128, 1], F32)
        A(f"clastcol{p}", [128, 1], F32)
        A(f"farg{p}", [128, 1], F32)
        A(f"fcol{p}", [128, 1], F32)
        A(f"gamcol{p}", [128, 1], F32)
        A(f"Gcol{p}", [128, 1], F32)
        A(f"combo{p}", [128, 1], F32)
        A(f"osb{p}", [128, 64], F32)
        A(f"osq{p}", [128, 64], F32)
        A(f"r1r{p}", [1, 64], F32)
        A(f"r2r{p}", [1, 64], F32)
        A(f"t1o{p}", [128, 64], F32)
        A(f"t2o{p}", [128, 64], F32)
    A("zeros64", [1, 64], F32)
    for l in (0, 1):
        A(f"S{l}", [128, 128], F32)
        A(f"Sbf{l}", [128, 128], BF16)
        A(f"stmp{l}", [128, 128], F32)
    for p in (0, 1):
        A(f"gch{p}", [128, 64], BF16)
        A(f"gout{p}", [128, 64], BF16)
    A("wo_sb", [128, 8 * 128], BF16)
    A("ogth", [128, 8 * 256], BF16)
    A("xres", [128, 256], F32)
    A("preLN", [128, 16 * 256], F32)
    A("pbf", [128, 256], BF16)
    A("sqbf", [128, 256], BF16)
    A("mrow", [1, 256], F32)
    A("m2row", [1, 256], F32)
    A("v1row", [1, 256], F32)
    A("varrow", [1, 256], F32)
    A("lnr1", [1, 256], F32)
    A("lnr2", [1, 256], F32)
    A("outt", [128, 512], F32)
    return sb


def declare_A_scratch_beta(nc, L):
    bscr = nc.dram_tensor("bscr", [2, 2, L], F32, kind="Internal").ap()
    gscr = nc.dram_tensor("gscr", [2, L], F32, kind="Internal").ap()
    gatescr = nc.dram_tensor("gatescr", [2, 128, L], BF16, kind="Internal").ap()
    return bscr, gscr, gatescr


def emit_A_scan(nc, em, t, sb, pb, L, bscr, gscr, gatescr):
    NSC = L // 64
    for l in (0, 1):
        em.dve(lambda v, l=l: v.memset(sb[f"S{l}"][:], 0.0), outs=(f"S{l}",))
        em.dve(lambda v, l=l: v.memset(sb[f"Sbf{l}"][:], 0.0), outs=(f"Sbf{l}",))
    em.dve(lambda v: v.memset(sb["zeros64"][:], 0.0), outs=("zeros64",))

    for sc in range(NSC):
        for l in (0, 1):
            p = (2 * sc + l) % 2
            P = lambda n: f"{n}{p}"
            SL = f"S{l}"; SBFL = f"Sbf{l}"
            em.dma("sync", lambda s, l=l, sc=sc, p=p: s.dma_start(
                out=sb[f"Kd{p}"][:], in_=t["kscr"][l, :, sc * 128:(sc + 1) * 128]),
                ins=(f"kscr{l}",), outs=(P("Kd"),))
            em.dma("sync", lambda s, l=l, sc=sc, p=p: s.dma_start(
                out=sb[f"Vd{p}"][:], in_=t["vscr"][l, :, sc * 128:(sc + 1) * 128]),
                ins=(f"vscr{l}",), outs=(P("Vd"),))
            em.dma("sync", lambda s, l=l, sc=sc, p=p: s.dma_start(
                out=sb[f"Qd{p}"][:], in_=t["qscr"][l, :, sc * 64:(sc + 1) * 64]),
                ins=(f"qscr{l}",), outs=(P("Qd"),))
            em.dma("scalar", lambda s, l=l, sc=sc, p=p: s.dma_start(
                out=sb[f"bA{p}"][:], in_=bscr[l, 0, sc * 64:(sc + 1) * 64].unsqueeze(0)),
                ins=("bscr",), outs=(P("bA"),))
            em.dma("scalar", lambda s, l=l, sc=sc, p=p: s.dma_start(
                out=sb[f"bB{p}"][:], in_=bscr[l, 1, sc * 64:(sc + 1) * 64].unsqueeze(0)),
                ins=("bscr",), outs=(P("bB"),))
            em.dma("scalar", lambda s, l=l, sc=sc, p=p: s.dma_start(
                out=sb[f"gsl{p}"][:], in_=gscr[l, sc * 64:(sc + 1) * 64].unsqueeze(0)),
                ins=("gscr",), outs=(P("gsl"),))
            em.dve(lambda v, p=p: v.tensor_tensor_scan(
                sb[f"crel64{p}"][:], sb[f"gsl{p}"][:], sb["zeros64"][:], 0.0,
                ALU.add, ALU.add), ins=(P("gsl"), "zeros64"), outs=(P("crel64"),))
            em.dve(lambda v, p=p: v.tensor_copy(sb[f"crelrow{p}"][0:1, 0:128:2], sb[f"crel64{p}"][:]),
                   ins=(P("crel64"),), outs=(P("crelrow"),))
            em.dve(lambda v, p=p: v.tensor_copy(sb[f"crelrow{p}"][0:1, 1:128:2], sb[f"crel64{p}"][:]),
                   ins=(P("crel64"),), outs=(P("crelrow"),))
            em.dve(lambda v, p=p: v.tensor_copy(sb[f"sE{p}"][0:1, 0:128:2], sb[f"bA{p}"][:]),
                   ins=(P("bA"),), outs=(P("brow"),))
            em.dve(lambda v, p=p: v.tensor_copy(sb[f"sE{p}"][0:1, 1:128:2], sb[f"bB{p}"][:]),
                   ins=(P("bB"),), outs=(P("brow"),))
            em.te(lambda te, p=p: te.matmul(pb["sm"][:, 0:1], sb[f"crelrow{p}"][:], sb["one1"][:],
                                            start=True, stop=True),
                  ins=(P("crelrow"), "one1"), outs=("sm",))
            em.dve(lambda v, p=p: v.tensor_copy(sb[f"ccol{p}"][:], pb["sm"][:, 0:1]),
                   ins=("sm",), outs=(P("ccol"),))
            em.te(lambda te, p=p: te.matmul(pb["sm"][:, 0:1], sb[f"sE{p}"][0:1, 0:128], sb["one1"][:],
                                            start=True, stop=True),
                  ins=(P("brow"), "one1"), outs=("sm",))
            em.dve(lambda v, p=p: v.tensor_copy(sb[f"bcol{p}"][:], pb["sm"][:, 0:1]),
                   ins=("sm",), outs=(P("bcol"),))
            em.te(lambda te, p=p: te.matmul(pb["sm"][:, 0:1], sb["ones_row"][:],
                                            sb[f"crelrow{p}"][0:1, 127:128], start=True, stop=True),
                  ins=(P("crelrow"), "ones_row"), outs=("sm",))
            em.dve(lambda v, p=p: v.tensor_copy(sb[f"clastcol{p}"][:], pb["sm"][:, 0:1]),
                   ins=("sm",), outs=(P("clastcol"),))
            em.dve(lambda v, p=p: v.tensor_scalar(sb[f"negb{p}"][:], sb[f"bcol{p}"][:],
                                                  -1.0, None, ALU.mult),
                   ins=(P("bcol"),), outs=(P("negb"),))
            em.dve(lambda v, p=p: v.tensor_tensor(sb[f"farg{p}"][:], sb[f"clastcol{p}"][:],
                                                  sb[f"ccol{p}"][:], ALU.subtract),
                   ins=(P("clastcol"), P("ccol")), outs=(P("farg"),))
            em.act(lambda sc_, p=p: sc_.activation(sb[f"Gcol{p}"][:], sb[f"ccol{p}"][:], AF.Exp),
                   ins=(P("ccol"),), outs=(P("Gcol"),))
            em.act(lambda sc_, p=p: sc_.activation(sb[f"fcol{p}"][:], sb[f"farg{p}"][:], AF.Exp),
                   ins=(P("farg"),), outs=(P("fcol"),))
            em.act(lambda sc_, p=p: sc_.activation(sb[f"gamcol{p}"][:], sb[f"clastcol{p}"][:], AF.Exp),
                   ins=(P("clastcol"),), outs=(P("gamcol"),))
            em.dve(lambda v, p=p: v.tensor_tensor(sb[f"combo{p}"][:], sb[f"bcol{p}"][:],
                                                  sb[f"Gcol{p}"][:], ALU.mult),
                   ins=(P("bcol"), P("Gcol")), outs=(P("combo"),))
            em.te(lambda te, p=p: te.matmul(pb["mm0"][:, 0:128], sb[f"Kd{p}"][:], sb[f"Kd{p}"][:],
                                            start=True, stop=True),
                  ins=(P("Kd"),), outs=("mm0",))
            em.te(lambda te, p=p: te.matmul(pb["mm1"][:, 0:128], sb["ones_row"][:],
                                            sb[f"crelrow{p}"][:], start=True, stop=True),
                  ins=("ones_row", P("crelrow")), outs=("mm1",))
            em.dve(lambda v, p=p: v.tensor_scalar(sb[f"sEx{p}"][:], pb["mm1"][:, 0:128],
                                                  sb[f"ccol{p}"][:], 0.0, ALU.subtract, ALU.max),
                   ins=("mm1", P("ccol")), outs=(P("sEx"),))
            em.act(lambda sc_, p=p: sc_.activation(sb[f"sEm{p}"][:], sb[f"sEx{p}"][:],
                                                   AF.Exp, scale=-1.0),
                   ins=(P("sEx"),), outs=(P("sEm"),))
            em.dve(lambda v, p=p: v.tensor_tensor(sb[f"sEm{p}"][:], sb[f"sEm{p}"][:],
                                                  sb["maskSL"][:], ALU.mult),
                   ins=(P("sEm"), "maskSL"), outs=(P("sEm"),))
            em.dve(lambda v, p=p: v.scalar_tensor_tensor(
                sb[f"Npow{p}"][:, 0:128], pb["mm0"][:, 0:128], sb[f"negb{p}"][:],
                sb[f"sEm{p}"][:], ALU.mult, ALU.mult),
                ins=("mm0", P("negb"), P("sEm")), outs=(P("Npow"),))
            em.te(lambda te, p=p: te.transpose(pb["tr"][:, 0:128], sb[f"Npow{p}"][:, 0:128],
                                               sb["ident"][:]),
                  ins=(P("Npow"), "ident"), outs=("tr",))
            em.dve(lambda v, p=p: v.tensor_copy(sb[f"Ppow{p}"][:, 0:128], pb["tr"][:, 0:128]),
                   ins=("tr",), outs=(P("Ppow"),))
            for k in range(6):
                em.te(lambda te, p=p, k=k: te.matmul(
                    pb["mm2"][:, 0:128], sb[f"Ppow{p}"][:, k * 128:(k + 1) * 128],
                    sb[f"Npow{p}"][:, k * 128:(k + 1) * 128], start=True, stop=True),
                    ins=(P("Ppow"), P("Npow")), outs=("mm2",))
                em.dve(lambda v, p=p, k=k: v.tensor_copy(
                    sb[f"Npow{p}"][:, (k + 1) * 128:(k + 2) * 128], pb["mm2"][:, 0:128]),
                    ins=("mm2",), outs=(P("Npow"),))
                em.te(lambda te, p=p, k=k: te.matmul(
                    pb["mm3"][:, 0:128], sb[f"Npow{p}"][:, k * 128:(k + 1) * 128],
                    sb[f"Ppow{p}"][:, k * 128:(k + 1) * 128], start=True, stop=True),
                    ins=(P("Ppow"), P("Npow")), outs=("mm3",))
                em.dve(lambda v, p=p, k=k: v.tensor_copy(
                    sb[f"Ppow{p}"][:, (k + 1) * 128:(k + 2) * 128], pb["mm3"][:, 0:128]),
                    ins=("mm3",), outs=(P("Ppow"),))
            em.te(lambda te, p=p: te.transpose(pb["tr"][:, 128:256], sb[f"Kd{p}"][:], sb["ident"][:]),
                  ins=(P("Kd"), "ident"), outs=("tr",))
            em.dve(lambda v, p=p: v.tensor_copy(sb[f"Ks{p}"][:], pb["tr"][:, 128:256]),
                   ins=("tr",), outs=(P("Ks"),))
            em.te(lambda te, p=p: te.transpose(pb["tr"][:, 256:384], sb[f"Vd{p}"][:], sb["ident"][:]),
                  ins=(P("Vd"), "ident"), outs=("tr",))
            em.dve(lambda v, p=p: v.tensor_copy(sb[f"Vs{p}"][:], pb["tr"][:, 256:384]),
                   ins=("tr",), outs=(P("Vs"),))
            em.dve(lambda v, p=p: v.tensor_scalar(sb[f"X{p}"][:, 0:128], sb[f"Vs{p}"][:],
                                                  sb[f"bcol{p}"][:], None, ALU.mult),
                   ins=(P("Vs"), P("bcol")), outs=(P("X"),))
            em.dve(lambda v, p=p: v.tensor_scalar(sb[f"X{p}"][:, 128:256], sb[f"Ks{p}"][:],
                                                  sb[f"combo{p}"][:], None, ALU.mult),
                   ins=(P("Ks"), P("combo")), outs=(P("X"),))
            for k in (6, 5, 4, 3, 2, 1, 0):
                em.te(lambda te, p=p, k=k: te.matmul(
                    pb["mm4"][:, 0:256], sb[f"Ppow{p}"][:, k * 128:(k + 1) * 128],
                    sb[f"X{p}"][:], start=True, stop=True),
                    ins=(P("Ppow"), P("X")), outs=("mm4",))
                em.dve(lambda v, p=p: v.tensor_tensor(sb[f"X{p}"][:], sb[f"X{p}"][:],
                                                      pb["mm4"][:, 0:256], ALU.add),
                       ins=(P("X"), "mm4"), outs=(P("X"),))
            em.dve(lambda v, p=p: v.tensor_scalar(sb[f"Ktil{p}"][:], sb[f"Ks{p}"][:],
                                                  sb[f"fcol{p}"][:], None, ALU.mult),
                   ins=(P("Ks"), P("fcol")), outs=(P("Ktil"),))
            em.dve(lambda v, p=p: v.tensor_copy(sb[f"coutc{p}"][:], sb[f"crelrow{p}"][0:1, 1:128:2]),
                   ins=(P("crelrow"),), outs=(P("coutc"),))
            em.te(lambda te, p=p: te.matmul(pb["mm0"][:, 0:64], sb[f"Kd{p}"][:], sb[f"Qd{p}"][:],
                                            start=True, stop=True),
                  ins=(P("Kd"), P("Qd")), outs=("mm0",))
            em.te(lambda te, p=p: te.matmul(pb["mm1"][:, 0:64], sb["ones_row"][:], sb[f"coutc{p}"][:],
                                            start=True, stop=True),
                  ins=("ones_row", P("coutc")), outs=("mm1",))
            em.dve(lambda v, p=p: v.tensor_scalar(sb[f"sD2{p}"][:], pb["mm1"][:, 0:64],
                                                  sb[f"ccol{p}"][:], 0.0, ALU.subtract, ALU.min),
                   ins=("mm1", P("ccol")), outs=(P("sD2"),))
            em.act(lambda sc_, p=p: sc_.activation(sb[f"sE2{p}"][:], sb[f"sD2{p}"][:], AF.Exp),
                   ins=(P("sD2"),), outs=(P("sE2"),))
            em.dve(lambda v, p=p: v.tensor_tensor(sb[f"sE2m{p}"][:], sb[f"sE2{p}"][:],
                                                  sb["mask64T"][:], ALU.mult),
                   ins=(P("sE2"), "mask64T"), outs=(P("sE2m"),))
            em.dve(lambda v, p=p: v.tensor_tensor(sb[f"A2T{p}"][:], pb["mm0"][:, 0:64],
                                                  sb[f"sE2m{p}"][:], ALU.mult),
                   ins=("mm0", P("sE2m")), outs=(P("A2T"),))
            em.te(lambda te, p=p: te.matmul(pb["mm2"][:, 0:64], sb[f"X{p}"][:, 128:256],
                                            sb[f"A2T{p}"][:], start=True, stop=True),
                  ins=(P("X"), P("A2T")), outs=("mm2",))
            em.dve(lambda v, p=p: v.tensor_scalar(sb[f"Zneg{p}"][:], pb["mm2"][:, 0:64],
                                                  -1.0, None, ALU.mult),
                   ins=("mm2",), outs=(P("Zneg"),))
            em.act(lambda sc_, p=p: sc_.activation(sb[f"goutr{p}"][:], sb[f"coutc{p}"][:], AF.Exp),
                   ins=(P("coutc"),), outs=(P("goutr"),))
            em.te(lambda te, p=p: te.matmul(pb["mm3"][:, 0:64], sb["ones_row"][:], sb[f"goutr{p}"][:],
                                            start=True, stop=True),
                  ins=("ones_row", P("goutr")), outs=("mm3",))
            em.dve(lambda v, p=p: v.tensor_tensor(sb[f"qbar{p}"][:], sb[f"Qd{p}"][:],
                                                  pb["mm3"][:, 0:64], ALU.mult),
                   ins=(P("Qd"), "mm3"), outs=(P("qbar"),))

            def omm(te, p=p, SBFL=SBFL):
                te.matmul(pb["mm5"][:, 0:64], sb[f"X{p}"][:, 0:128], sb[f"A2T{p}"][:],
                          start=True, stop=False)
                te.matmul(pb["mm5"][:, 0:64], sb[SBFL][:], sb[f"qbar{p}"][:],
                          start=False, stop=False)
                return te.matmul(pb["mm5"][:, 0:64], sb[SBFL][:], sb[f"Zneg{p}"][:],
                                 start=False, stop=True)
            em.op("te", "te", omm,
                  ins=(P("X"), P("A2T"), SBFL, P("qbar"), P("Zneg")), outs=("mm5",))
            em.te(lambda te, p=p: te.matmul(pb["mm0"][:, 0:128], sb[f"X{p}"][:, 128:256],
                                            sb[f"Ktil{p}"][:], start=True, stop=True),
                  ins=(P("X"), P("Ktil")), outs=("mm0",))
            em.dve(lambda v, p=p: v.tensor_copy(sb[f"GTsb{p}"][:], pb["mm0"][:, 0:128]),
                   ins=("mm0",), outs=(P("GTsb"),))
            em.te(lambda te, p=p: te.matmul(pb["mm1"][:, 0:128], sb[f"Ktil{p}"][:],
                                            sb[f"X{p}"][:, 0:128], start=True, stop=True),
                  ins=(P("Ktil"), P("X")), outs=("mm1",))
            em.te(lambda te, p=p, SBFL=SBFL: te.matmul(pb["mm2"][:, 0:128], sb[f"GTsb{p}"][:],
                                                       sb[SBFL][:], start=True, stop=True),
                  ins=(P("GTsb"), SBFL), outs=("mm2",))
            em.dve(lambda v, p=p, SL=SL, l=l: v.scalar_tensor_tensor(
                sb[f"stmp{l}"][:], sb[SL][:], sb[f"gamcol{p}"][:], pb["mm2"][:, 0:128],
                ALU.mult, ALU.subtract),
                ins=(SL, P("gamcol"), "mm2"), outs=(f"stmp{l}",))
            em.dve(lambda v, p=p, SL=SL, l=l: v.tensor_tensor(
                sb[SL][:], sb[f"stmp{l}"][:], pb["mm1"][:, 0:128], ALU.add),
                ins=(f"stmp{l}", "mm1"), outs=(SL,))
            em.dve(lambda v, SL=SL, SBFL=SBFL: v.tensor_copy(sb[SBFL][:], sb[SL][:]),
                   ins=(SL,), outs=(SBFL,))
            em.dve(lambda v, p=p: v.tensor_copy(sb[f"osb{p}"][:], pb["mm5"][:, 0:64]),
                   ins=("mm5",), outs=(P("osb"),))
            em.dve(lambda v, p=p: v.tensor_tensor(sb[f"osq{p}"][:], sb[f"osb{p}"][:],
                                                  sb[f"osb{p}"][:], ALU.mult),
                   ins=(P("osb"),), outs=(P("osq"),))
            em.te(lambda te, p=p: te.matmul(pb["sm"][0:1, 64:128], sb["ones_col"][:],
                                            sb[f"osq{p}"][:], start=True, stop=True),
                  ins=("ones_col", P("osq")), outs=("sm",))
            em.act(lambda sc_, p=p: sc_.activation(sb[f"r1r{p}"][:], pb["sm"][0:1, 64:128],
                                                   AF.Ln, scale=1.0 / 128.0, bias=EPS),
                   ins=("sm",), outs=(P("r1r"),))
            em.act(lambda sc_, p=p: sc_.activation(sb[f"r2r{p}"][:], sb[f"r1r{p}"][:],
                                                   AF.Exp, scale=-0.5),
                   ins=(P("r1r"),), outs=(P("r2r"),))
            em.te(lambda te, p=p: te.matmul(pb["mm4"][:, 0:64], sb["ones_row"][:], sb[f"r2r{p}"][:],
                                            start=True, stop=True),
                  ins=("ones_row", P("r2r")), outs=("mm4",))
            em.dve(lambda v, p=p: v.tensor_scalar(sb[f"t1o{p}"][:], sb[f"osb{p}"][:],
                                                  sb["normw"][:, 0:1], None, ALU.mult),
                   ins=(P("osb"), "normw"), outs=(P("t1o"),))
            em.dve(lambda v, p=p: v.tensor_tensor(sb[f"t2o{p}"][:], sb[f"t1o{p}"][:],
                                                  pb["mm4"][:, 0:64], ALU.mult),
                   ins=(P("t1o"), "mm4"), outs=(P("t2o"),))
            em.dma("scalar", lambda s, l=l, sc=sc, p=p: s.dma_start(
                out=sb[f"gch{p}"][:], in_=gatescr[l, :, sc * 64:(sc + 1) * 64]),
                ins=(f"gatescr{l}",), outs=(P("gch"),))
            em.dve(lambda v, p=p: v.tensor_tensor(
                sb[f"gout{p}"][:], sb[f"t2o{p}"][:], sb[f"gch{p}"][:], ALU.mult),
                ins=(P("t2o"), P("gch")), outs=(P("gout"),))
            em.dma("sync", lambda s, p=p, l=l, sc=sc: s.dma_start(
                out=t["e1_in"][l * 128:(l + 1) * 128, sc * 64:(sc + 1) * 64],
                in_=sb[f"gout{p}"][:]),
                ins=(P("gout"),), outs=("e1_in",))


def emit_A_wo(nc, em, t, sb, pb, L, resid0, resid0bf=None):
    em.cc(lambda g: g.collective_compute(
        "AllGather", ALU.bypass,
        replica_groups=[[0, 1, 2, 3], [4, 5, 6, 7]],
        ins=[t["e1_in"].opt()], outs=[t["e1_out"].opt()]),
        ins=("e1_in",), outs=("e1_out",))
    em.wait_all("pool", [em.ctx.buf("tqsel").w])
    LQ = L // 4
    W = 256
    NTC = LQ // W

    def dynld(g):
        return nc.values_load(sb["tqsel"][0:1, 0:1], engines=[Pool],
                              min_val=0, max_val=3, skip_runtime_bounds_check=True)
    for tc in range(NTC):
        em.dma("pool", lambda g, tc=tc: g.dma_start(
            out=sb["ogth"][:].rearrange("p (a w) -> p a w", a=8),
            in_=t["e1_out"].rearrange("(a p) t -> p a t", p=128)[
                :, :, bass.ts(dynld(g) * NTC + tc, W)]),
            ins=("e1_out",), outs=tuple(f"ogth{a}" for a in range(8)))
        oins = tuple(f"ogth{a}" for a in range(8))
        for ft in range(16):
            bank = ("mm0", "mm1")[ft % 2]
            em.dma("pool", lambda g, ft=ft: g.dma_start(
                out=sb["wo_sb"][:].rearrange("p (a o) -> p a o", a=8),
                in_=t["wo"][:, ft * 128:(ft + 1) * 128].rearrange("(a p) o -> p a o", p=128)),
                outs=("wo_sb",))

            def mmwo(te, ft=ft, bank=bank):
                last = None
                for a in range(8):
                    last = te.matmul(pb[bank][:, 0:W],
                                     sb["wo_sb"][:, a * 128:(a + 1) * 128],
                                     sb["ogth"][:, a * W:(a + 1) * W],
                                     start=(a == 0), stop=(a == 7))
                return last
            em.op("te", "te", mmwo, ins=("wo_sb",) + oins, outs=(bank,))
            em.dma("sync", lambda s, ft=ft, tc=tc: s.dma_start(
                out=sb["xres"][:],
                in_=t["xq"][ft * 128:(ft + 1) * 128, tc * W:(tc + 1) * W]),
                ins=(), outs=("xres",))
            em.dve(lambda v, ft=ft, bank=bank: v.tensor_tensor(
                sb["preLN"][:, ft * W:(ft + 1) * W], pb[bank][:, 0:W],
                sb["xres"][:], ALU.add),
                ins=(bank, "xres"), outs=(f"preLN{ft}",))
        emit_LN(nc, em, sb, pb, ntiles=16, width=W,
                consume=mk_r0_writer(em, sb, t, resid0, resid0bf, tc, W))
    return


def mk_r0_writer(em, sb, t, resid0, resid0bf, tc, W):
    def consume(ft):
        half = ft % 2
        em.dma("sync", lambda s, ft=ft, tc=tc: s.dma_start(
            out=resid0[ft * 128:(ft + 1) * 128, tc * W:(tc + 1) * W],
            in_=sb["outt"][:, (ft % 2) * W:(ft % 2 + 1) * W]),
            ins=(f"outt{half}",), outs=(f"r0out_{ft}_{tc}",))
        if resid0bf is not None:
            em.dma("pool", lambda g, ft=ft, tc=tc: g.dma_start(
                out=resid0bf[ft * 128:(ft + 1) * 128, tc * W:(tc + 1) * W],
                in_=sb["outt"][:, (ft % 2) * W:(ft % 2 + 1) * W]),
                ins=(f"outt{half}",), outs=(f"r0bf_{ft}_{tc}",))
    return consume


def emit_LN(nc, em, sb, pb, ntiles, width, consume):
    for ft in range(ntiles):
        em.dve(lambda v, ft=ft: v.tensor_copy(sb["pbf"][:],
                                              sb["preLN"][:, ft * width:(ft + 1) * width]),
               ins=(f"preLN{ft}",), outs=("pbf",))
        em.act(lambda sc_, ft=ft: sc_.activation(sb["sqbf"][:],
                                                 sb["preLN"][:, ft * width:(ft + 1) * width],
                                                 AF.Square),
               ins=(f"preLN{ft}",), outs=("sqbf",))
        em.te(lambda te, ft=ft: te.matmul(pb["sm"][0:1, 0:width], sb["onescol_b"][:],
                                          sb["pbf"][:], start=(ft == 0), stop=(ft == ntiles - 1)),
              ins=("pbf", "onescol_b"), outs=("sm",))
        em.te(lambda te, ft=ft: te.matmul(pb["mm5"][0:1, 0:width], sb["onescol_b"][:],
                                          sb["sqbf"][:], start=(ft == 0), stop=(ft == ntiles - 1)),
              ins=("sqbf", "onescol_b"), outs=("mm5",))
    n = float(ntiles * 128)
    em.dve(lambda v: v.tensor_scalar(sb["mrow"][0:1, 0:width], pb["sm"][0:1, 0:width],
                                     1.0 / n, None, ALU.mult),
           ins=("sm",), outs=("mrow",))
    em.dve(lambda v: v.tensor_tensor(sb["m2row"][0:1, 0:width], sb["mrow"][0:1, 0:width],
                                     sb["mrow"][0:1, 0:width], ALU.mult),
           ins=("mrow",), outs=("m2row",))
    em.dve(lambda v: v.tensor_scalar(sb["v1row"][0:1, 0:width], pb["mm5"][0:1, 0:width],
                                     1.0 / n, None, ALU.mult),
           ins=("mm5",), outs=("v1row",))
    em.dve(lambda v: v.tensor_tensor(sb["varrow"][0:1, 0:width], sb["v1row"][0:1, 0:width],
                                     sb["m2row"][0:1, 0:width], ALU.subtract),
           ins=("v1row", "m2row"), outs=("varrow",))
    em.act(lambda sc_: sc_.activation(sb["lnr1"][0:1, 0:width], sb["varrow"][0:1, 0:width],
                                      AF.Ln, bias=EPS),
           ins=("varrow",), outs=("lnr1",))
    em.act(lambda sc_: sc_.activation(sb["lnr2"][0:1, 0:width], sb["lnr1"][0:1, 0:width],
                                      AF.Exp, scale=-0.5),
           ins=("lnr1",), outs=("lnr2",))
    em.te(lambda te: te.matmul(pb["mm2"][:, 0:width], sb["ones_row"][:],
                               sb["mrow"][0:1, 0:width], start=True, stop=True),
          ins=("ones_row", "mrow"), outs=("mm2",))
    em.te(lambda te: te.matmul(pb["mm3"][:, 0:width], sb["ones_row"][:],
                               sb["lnr2"][0:1, 0:width], start=True, stop=True),
          ins=("ones_row", "lnr2"), outs=("mm3",))
    for ft in range(ntiles):
        half = ft % 2
        em.dve(lambda v, ft=ft: v.tensor_tensor(sb["xres"][:],
                                                sb["preLN"][:, ft * width:(ft + 1) * width],
                                                pb["mm2"][:, 0:width], ALU.subtract),
               ins=(f"preLN{ft}", "mm2"), outs=("xres",))
        em.dve(lambda v, half=half: v.tensor_tensor(sb["outt"][:, half * width:(half + 1) * width],
                                                    sb["xres"][:], pb["mm3"][:, 0:width],
                                                    ALU.mult),
               ins=("xres", "mm3"), outs=(f"outt{half}",))
        consume(ft)



def host_prep_B(inputs, L=8192):
    wg = np.asarray(inputs["L0_ffn_w_gate"])
    wu = np.asarray(inputs["L0_ffn_w_up"])
    wd = np.asarray(inputs["L0_ffn_w_down"])
    sg = np.asarray(inputs["L0_ffn_s_gate"]).astype(np.float32)
    su = np.asarray(inputs["L0_ffn_s_up"]).astype(np.float32)
    sd = np.asarray(inputs["L0_ffn_s_down"]).astype(np.float32)
    consts = host_constants()
    maps = []
    for c in range(N_CORES):
        m = {}
        m["wg8"] = wg
        m["wu8"] = wu
        m["wd8"] = wd
        m["sgate"] = np.ascontiguousarray(sg.reshape(64, 128).T)
        m["sup"] = np.ascontiguousarray(su.reshape(64, 128).T)
        m["sdown"] = np.ascontiguousarray(sd.reshape(16, 128).T)
        for k in ("ones_row", "ones_col", "cmisc"):
            m[k] = consts[k]
        maps.append(m)
    return maps


def declare_B(nc, L):
    t = {}
    t["wg8"] = nc.dram_tensor("wg8", [HID, 8192], I8, kind="ExternalInput").ap()
    t["wu8"] = nc.dram_tensor("wu8", [HID, 8192], I8, kind="ExternalInput").ap()
    t["wd8"] = nc.dram_tensor("wd8", [8192, HID], I8, kind="ExternalInput").ap()
    t["sgate"] = nc.dram_tensor("sgate", [128, 64], F32, kind="ExternalInput").ap()
    t["sup"] = nc.dram_tensor("sup", [128, 64], F32, kind="ExternalInput").ap()
    t["sdown"] = nc.dram_tensor("sdown", [128, 16], F32, kind="ExternalInput").ap()
    for nme, shp in (("ones_row", [1, 128]), ("ones_col", [128, 1]),
                     ("cmisc", [128, len(CONST_VALS)])):
        t[nme] = nc.dram_tensor(nme, shp, F32, kind="ExternalInput").ap()
    return t


def alloc_B(nc, stack, L):
    sb = {}

    def A(name, shape, dt):
        sb[name] = stack.enter_context(nc.sbuf_tensor("sbB_" + name, shape, dt))
    A("cmisc", [128, len(CONST_VALS)], F32)
    A("ones_row", [1, 128], F32)
    A("ones_col", [128, 1], F32)
    A("onescol_b", [128, 1], BF16)
    A("sgate", [128, 64], F32)
    A("sup", [128, 64], F32)
    A("sdown", [128, 16], F32)
    A("r0f", [128, 16 * 512], F32)
    A("r0b", [128, 16 * 512], BF16)
    A("wgt", [128, 2 * 2048], BF16)
    A("wut", [128, 2 * 2048], BF16)
    A("wdt", [128, 2 * 768], BF16)
    A("ybuf", [128, 64 * 512], BF16)
    A("gtile", [128, 512], BF16)
    A("utile", [128, 512], BF16)
    A("preLN", [128, 16 * 512], F32)
    A("pbf", [128, 512], BF16)
    A("sqbf", [128, 512], BF16)
    A("xres", [128, 512], F32)
    A("outt", [128, 1024], F32)
    for nme in ("mrow", "m2row", "v1row", "varrow", "lnr1", "lnr2"):
        A(nme, [1, 512], F32)
    return sb


def emit_const_load(nc, em, t, sb, names=("ones_row", "ones_col", "cmisc")):
    for nme in names:
        em.dma("sync", lambda s, nme=nme: s.dma_start(out=sb[nme][:], in_=t[nme][:]),
               outs=(nme,))
    if "onescol_b" in sb:
        em.dma("pool", lambda g: g.dma_start(out=sb["onescol_b"][:], in_=t["ones_col"][:]),
               outs=("onescol_b",))
    for i, vv in enumerate(CONST_VALS):
        if vv in (0.0, 1.0, -1.0, 2.0, 0.5):
            continue
        try:
            nc.const_aps.register(F32, vv, sb["cmisc"][:, i:i + 1])
        except Exception:
            pass
    tok = em.ctx.buf("cmisc").w
    for e in ("act", "dve", "te", "pool", "sync", "scalar"):
        em.wait_all(e, [tok])


def emit_B(nc, em, t, sb, pb, L, resid0_in, resid1_out, resid1bf_out=None):
    LQ = L // 4
    NTC = LQ // 512
    emit_const_load(nc, em, t, sb)
    for nme in ("sgate", "sup", "sdown"):
        em.dma("sync", lambda s, nme=nme: s.dma_start(out=sb[nme][:], in_=t[nme][:]),
               outs=(nme,))
    for tc in range(NTC):
        cs = slice(tc * 512, (tc + 1) * 512)
        for ft in range(16):
            em.dma("sync", lambda s, ft=ft, cs=cs: s.dma_start(
                out=sb["r0f"][:, ft * 512:(ft + 1) * 512],
                in_=resid0_in[ft * 128:(ft + 1) * 128, cs]), outs=(f"r0f{ft}",))
            em.dma("pool", lambda g, ft=ft, cs=cs: g.dma_start(
                out=sb["r0b"][:, ft * 512:(ft + 1) * 512],
                in_=resid0_in[ft * 128:(ft + 1) * 128, cs]), outs=(f"r0b{ft}",))
        rins = tuple(f"r0b{ft}" for ft in range(16))
        for ot in range(64):
            par = ot % 2
            em.dma("pool", lambda g, ot=ot, par=par: g.dma_start(
                out=sb["wgt"][:, par * 2048:(par + 1) * 2048].rearrange("p (a o) -> p a o", a=16),
                in_=t["wg8"][:, ot * 128:(ot + 1) * 128].rearrange("(a p) o -> p a o", p=128)),
                outs=(f"wgt{par}",))
            em.dma("pool", lambda g, ot=ot, par=par: g.dma_start(
                out=sb["wut"][:, par * 2048:(par + 1) * 2048].rearrange("p (a o) -> p a o", a=16),
                in_=t["wu8"][:, ot * 128:(ot + 1) * 128].rearrange("(a p) o -> p a o", p=128)),
                outs=(f"wut{par}",))

            def mmg(te, par=par):
                last = None
                for a in range(16):
                    last = te.matmul(pb["mm0"][:, 0:512],
                                     sb["wgt"][:, par * 2048 + a * 128: par * 2048 + (a + 1) * 128],
                                     sb["r0b"][:, a * 512:(a + 1) * 512],
                                     start=(a == 0), stop=(a == 15))
                return last
            em.op("te", "te", mmg, ins=(f"wgt{par}",) + rins, outs=("mm0",))

            def mmu(te, par=par):
                last = None
                for a in range(16):
                    last = te.matmul(pb["mm1"][:, 0:512],
                                     sb["wut"][:, par * 2048 + a * 128: par * 2048 + (a + 1) * 128],
                                     sb["r0b"][:, a * 512:(a + 1) * 512],
                                     start=(a == 0), stop=(a == 15))
                return last
            em.op("te", "te", mmu, ins=(f"wut{par}",) + rins, outs=("mm1",))
            em.act(lambda sc_, ot=ot: sc_.activation(sb["gtile"][:], pb["mm0"][:, 0:512],
                                                     AF.Silu, scale=sb["sgate"][:, ot:ot + 1]),
                   ins=("mm0", "sgate"), outs=("gtile",))
            em.dve(lambda v, ot=ot: v.tensor_scalar(sb["utile"][:], pb["mm1"][:, 0:512],
                                                    sb["sup"][:, ot:ot + 1], None, ALU.mult),
                   ins=("mm1", "sup"), outs=("utile",))
            em.dve(lambda v, ot=ot: v.tensor_tensor(sb["ybuf"][:, ot * 512:(ot + 1) * 512],
                                                    sb["gtile"][:], sb["utile"][:], ALU.mult),
                   ins=("gtile", "utile"), outs=(f"yb{ot}",))
        yins = tuple(f"yb{ot}" for ot in range(64))
        groups = [(0, 6), (6, 12), (12, 16)]
        for g0, g1 in groups:
            ncols = (g1 - g0) * 128
            for kt in range(64):
                par = kt % 2
                em.dma("pool", lambda g, kt=kt, g0=g0, ncols=ncols, par=par: g.dma_start(
                    out=sb["wdt"][:, par * 768: par * 768 + ncols],
                    in_=t["wd8"][kt * 128:(kt + 1) * 128, g0 * 128: g0 * 128 + ncols]),
                    outs=(f"wdt{par}",))

                def mmd(te, kt=kt, g0=g0, g1=g1, par=par):
                    last = None
                    for i, o2 in enumerate(range(g0, g1)):
                        last = te.matmul(pb[f"mm{i}"][:, 0:512],
                                         sb["wdt"][:, par * 768 + i * 128: par * 768 + (i + 1) * 128],
                                         sb["ybuf"][:, kt * 512:(kt + 1) * 512],
                                         start=(kt == 0), stop=(kt == 63))
                    return last
                em.op("te", "te", mmd, ins=(f"wdt{par}",) + yins,
                      outs=tuple(f"mm{i}" for i in range(g1 - g0)))
            for i, o2 in enumerate(range(g0, g1)):
                em.dve(lambda v, i=i, o2=o2: v.scalar_tensor_tensor(
                    sb["preLN"][:, o2 * 512:(o2 + 1) * 512], pb[f"mm{i}"][:, 0:512],
                    sb["sdown"][:, o2:o2 + 1], sb["r0f"][:, o2 * 512:(o2 + 1) * 512],
                    ALU.mult, ALU.add),
                    ins=(f"mm{i}", "sdown", f"r0f{o2}"), outs=(f"preLN{o2}",))
        emit_LN(nc, em, sb, pb, ntiles=16, width=512,
                consume=mk_ln_writer(em, sb, resid1_out, resid1bf_out, tc, 512, "r1o"))


def mk_ln_writer(em, sb, out_ap, outbf_ap, tc, W, tag):
    def consume(ft):
        half = ft % 2
        em.dma("sync", lambda s, ft=ft, tc=tc: s.dma_start(
            out=out_ap[ft * 128:(ft + 1) * 128, tc * W:(tc + 1) * W],
            in_=sb["outt"][:, (ft % 2) * W:(ft % 2 + 1) * W]),
            ins=(f"outt{half}",), outs=(f"{tag}_{ft}_{tc}",))
        if outbf_ap is not None:
            em.dma("pool", lambda g, ft=ft, tc=tc: g.dma_start(
                out=outbf_ap[ft * 128:(ft + 1) * 128, tc * W:(tc + 1) * W],
                in_=sb["outt"][:, (ft % 2) * W:(ft % 2 + 1) * W]),
                ins=(f"outt{half}",), outs=(f"{tag}bf_{ft}_{tc}",))
    return consume



def host_prep_C(inputs, L=8192):
    Wq = np.asarray(inputs["L1_attn_Wq"]).astype(np.float32)
    Wk = np.asarray(inputs["L1_attn_Wk"]).astype(np.float32)
    Wv = np.asarray(inputs["L1_attn_Wv"]).astype(np.float32)
    Wo = np.asarray(inputs["L1_attn_Wo"]).astype(np.float32)
    consts = host_constants()
    NTQ = L // 512
    cm = np.zeros((128, 4 * 512), dtype=np.float32)
    tk = np.arange(128)[:, None]
    tq = np.arange(512)[None, :]
    for j in range(4):
        cm[:, j * 512:(j + 1) * 512] = ((tk + 128 * j) <= tq).astype(np.float32)
    denoff = np.zeros((1, NTQ), dtype=np.float32)
    for c in range(NTQ):
        denoff[0, c] = float(L - (c + 1) * 512)
    maps = []
    for c in range(N_CORES):
        h0 = 4 * (c % 4)
        m = {}
        cols = []
        for W in (Wq, Wk, Wv):
            for h in range(h0, h0 + 4):
                cols.append(W[:, h * 128:(h + 1) * 128])
        m["wqkvC"] = np.ascontiguousarray(np.concatenate(cols, axis=1))
        worder = []
        for r in range(4):
            for hi in (0, 1):
                worder.append(4 * r + hi)
        for r in range(4):
            for hi in (2, 3):
                worder.append(4 * r + hi)
        wor = np.concatenate([Wo[h * 128:(h + 1) * 128, :] for h in worder], axis=0)
        m["woC"] = np.ascontiguousarray(wor)
        m["cmask"] = cm
        m["denoff"] = denoff
        m["tqselC"] = np.array([[c % 4]], dtype=np.uint32)
        m["identC"] = np.eye(128, dtype=np.float32)
        for k in ("ones_row", "ones_col", "cmisc"):
            m[k + "C"] = consts[k]
        maps.append(m)
    return maps


def declare_C(nc, L):
    t = {}
    t["wqkvC"] = nc.dram_tensor("wqkvC", [HID, 1536], F32, kind="ExternalInput").ap()
    t["woC"] = nc.dram_tensor("woC", [HID, HID], F32, kind="ExternalInput").ap()
    t["cmask"] = nc.dram_tensor("cmask", [128, 2048], F32, kind="ExternalInput").ap()
    t["denoff"] = nc.dram_tensor("denoff", [1, L // 512], F32, kind="ExternalInput").ap()
    t["tqsel"] = nc.dram_tensor("tqselC", [1, 1], U32, kind="ExternalInput").ap()
    t["identC"] = nc.dram_tensor("identC", [128, 128], F32, kind="ExternalInput").ap()
    for nme, shp in (("ones_row", [1, 128]), ("ones_col", [128, 1]),
                     ("cmisc", [128, len(CONST_VALS)])):
        t[nme] = nc.dram_tensor(nme + "C", shp, F32, kind="ExternalInput").ap()
    t["r1g_in"] = nc.dram_tensor("r1g_in", [HID, L // 4], BF16, kind="Internal").ap()
    t["r1ga"] = nc.dram_tensor("r1ga", [4 * 1024, L // 4], BF16, kind="Internal").ap()
    t["r1gb"] = nc.dram_tensor("r1gb", [4 * 1024, L // 4], BF16, kind="Internal").ap()
    t["qkvscr"] = nc.dram_tensor("qkvscrC", [12, 128, L], BF16, kind="Internal").ap()
    t["e3_in"] = nc.dram_tensor("e3_in", [512, L], BF16, kind="Internal").ap()
    t["e3cat"] = nc.dram_tensor("e3cat", [2048, L], BF16, kind="Internal").ap()
    t["e3a"] = t["e3cat"][0:1024, :]
    t["e3b"] = t["e3cat"][1024:2048, :]
    return t


def alloc_C(nc, stack, L):
    sb = {}

    def A(name, shape, dt):
        sb[name] = stack.enter_context(nc.sbuf_tensor("sbC_" + name, shape, dt))
    A("cmisc", [128, len(CONST_VALS)], F32)
    A("ones_row", [1, 128], F32)
    A("ones_col", [128, 1], F32)
    A("onescol_b", [128, 1], BF16)
    A("wqkv", [128, 16 * 1536], BF16)
    A("cmask", [128, 2048], F32)
    A("denoff", [1, L // 512], F32)
    A("tqsel", [1, 1], U32)
    A("identC", [128, 128], BF16)
    A("rch", [128, 16 * 512], BF16)
    A("stg", [128, 512], BF16)
    A("ptile", [128, 512], BF16)
    A("kbuf", [128, L], BF16)
    A("vbuf", [128, L], BF16)
    A("qbuf", [128, L], BF16)
    A("vtb", [128, L], BF16)
    A("vsums", [128, 16], F32)
    A("vpref", [128, 16], F32)
    A("negsuf", [128, 16], F32)
    A("vtot", [128, 1], F32)
    A("zeros16", [128, 16], F32)
    A("sE", [128, 4 * 512], F32)
    A("Ew", [128, 4 * 512], BF16)
    A("numo", [128, 512], F32)
    A("oatt", [128, 512], BF16)
    A("denr", [1, 512], F32)
    A("brec", [1, 512], F32)
    A("wo_sb", [128, 16 * 128], BF16)
    A("ogth", [128, 16 * 256], BF16)
    A("xres", [128, 256], F32)
    A("preLN", [128, 16 * 256], F32)
    A("pbf", [128, 256], BF16)
    A("sqbf", [128, 256], BF16)
    A("outt", [128, 512], F32)
    for nme in ("mrow", "m2row", "v1row", "varrow", "lnr1", "lnr2"):
        A(nme, [1, 256], F32)
    return sb


def emit_C(nc, em, t, sb, pb, L, r1f_in, r2_out, r2bf_out=None, invsq_idx=12):
    LQ = L // 4
    NTCq = LQ // 512
    NTQ = L // 512
    NTK = L // 128
    emit_const_load(nc, em, t, sb)
    em.dma("pool", lambda g: g.dma_start(
        out=sb["wqkv"][:].rearrange("p (a o) -> p a o", a=16),
        in_=t["wqkvC"].rearrange("(a p) o -> p a o", p=128)), outs=("wqkv",))
    em.dma("sync", lambda s: s.dma_start(out=sb["cmask"][:], in_=t["cmask"][:]), outs=("cmask",))
    em.dma("sync", lambda s: s.dma_start(out=sb["denoff"][:], in_=t["denoff"][:]), outs=("denoff",))
    em.dma("sync", lambda s: s.dma_start(out=sb["tqsel"][:], in_=t["tqsel"][:]), outs=("tqsel",))
    em.dma("pool", lambda g: g.dma_start(out=sb["identC"][:], in_=t["identC"][:]), outs=("identC",))
    for ft in range(16):
        for tcq in range(NTCq):
            em.dma("pool", lambda g, ft=ft, tcq=tcq: g.dma_start(
                out=sb["stg"][:], in_=r1f_in[ft * 128:(ft + 1) * 128, tcq * 512:(tcq + 1) * 512]),
                outs=("stg",))
            em.dma("sync", lambda s, ft=ft, tcq=tcq: s.dma_start(
                out=t["r1g_in"][ft * 128:(ft + 1) * 128, tcq * 512:(tcq + 1) * 512],
                in_=sb["stg"][:]), ins=("stg",), outs=("r1g_in",))
    em.cc(lambda g: g.collective_compute(
        "AllGather", ALU.bypass,
        replica_groups=[[0, 1, 2, 3], [4, 5, 6, 7]],
        ins=[t["r1g_in"][0:1024, :].opt()], outs=[t["r1ga"].opt()]),
        ins=("r1g_in",), outs=("r1ga",))
    em.cc(lambda g: g.collective_compute(
        "AllGather", ALU.bypass,
        replica_groups=[[0, 1, 2, 3], [4, 5, 6, 7]],
        ins=[t["r1g_in"][1024:2048, :].opt()], outs=[t["r1gb"].opt()]),
        ins=("r1g_in",), outs=("r1gb",))
    for r in range(4):
        for tcq in range(NTCq):
            for ft in range(16):
                gk, fl = ("r1ga", ft) if ft < 8 else ("r1gb", ft - 8)
                em.dma("pool", lambda g, r=r, tcq=tcq, gk=gk, fl=fl, ft=ft: g.dma_start(
                    out=sb["rch"][:, ft * 512:(ft + 1) * 512],
                    in_=t[gk][r * 1024 + fl * 128: r * 1024 + (fl + 1) * 128,
                              tcq * 512:(tcq + 1) * 512]),
                    ins=(gk,), outs=(f"rch{ft}",))
            rins = tuple(f"rch{ft}" for ft in range(16))
            for ot in range(12):
                bank = ("mm0", "mm1")[ot % 2]

                def mmp(te, ot=ot, bank=bank):
                    last = None
                    for a in range(16):
                        last = te.matmul(pb[bank][:, 0:512],
                                         sb["wqkv"][:, a * 1536 + ot * 128: a * 1536 + (ot + 1) * 128],
                                         sb["rch"][:, a * 512:(a + 1) * 512],
                                         start=(a == 0), stop=(a == 15))
                    return last
                em.op("te", "te", mmp, ins=("wqkv",) + rins, outs=(bank,))
                em.dve(lambda v, bank=bank: v.tensor_copy(sb["ptile"][:], pb[bank][:, 0:512]),
                       ins=(bank,), outs=("ptile",))
                em.dma("sync", lambda s, ot=ot, r=r, tcq=tcq: s.dma_start(
                    out=t["qkvscr"][ot, :, r * LQ + tcq * 512: r * LQ + (tcq + 1) * 512],
                    in_=sb["ptile"][:]), ins=("ptile",), outs=(f"qkv{ot}",))
    for hi in range(4):
        em.dma("pool", lambda g, hi=hi: g.dma_start(out=sb["qbuf"][:], in_=t["qkvscr"][hi]),
               ins=(f"qkv{hi}",), outs=("qbuf",))
        em.dma("pool", lambda g, hi=hi: g.dma_start(out=sb["kbuf"][:], in_=t["qkvscr"][4 + hi]),
               ins=(f"qkv{4 + hi}",), outs=("kbuf",))
        em.dma("pool", lambda g, hi=hi: g.dma_start(out=sb["vbuf"][:], in_=t["qkvscr"][8 + hi]),
               ins=(f"qkv{8 + hi}",), outs=("vbuf",))
        for j in range(NTK):
            em.te(lambda te, j=j: te.transpose(
                pb["tr"][:, 0:128],
                sb["vbuf"][:, j * 128:(j + 1) * 128], sb["identC"][:]),
                ins=("vbuf", "identC"), outs=("tr",))
            em.dve(lambda v, j=j: v.tensor_copy(
                sb["vtb"][:, j * 128:(j + 1) * 128], pb["tr"][:, 0:128]),
                ins=("tr",), outs=("vtb",))
        em.dve(lambda v: v.memset(sb["zeros16"][:], 0.0), outs=("zeros16",))
        for cch in range(NTQ):
            em.dve(lambda v, cch=cch: v.reduce_sum(
                sb["vsums"][:, cch:cch + 1], sb["vbuf"][:, cch * 512:(cch + 1) * 512],
                mybir.AxisListType.X), ins=("vbuf",), outs=("vsums",))
        em.dve(lambda v: v.tensor_tensor_scan(
            sb["vpref"][:, 0:NTQ], sb["vsums"][:, 0:NTQ], sb["zeros16"][:, 0:NTQ], 0.0,
            ALU.add, ALU.add), ins=("vsums", "zeros16"), outs=("vpref",))
        em.dve(lambda v: v.tensor_copy(sb["vtot"][:], sb["vpref"][:, NTQ - 1:NTQ]),
               ins=("vpref",), outs=("vtot",))
        em.dve(lambda v: v.tensor_scalar(sb["negsuf"][:, 0:NTQ], sb["vpref"][:, 0:NTQ],
                                         sb["vtot"][:, 0:1], None, ALU.subtract),
               ins=("vpref", "vtot"), outs=("negsuf",))
        for cq in range(NTQ):
            ntk = 4 * (cq + 1)
            for gbase in range(0, ntk, 4):
                gn = min(4, ntk - gbase)
                for gi in range(gn):
                    tkt = gbase + gi
                    sbank = ("mm0", "mm1")[tkt % 2]
                    em.te(lambda te, tkt=tkt, cq=cq, sbank=sbank: te.matmul(
                        pb[sbank][:, 0:512], sb["kbuf"][:, tkt * 128:(tkt + 1) * 128],
                        sb["qbuf"][:, cq * 512:(cq + 1) * 512], start=True, stop=True),
                        ins=("kbuf", "qbuf"), outs=(sbank,))
                    if tkt >= 4 * cq:
                        j = tkt - 4 * cq
                        em.dve(lambda v, j=j, gi=gi, sbank=sbank: v.tensor_tensor(
                            sb["sE"][:, gi * 512:(gi + 1) * 512], pb[sbank][:, 0:512],
                            sb["cmask"][:, j * 512:(j + 1) * 512], ALU.mult),
                            ins=(sbank, "cmask"), outs=(f"sE{gi}",))
                    else:
                        em.dve(lambda v, gi=gi, sbank=sbank: v.tensor_copy(
                            sb["sE"][:, gi * 512:(gi + 1) * 512], pb[sbank][:, 0:512]),
                            ins=(sbank,), outs=(f"sE{gi}",))
                em.act(lambda sc_, gn=gn: sc_.activation(
                    sb["Ew"][:, 0:gn * 512], sb["sE"][:, 0:gn * 512], AF.Exp,
                    scale=sb["cmisc"][:, invsq_idx:invsq_idx + 1]),
                    ins=tuple(f"sE{gi}" for gi in range(gn)), outs=("Ew",))
                for gi in range(gn):
                    tkt = gbase + gi
                    em.te(lambda te, tkt=tkt, gi=gi, ntk=ntk: te.matmul(
                        pb["mm2"][:, 0:512], sb["vtb"][:, tkt * 128:(tkt + 1) * 128],
                        sb["Ew"][:, gi * 512:(gi + 1) * 512],
                        start=(tkt == 0), stop=(tkt == ntk - 1)),
                        ins=("vtb", "Ew"), outs=("mm2",))
                    em.te(lambda te, tkt=tkt, gi=gi, ntk=ntk: te.matmul(
                        pb["mm3"][0:1, 0:512], sb["onescol_b"][:],
                        sb["Ew"][:, gi * 512:(gi + 1) * 512],
                        start=(tkt == 0), stop=(tkt == ntk - 1)),
                        ins=("onescol_b", "Ew"), outs=("mm3",))
            em.dve(lambda v, cq=cq: v.tensor_scalar(
                sb["denr"][0:1, 0:512], pb["mm3"][0:1, 0:512],
                sb["denoff"][0:1, cq:cq + 1], None, ALU.add),
                ins=("mm3", "denoff"), outs=("denr",))
            em.dve(lambda v: v.reciprocal(sb["brec"][0:1, 0:512], sb["denr"][0:1, 0:512]),
                   ins=("denr",), outs=("brec",))
            em.te(lambda te: te.matmul(pb["mm4"][:, 0:512], sb["ones_row"][:],
                                       sb["brec"][0:1, 0:512], start=True, stop=True),
                  ins=("ones_row", "brec"), outs=("mm4",))
            em.dve(lambda v, cq=cq: v.tensor_scalar(
                sb["numo"][:], pb["mm2"][:, 0:512], sb["negsuf"][:, cq:cq + 1], None,
                ALU.subtract), ins=("mm2", "negsuf"), outs=("numo",))
            em.dve(lambda v: v.tensor_tensor(sb["oatt"][:], sb["numo"][:],
                                             pb["mm4"][:, 0:512], ALU.mult),
                   ins=("numo", "mm4"), outs=("oatt",))
            em.dma("sync", lambda s, hi=hi, cq=cq: s.dma_start(
                out=t["e3_in"][hi * 128:(hi + 1) * 128, cq * 512:(cq + 1) * 512],
                in_=sb["oatt"][:]), ins=("oatt",), outs=("e3_in",))
    em.cc(lambda g: g.collective_compute(
        "AllGather", ALU.bypass,
        replica_groups=[[0, 1, 2, 3], [4, 5, 6, 7]],
        ins=[t["e3_in"][0:256, :].opt()], outs=[t["e3a"].opt()]),
        ins=("e3_in",), outs=("e3a",))
    em.cc(lambda g: g.collective_compute(
        "AllGather", ALU.bypass,
        replica_groups=[[0, 1, 2, 3], [4, 5, 6, 7]],
        ins=[t["e3_in"][256:512, :].opt()], outs=[t["e3b"].opt()]),
        ins=("e3_in",), outs=("e3b",))
    em.wait_all("sync", [em.ctx.buf("tqsel").w])
    W = 256
    NTC = LQ // W

    def dynldC(s):
        return nc.values_load(sb["tqsel"][0:1, 0:1], engines=[mybir.EngineType.SP],
                              min_val=0, max_val=3, skip_runtime_bounds_check=True)
    for tc in range(NTC):
        em.dma("sync", lambda s, tc=tc: s.dma_start(
            out=sb["ogth"][:].rearrange("p (a w) -> p a w", a=16),
            in_=t["e3cat"].rearrange("(a p) t -> p a t", p=128)[
                :, :, bass.ts(dynldC(s) * NTC + tc, W)]),
            ins=("e3a", "e3b"), outs=tuple(f"ogth{a}" for a in range(16)))
        oins = tuple(f"ogth{a}" for a in range(16))
        for ft in range(16):
            bank = ("mm0", "mm1")[ft % 2]
            em.dma("pool", lambda g, ft=ft: g.dma_start(
                out=sb["wo_sb"][:].rearrange("p (a o) -> p a o", a=16),
                in_=t["woC"][:, ft * 128:(ft + 1) * 128].rearrange("(a p) o -> p a o", p=128)),
                outs=("wo_sb",))

            def mmwo(te, ft=ft, bank=bank):
                last = None
                for a in range(16):
                    last = te.matmul(pb[bank][:, 0:W],
                                     sb["wo_sb"][:, a * 128:(a + 1) * 128],
                                     sb["ogth"][:, a * W:(a + 1) * W],
                                     start=(a == 0), stop=(a == 15))
                return last
            em.op("te", "te", mmwo, ins=("wo_sb",) + oins, outs=(bank,))
            em.dma("scalar", lambda s, ft=ft, tc=tc: s.dma_start(
                out=sb["xres"][:],
                in_=r1f_in[ft * 128:(ft + 1) * 128, tc * W:(tc + 1) * W]),
                ins=(), outs=("xres",))
            em.dve(lambda v, ft=ft, bank=bank: v.tensor_tensor(
                sb["preLN"][:, ft * W:(ft + 1) * W], pb[bank][:, 0:W],
                sb["xres"][:], ALU.add),
                ins=(bank, "xres"), outs=(f"preLN{ft}",))
        emit_LN(nc, em, sb, pb, ntiles=16, width=W,
                consume=mk_ln_writer(em, sb, r2_out, r2bf_out, tc, W, "r2o"))



def _sylvester(n):
    H = np.array([[1.0]], dtype=np.float32)
    while H.shape[0] < n:
        H = np.block([[H, H], [H, -H]])
    return H.astype(np.float32)


def host_prep_D(inputs, L=8192):
    SU_ug = np.asarray(inputs["L1_ffn_SU_ug"]).astype(np.float32)
    SU_dp = np.asarray(inputs["L1_ffn_SU_dp"]).astype(np.float32)
    Wsug = np.asarray(inputs["L1_ffn_Wscale_ug"]).astype(np.float32)
    Wsdp = np.asarray(inputs["L1_ffn_Wscale_dp"]).astype(np.float32)
    W_up = np.asarray(inputs["L1_ffn_W_up"]).astype(np.float32)
    W_gate = np.asarray(inputs["L1_ffn_W_gate"]).astype(np.float32)
    W_down = np.asarray(inputs["L1_ffn_W_down"]).astype(np.float32)
    consts = host_constants()
    perm1 = np.array([p * 128 + q for q in range(128) for p in range(16)])
    perm2 = np.array([p * 128 + q for q in range(128) for p in range(64)])
    H128 = _sylvester(128)
    maps = []
    for c in range(N_CORES):
        m = {}
        m["wupP"] = np.ascontiguousarray(W_up.T[perm1, :])
        m["wgateP"] = np.ascontiguousarray(W_gate.T[perm1, :])
        m["wdownP"] = np.ascontiguousarray(W_down.T[perm2, :])
        m["suug"] = np.ascontiguousarray(SU_ug.reshape(16, 128).T)
        m["sudp"] = np.ascontiguousarray(SU_dp.reshape(64, 128).T)
        m["scup"] = np.ascontiguousarray(Wsug[:8192].reshape(64, 128).T)
        m["scgate"] = np.ascontiguousarray(Wsug[8192:].reshape(64, 128).T)
        m["scdown"] = np.ascontiguousarray(Wsdp.reshape(16, 128).T)
        m["h128"] = H128
        m["hbd16"] = np.kron(np.eye(8, dtype=np.float32), _sylvester(16))
        m["hbd64"] = np.kron(np.eye(2, dtype=np.float32), _sylvester(64))
        for k in ("ones_row", "ones_col", "cmisc"):
            m[k + "D"] = consts[k]
        maps.append(m)
    return maps


def declare_D(nc, L):
    t = {}
    t["wupP"] = nc.dram_tensor("wupP", [HID, 8192], F32, kind="ExternalInput").ap()
    t["wgateP"] = nc.dram_tensor("wgateP", [HID, 8192], F32, kind="ExternalInput").ap()
    t["wdownP"] = nc.dram_tensor("wdownP", [8192, HID], F32, kind="ExternalInput").ap()
    for nme, shp in (("suug", [128, 16]), ("sudp", [128, 64]), ("scup", [128, 64]),
                     ("scgate", [128, 64]), ("scdown", [128, 16]),
                     ("h128", [128, 128]), ("hbd16", [128, 128]), ("hbd64", [128, 128])):
        t[nme] = nc.dram_tensor(nme, shp, F32, kind="ExternalInput").ap()
    for nme, shp in (("ones_row", [1, 128]), ("ones_col", [128, 1]),
                     ("cmisc", [128, len(CONST_VALS)])):
        t[nme] = nc.dram_tensor(nme + "D", shp, F32, kind="ExternalInput").ap()
    t["Zs"] = nc.dram_tensor("ZsD", [HID, 512], BF16, kind="Internal").ap()
    t["Ys"] = nc.dram_tensor("YsD", [8192, 512], BF16, kind="Internal").ap()
    return t


def alloc_D(nc, stack, L):
    sb = {}

    def A(name, shape, dt):
        sb[name] = stack.enter_context(nc.sbuf_tensor("sbD_" + name, shape, dt))
    A("cmisc", [128, len(CONST_VALS)], F32)
    A("ones_row", [1, 128], F32)
    A("ones_col", [128, 1], F32)
    A("onescol_b", [128, 1], BF16)
    for nme in ("suug", "scdown"):
        A(nme, [128, 16], F32)
    for nme in ("sudp", "scup", "scgate"):
        A(nme, [128, 64], F32)
    for nme in ("h128b", "hbd16b", "hbd64b"):
        A(nme, [128, 128], BF16)
    A("r2f", [128, 16 * 512], F32)
    A("r2b", [128, 16 * 512], BF16)
    A("zt", [128, 512], BF16)
    A("zh", [128, 16 * 512], BF16)
    A("wgt", [128, 2 * 2048], BF16)
    A("wut", [128, 2 * 2048], BF16)
    A("wdt", [128, 2 * 768], BF16)
    A("ybuf", [128, 64 * 512], BF16)
    A("gtile", [128, 512], BF16)
    A("utile", [128, 512], BF16)
    A("ftile", [128, 512], BF16)
    A("preLN", [128, 16 * 512], F32)
    A("pbf", [128, 512], BF16)
    A("sqbf", [128, 512], BF16)
    A("xres", [128, 512], F32)
    A("outt", [128, 1024], F32)
    for nme in ("mrow", "m2row", "v1row", "varrow", "lnr1", "lnr2"):
        A(nme, [1, 512], F32)
    return sb


def emit_D(nc, em, t, sb, pb, L, r2f_in, out_ap, sqrt2048_idx=12, sqrt8192_idx=11):
    LQ = L // 4
    NTC = LQ // 512
    emit_const_load(nc, em, t, sb)
    for nme in ("suug", "sudp", "scup", "scgate", "scdown"):
        em.dma("sync", lambda s, nme=nme: s.dma_start(out=sb[nme][:], in_=t[nme][:]),
               outs=(nme,))
    for nme, src_ in (("h128b", "h128"), ("hbd16b", "hbd16"), ("hbd64b", "hbd64")):
        em.dma("pool", lambda g, nme=nme, src_=src_: g.dma_start(out=sb[nme][:], in_=t[src_][:]),
               outs=(nme,))
    em.dve(lambda v: v.tensor_scalar(sb["scup"][:], sb["scup"][:],
                                     sb["cmisc"][:, sqrt2048_idx:sqrt2048_idx + 1], None, ALU.mult),
           ins=("scup", "cmisc"), outs=("scup",))
    em.dve(lambda v: v.tensor_scalar(sb["scgate"][:], sb["scgate"][:],
                                     sb["cmisc"][:, sqrt2048_idx:sqrt2048_idx + 1], None, ALU.mult),
           ins=("scgate", "cmisc"), outs=("scgate",))
    em.dve(lambda v: v.tensor_scalar(sb["scdown"][:], sb["scdown"][:],
                                     sb["cmisc"][:, sqrt8192_idx:sqrt8192_idx + 1], None, ALU.mult),
           ins=("scdown", "cmisc"), outs=("scdown",))
    for tc in range(NTC):
        cs = slice(tc * 512, (tc + 1) * 512)
        for ft in range(16):
            em.dma("sync", lambda s, ft=ft, cs=cs: s.dma_start(
                out=sb["r2f"][:, ft * 512:(ft + 1) * 512],
                in_=r2f_in[ft * 128:(ft + 1) * 128, cs]), outs=(f"r2f{ft}",))
            em.dma("pool", lambda g, ft=ft, cs=cs: g.dma_start(
                out=sb["r2b"][:, ft * 512:(ft + 1) * 512],
                in_=r2f_in[ft * 128:(ft + 1) * 128, cs]), outs=(f"r2b{ft}",))
        for ft in range(16):
            em.dve(lambda v, ft=ft: v.tensor_scalar(
                sb["zt"][:], sb["r2b"][:, ft * 512:(ft + 1) * 512],
                sb["suug"][:, ft:ft + 1], None, ALU.mult),
                ins=(f"r2b{ft}", "suug"), outs=("zt",))
            em.te(lambda te: te.matmul(pb["mm0"][:, 0:512], sb["h128b"][:], sb["zt"][:],
                                       start=True, stop=True),
                  ins=("h128b", "zt"), outs=("mm0",))
            em.dve(lambda v: v.tensor_copy(sb["zt"][:], pb["mm0"][:, 0:512]),
                   ins=("mm0",), outs=("zt",))
            em.dma("sync", lambda s, ft=ft: s.dma_start(
                out=t["Zs"][ft * 128:(ft + 1) * 128, :], in_=sb["zt"][:]),
                ins=("zt",), outs=("Zs",))
        for rt in range(16):
            em.dma("sync", lambda s, rt=rt: s.dma_start(
                out=sb["zt"][:],
                in_=t["Zs"].rearrange("(p q) t -> q p t", p=16)[rt * 8:(rt + 1) * 8, :, :]),
                ins=("Zs",), outs=("zt",))
            em.te(lambda te: te.matmul(pb["mm0"][:, 0:512], sb["hbd16b"][:], sb["zt"][:],
                                       start=True, stop=True),
                  ins=("hbd16b", "zt"), outs=("mm0",))
            em.dve(lambda v, rt=rt: v.tensor_copy(sb["zh"][:, rt * 512:(rt + 1) * 512],
                                               pb["mm0"][:, 0:512]),
                   ins=("mm0",), outs=(f"zh{rt}",))
        zins = tuple(f"zh{rt}" for rt in range(16))
        for ot in range(64):
            par = ot % 2
            em.dma("pool", lambda g, ot=ot, par=par: g.dma_start(
                out=sb["wgt"][:, par * 2048:(par + 1) * 2048].rearrange("p (a o) -> p a o", a=16),
                in_=t["wgateP"][:, ot * 128:(ot + 1) * 128].rearrange("(a p) o -> p a o", p=128)),
                outs=(f"wgt{par}",))
            em.dma("pool", lambda g, ot=ot, par=par: g.dma_start(
                out=sb["wut"][:, par * 2048:(par + 1) * 2048].rearrange("p (a o) -> p a o", a=16),
                in_=t["wupP"][:, ot * 128:(ot + 1) * 128].rearrange("(a p) o -> p a o", p=128)),
                outs=(f"wut{par}",))

            def mmg(te, par=par):
                last = None
                for a in range(16):
                    last = te.matmul(pb["mm0"][:, 0:512],
                                     sb["wgt"][:, par * 2048 + a * 128: par * 2048 + (a + 1) * 128],
                                     sb["zh"][:, a * 512:(a + 1) * 512],
                                     start=(a == 0), stop=(a == 15))
                return last
            em.op("te", "te", mmg, ins=(f"wgt{par}",) + zins, outs=("mm0",))

            def mmu(te, par=par):
                last = None
                for a in range(16):
                    last = te.matmul(pb["mm1"][:, 0:512],
                                     sb["wut"][:, par * 2048 + a * 128: par * 2048 + (a + 1) * 128],
                                     sb["zh"][:, a * 512:(a + 1) * 512],
                                     start=(a == 0), stop=(a == 15))
                return last
            em.op("te", "te", mmu, ins=(f"wut{par}",) + zins, outs=("mm1",))
            em.act(lambda sc_, ot=ot: sc_.activation(sb["gtile"][:], pb["mm0"][:, 0:512],
                                                     AF.Silu, scale=sb["scgate"][:, ot:ot + 1]),
                   ins=("mm0", "scgate"), outs=("gtile",))
            em.dve(lambda v, ot=ot: v.tensor_scalar(sb["utile"][:], pb["mm1"][:, 0:512],
                                                    sb["scup"][:, ot:ot + 1], None, ALU.mult),
                   ins=("mm1", "scup"), outs=("utile",))
            em.dve(lambda v: v.tensor_tensor(sb["ftile"][:], sb["gtile"][:], sb["utile"][:],
                                             ALU.mult),
                   ins=("gtile", "utile"), outs=("ftile",))
            em.dve(lambda v, ot=ot: v.tensor_scalar(sb["ftile"][:], sb["ftile"][:],
                                                    sb["sudp"][:, ot:ot + 1], None, ALU.mult),
                   ins=("ftile", "sudp"), outs=("ftile",))
            em.te(lambda te: te.matmul(pb["mm2"][:, 0:512], sb["h128b"][:], sb["ftile"][:],
                                       start=True, stop=True),
                  ins=("h128b", "ftile"), outs=("mm2",))
            em.dve(lambda v: v.tensor_copy(sb["ftile"][:], pb["mm2"][:, 0:512]),
                   ins=("mm2",), outs=("ftile",))
            em.dma("sync", lambda s, ot=ot: s.dma_start(
                out=t["Ys"][ot * 128:(ot + 1) * 128, :], in_=sb["ftile"][:]),
                ins=("ftile",), outs=("Ys",))
        for rt in range(64):
            em.dma("sync", lambda s, rt=rt: s.dma_start(
                out=sb["ftile"][:],
                in_=t["Ys"].rearrange("(p q) t -> q p t", p=64)[rt * 2:(rt + 1) * 2, :, :]),
                ins=("Ys",), outs=("ftile",))
            em.te(lambda te: te.matmul(pb["mm2"][:, 0:512], sb["hbd64b"][:], sb["ftile"][:],
                                       start=True, stop=True),
                  ins=("hbd64b", "ftile"), outs=("mm2",))
            em.dve(lambda v, rt=rt: v.tensor_copy(sb["ybuf"][:, rt * 512:(rt + 1) * 512],
                                               pb["mm2"][:, 0:512]),
                   ins=("mm2",), outs=(f"yb{rt}",))
        yins = tuple(f"yb{rt}" for rt in range(64))
        groups = [(0, 6), (6, 12), (12, 16)]
        for g0, g1 in groups:
            ncols = (g1 - g0) * 128
            for kt in range(64):
                par = kt % 2
                em.dma("pool", lambda g, kt=kt, g0=g0, ncols=ncols, par=par: g.dma_start(
                    out=sb["wdt"][:, par * 768: par * 768 + ncols],
                    in_=t["wdownP"][kt * 128:(kt + 1) * 128, g0 * 128: g0 * 128 + ncols]),
                    outs=(f"wdt{par}",))

                def mmd(te, kt=kt, g0=g0, g1=g1, par=par):
                    last = None
                    for i in range(g1 - g0):
                        last = te.matmul(pb[f"mm{i}"][:, 0:512],
                                         sb["wdt"][:, par * 768 + i * 128: par * 768 + (i + 1) * 128],
                                         sb["ybuf"][:, kt * 512:(kt + 1) * 512],
                                         start=(kt == 0), stop=(kt == 63))
                    return last
                em.op("te", "te", mmd, ins=(f"wdt{par}",) + yins,
                      outs=tuple(f"mm{i}" for i in range(g1 - g0)))
            for i, o2 in enumerate(range(g0, g1)):
                em.dve(lambda v, i=i, o2=o2: v.scalar_tensor_tensor(
                    sb["preLN"][:, o2 * 512:(o2 + 1) * 512], pb[f"mm{i}"][:, 0:512],
                    sb["scdown"][:, o2:o2 + 1], sb["r2f"][:, o2 * 512:(o2 + 1) * 512],
                    ALU.mult, ALU.add),
                    ins=(f"mm{i}", "scdown", f"r2f{o2}"), outs=(f"preLN{o2}",))
        emit_LN(nc, em, sb, pb, ntiles=16, width=512,
                consume=mk_ln_writer(em, sb, out_ap, None, tc, 512, "outo"))


from contextlib import ExitStack
from concourse.bass_utils import run_bass_kernel_spmd

LFULL = 8192
_PROG_CACHE = {}


def _build_A(L):
    nc = bass.Bass("TRN2", target_bir_lowering=False)
    t = declare_A(nc, L)
    bscr, gscr, gatescr = declare_A_scratch_beta(nc, L)
    resid0 = nc.dram_tensor("resid0", [HID, L // 4], F32, kind="ExternalOutput").ap()
    with ExitStack() as stack:
        sb = alloc_A(nc, stack, L)
        pb = mk_psum_banks(nc, stack)
        block = stack.enter_context(nc.Block())
        em = Emitter(nc, block, stack)
        emit_A(nc, em, t, sb, pb, L, bscr, gscr, gatescr)
        emit_A_scan(nc, em, t, sb, pb, L, bscr, gscr, gatescr)
        emit_A_wo(nc, em, t, sb, pb, L, resid0)
        toks = [em.ctx.buf(f"r0out_{ft}_{tc}").w
                for ft in range(16) for tc in range(L // 4 // 256)]
        em.wait_all("sync", toks)
        em.flush()
    return nc


def _build_B(L):
    nc = bass.Bass("TRN2", target_bir_lowering=False)
    t = declare_B(nc, L)
    r0in = nc.dram_tensor("r0in", [HID, L // 4], F32, kind="ExternalInput").ap()
    r1out = nc.dram_tensor("r1out", [HID, L // 4], F32, kind="ExternalOutput").ap()
    with ExitStack() as stack:
        sb = alloc_B(nc, stack, L)
        pb = mk_psum_banks(nc, stack)
        block = stack.enter_context(nc.Block())
        em = Emitter(nc, block, stack)
        emit_B(nc, em, t, sb, pb, L, r0in, r1out)
        toks = [em.ctx.buf(f"r1o_{ft}_{tc}").w
                for ft in range(16) for tc in range(L // 4 // 512)]
        em.wait_all("sync", toks)
        em.flush()
    return nc


def _build_C(L):
    nc = bass.Bass("TRN2", target_bir_lowering=False)
    t = declare_C(nc, L)
    r1in = nc.dram_tensor("r1in", [HID, L // 4], F32, kind="ExternalInput").ap()
    r2out = nc.dram_tensor("r2out", [HID, L // 4], F32, kind="ExternalOutput").ap()
    with ExitStack() as stack:
        sb = alloc_C(nc, stack, L)
        pb = mk_psum_banks(nc, stack)
        block = stack.enter_context(nc.Block())
        em = Emitter(nc, block, stack)
        emit_C(nc, em, t, sb, pb, L, r1in, r2out,
               invsq_idx=(11 if L == 8192 else 12))
        toks = [em.ctx.buf(f"r2o_{ft}_{tc}").w
                for ft in range(16) for tc in range(L // 4 // 256)]
        em.wait_all("sync", toks)
        em.flush()
    return nc


def _build_D(L):
    nc = bass.Bass("TRN2", target_bir_lowering=False)
    t = declare_D(nc, L)
    r2in = nc.dram_tensor("r2in", [HID, L // 4], F32, kind="ExternalInput").ap()
    oout = nc.dram_tensor("oout", [HID, L // 4], F32, kind="ExternalOutput").ap()
    with ExitStack() as stack:
        sb = alloc_D(nc, stack, L)
        pb = mk_psum_banks(nc, stack)
        block = stack.enter_context(nc.Block())
        em = Emitter(nc, block, stack)
        emit_D(nc, em, t, sb, pb, L, r2in, oout)
        toks = [em.ctx.buf(f"outo_{ft}_{tc}").w
                for ft in range(16) for tc in range(L // 4 // 512)]
        em.wait_all("sync", toks)
        em.flush()
    return nc


def _prog(key, builder, L):
    if key not in _PROG_CACHE:
        _PROG_CACHE[key] = builder(L)
    return _PROG_CACHE[key]


def kernel(**inputs):
    L = int(np.asarray(inputs["L0_attn_x"]).shape[1])
    mapsA = host_prep_A(inputs, L=L)
    ncA = _prog(("A", L), _build_A, L)
    resA = run_bass_kernel_spmd(ncA, mapsA, core_ids=list(range(N_CORES)))
    r0 = [np.asarray(resA.results[c]["resid0"]) for c in range(N_CORES)]
    mapsB = host_prep_B(inputs, L=L)
    for c in range(N_CORES):
        mapsB[c]["r0in"] = r0[c]
    ncB = _prog(("B", L), _build_B, L)
    resB = run_bass_kernel_spmd(ncB, mapsB, core_ids=list(range(N_CORES)))
    r1 = [np.asarray(resB.results[c]["r1out"]) for c in range(N_CORES)]
    mapsC = host_prep_C(inputs, L=L)
    for c in range(N_CORES):
        mapsC[c]["r1in"] = r1[c]
    ncC = _prog(("C", L), _build_C, L)
    resC = run_bass_kernel_spmd(ncC, mapsC, core_ids=list(range(N_CORES)))
    r2 = [np.asarray(resC.results[c]["r2out"]) for c in range(N_CORES)]
    mapsD = host_prep_D(inputs, L=L)
    for c in range(N_CORES):
        mapsD[c]["r2in"] = r2[c]
    ncD = _prog(("D", L), _build_D, L)
    resD = run_bass_kernel_spmd(ncD, mapsD, core_ids=list(range(N_CORES)))
    out = np.zeros((B, L, HID), np.float32)
    LQ = L // 4
    for c in range(N_CORES):
        b, q = c // 4, c % 4
        out[b, q * LQ:(q + 1) * LQ, :] = np.asarray(resD.results[c]["oout"]).T
    return out
